# version 14
# baseline (speedup 1.0000x reference)
import math
import numpy as np
import ml_dtypes
import concourse.bass as bass
import concourse.mybir as mybir
from concourse.bass_utils import run_bass_kernel_spmd

F32 = mybir.dt.float32
BF16 = mybir.dt.bfloat16
F16 = mybir.dt.float16
I32 = mybir.dt.int32
I16 = mybir.dt.int16
AF = mybir.ActivationFunctionType
ALU = mybir.AluOpType
AX = mybir.AxisListType.X

NCORE = 8
T = 4096
D = 2048
NH, NKV, HD = 16, 4, 128
GQ = NH // NKV
DI = 5632
DTR, NST = 64, 16
ROPE_BASE = 100000.0
LN_EPS = 1e-5
QK_EPS = 1e-6
RMS_EPS = 1e-6
ATT_SCALE = float(np.float32(1.0) / np.sqrt(np.float32(HD)))
ATT_SHIFT = -11.3137085

TT = T // 128
DT_ = D // 128
TC = T // 512

ENG_ORDER = ["sync", "sca", "vec", "gps", "ten"]


class PB:

    def __init__(self):
        self.nc = bass.Bass("TRN2", target_bir_lowering=False)
        self.sem = {}
        self.cnt = {}
        self.marks = {}
        self.ops = {e: [] for e in ENG_ORDER}
        self.dma_rot = {}
        self._consts = {}
        self._ctile = None
        self.pre = ""
        self.sb_arena = None

    ESZ = {F32: 4, BF16: 2, F16: 2, I32: 4, I16: 2}

    def init_arena(self, sbytes=210944):
        self.sbytes = sbytes
        self.sb_arena = self.nc.alloc_sbuf_tensor("ARENA", [128, sbytes],
                                                   mybir.dt.uint8)
        self.ps_arena = self.nc.alloc_psum_tensor("PARENA", [128, 4096], F32)
        self.open_frame()

    def open_frame(self):
        self.sb_off = 0
        self.ps_bank = 0

    def salloc(self, name, shape, dtype):
        if self.sb_arena is None:
            self.init_arena()
        esz = self.ESZ[dtype]
        nb = shape[1] * esz
        off = (self.sb_off + 63) & ~63
        self.sb_off = off + nb
        if not hasattr(self, "salloc_map"):
            self.salloc_map = {}
        self.salloc_map[self.pre + name] = (off, shape, dtype)
        assert self.sb_off <= self.sbytes, (
            f"sbuf arena overflow at {name}: {self.sb_off}")
        v = self.sb_arena[:, off:off + nb].bitcast(dtype)
        if shape[0] != 128:
            v = v[0:shape[0], :]
        return v

    def palloc(self, name, shape, dtype=F32):
        esz = self.ESZ[dtype]
        nf32 = (shape[1] * esz + 3) // 4
        banks = (nf32 + 511) // 512
        start = self.ps_bank
        self.ps_bank += banks
        assert self.ps_bank <= 8, f"psum arena overflow at {name}"
        v = self.ps_arena[:, start * 512:(start + banks) * 512]
        if dtype != F32:
            v = v.bitcast(dtype)
        if shape[0] != 128:
            v = v[0:shape[0], :]
        return v

    def global_barrier(self):
        snap = [(q, self.cnt[q]) for q in list(self.cnt) if self.cnt[q] > 0]
        for ename in ENG_ORDER:
            self.ops[ename].append((list(snap), None, None, 0))

    def constcol(self, value):
        if value not in self._consts:
            if self._ctile is None:
                self._ctile = self.nc.alloc_sbuf_tensor("konst", [128, 16], F32)
            k = len(self._consts)
            ap = self._ctile[:, k:k+1]
            self.s("kon")
            self.cnt["kon"] += 1
            self.ops["gps"].append(
                ([], (lambda e, ap=ap, value=value: e.memset(ap, value)),
                 "kon", 1))
            self._consts[value] = (ap, ("kon", self.cnt["kon"]))
        return self._consts[value]

    def alias(self, dst, srcmark):
        self.marks[self.pre + dst] = (self.marks[self.pre + srcmark]
                                      if isinstance(srcmark, str) else srcmark)

    def getmark(self, mk):
        return self.marks[self.pre + mk]

    def s(self, q):
        if q not in self.sem:
            self.sem[q] = self.nc.alloc_semaphore(name="S_" + q)
            self.cnt[q] = 0
        return self.sem[q]

    def _resolve(self, waits):
        out = []
        for wt in waits:
            if isinstance(wt, tuple):
                out.append(wt)
            else:
                q, v = self.marks[self.pre + wt]
                if v > 0:
                    out.append((q, v))
        return out

    def op(self, eng, fn, q=None, mark=None, waits=(), chain=False, n=1,
           inc=True):
        q = q or eng
        self.s(q)
        wl = list(waits)
        if chain and self.cnt[q] > 0:
            wl.append((q, self.cnt[q]))
        resolved = self._resolve(wl)
        if not inc:
            n = 0
        self.cnt[q] += n
        if mark is not None:
            self.marks[self.pre + mark] = (q, self.cnt[q])
        self.ops[eng].append((resolved, fn, q, n))

    def dma(self, eng, stream, fn, mark=None, waits=(), rot=2, bucket=None):
        stream = self.pre + stream
        key = (stream, rot)
        idx = self.dma_rot.get(key, 0)
        self.dma_rot[key] = idx + 1
        b = idx if bucket is None else bucket
        q = f"{stream}{b % rot}"
        self.op(eng, fn, q=q, mark=mark, waits=waits, n=16)

    def barrier_wait(self, eng, waits):
        resolved = self._resolve(list(waits))
        self.ops[eng].append((resolved, None, None, 0))

    def mark_here(self, q, mark):
        self.s(q)
        self.marks[self.pre + mark] = (q, self.cnt[q])

    def finalize(self):
        nc = self.nc
        with nc.Block() as blk:
            def make_section(ename):
                def section(e):
                    for resolved, fn, q, n in self.ops[ename]:
                        for (qq, v) in resolved:
                            e.wait_ge(self.sem[qq], v)
                        if fn is not None:
                            inst = fn(e)
                            if n:
                                inst.then_inc(self.sem[q], n)
                return section
            for ename in ENG_ORDER:
                if self.ops[ename]:
                    getattr(blk, {"sync": "sync", "sca": "scalar", "vec": "vector",
                                  "gps": "gpsimd", "ten": "tensor"}[ename])(
                        make_section(ename))
        return nc


def hilo_split(p, hi, lo, src, tag, waits=()):
    p.op("vec", lambda v: v.tensor_copy(hi, src), chain=True, waits=waits,
         mark=tag + "_hi")
    p.op("vec", lambda v: v.tensor_tensor(out=lo, in0=src, in1=hi, op=ALU.subtract),
         chain=True, mark=tag)



def build_ln(store_pair):
    p = PB()
    nc = p.nc
    io = {"x": nc.dram_tensor("x", [T, D], F32, kind="ExternalInput").ap(),
          "xn_hi": nc.dram_tensor("xn_hi", [T, D], BF16, kind="ExternalOutput").ap(),
          "xn_lo": (nc.dram_tensor("xn_lo", [T, D], BF16,
                                   kind="ExternalOutput").ap()
                    if store_pair else None)}
    emit_ln(p, io, store_pair)
    return p.finalize()


def emit_ln(p, io, store_pair):
    nc = p.nc
    X = io["x"]; XNH = io["xn_hi"]; XNL = io.get("xn_lo")
    xt = [p.salloc(f"xt{j}", [128, D], F32) for j in range(2)]
    sq = p.salloc("sq", [128, D], F32)
    xn = p.salloc("xn", [128, D], F32)
    xh = [p.salloc(f"xh{j}", [128, D], BF16) for j in range(2)]
    xl = [p.salloc(f"xl{j}", [128, D], BF16) for j in range(2)]
    st = p.salloc("st", [128, 8], F32)

    for i in range(TT):
        b = xt[i % 2]
        lw = [f"ln_apply{i-2}", f"sq{i-2}"] if i >= 2 else []
        p.dma("sync", "ld", (lambda e, b=b, i=i: e.dma_start(
            out=b[:], in_=X[i*128:(i+1)*128, :])), mark=f"x{i}", waits=lw)
        p.op("sca", lambda e, b=b: e.activation(out=sq[:], in_=b[:], func=AF.Square,
                                                accum_out=st[:, 2:3]),
             waits=[f"x{i}", f"var{i-1}"] if i >= 1 else [f"x{i}"],
             chain=True, mark=f"sq{i}")
        p.op("vec", lambda e, b=b: e.reduce_sum(st[:, 0:1], b[:], axis=AX),
             waits=[f"x{i}"] + ([f"nmr{i-1}"] if i >= 1 else []), chain=True)
        p.op("vec", lambda e: e.tensor_scalar_mul(st[:, 1:2], st[:, 0:1], 1.0 / D),
             chain=True)
        p.op("vec", lambda e: e.tensor_scalar_mul(st[:, 2:3], st[:, 2:3], 1.0 / D),
             waits=[f"sq{i}"], chain=True)
        p.op("vec", lambda e: e.scalar_tensor_tensor(
            out=st[:, 3:4], in0=st[:, 1:2], scalar=-1.0, in1=st[:, 1:2],
            op0=ALU.mult, op1=ALU.mult), chain=True)
        p.op("vec", lambda e: e.tensor_tensor(out=st[:, 3:4], in0=st[:, 2:3],
                                              in1=st[:, 3:4], op=ALU.add),
             chain=True, mark=f"var{i}")
        eps_ap, eps_mk = p.constcol(LN_EPS)
        p.op("sca", lambda e: e.activation(out=st[:, 4:5], in_=st[:, 3:4],
                                           func=AF.Ln, bias=eps_ap),
             waits=[f"var{i}", eps_mk], chain=True)
        p.op("sca", lambda e: e.activation(out=st[:, 5:6], in_=st[:, 4:5],
                                           func=AF.Exp, scale=-0.5),
             chain=True, mark=f"rstd{i}")
        p.op("vec", lambda e: e.scalar_tensor_tensor(
            out=st[:, 6:7], in0=st[:, 1:2], scalar=-1.0, in1=st[:, 5:6],
            op0=ALU.mult, op1=ALU.mult), waits=[f"rstd{i}"], chain=True,
            mark=f"nmr{i}")
        p.op("vec", lambda e, b=b: e.tensor_scalar(
            out=xn[:], in0=b[:], scalar1=st[:, 5:6], scalar2=st[:, 6:7],
            op0=ALU.mult, op1=ALU.add), chain=True, mark=f"ln_apply{i}")
        hw = [f"st_hi{i-2}"] if i >= 2 else []
        p.op("vec", lambda e, i=i: e.tensor_copy(xh[i % 2][:], xn[:]),
             chain=True, waits=hw, mark=f"hi{i}")
        if store_pair:
            lww = [f"st_lo{i-2}"] if i >= 2 else []
            p.op("vec", lambda e, i=i: e.tensor_tensor(
                out=xl[i % 2][:], in0=xn[:], in1=xh[i % 2][:], op=ALU.subtract),
                 chain=True, waits=lww, mark=f"lo{i}")
        p.dma("sca", "st", (lambda e, i=i: e.dma_start(
            out=XNH[i*128:(i+1)*128, :], in_=xh[i % 2][:])),
            mark=f"st_hi{i}", waits=[f"hi{i}"])
        if store_pair:
            p.dma("sca", "st2", (lambda e, i=i: e.dma_start(
                out=XNL[i*128:(i+1)*128, :], in_=xl[i % 2][:])),
                mark=f"st_lo{i}", waits=[f"lo{i}"])
    final = [f"st_hi{TT-1}", f"st_hi{TT-2}"]
    if store_pair:
        final += [f"st_lo{TT-1}", f"st_lo{TT-2}"]
    p.barrier_wait("sync", final)



_LAUNCH_CACHE = {}


def _get(name, builder):
    if name not in _LAUNCH_CACHE:
        _LAUNCH_CACHE[name] = builder()
    return _LAUNCH_CACHE[name]


def run_launch(nc, in_maps):
    res = run_bass_kernel_spmd(nc, in_maps, core_ids=list(range(len(in_maps))))
    return res.results


def kernel(**inputs):
    return kernel_fused(**inputs)


def kernel_unfused(L0_attn_x, L0_attn_wq, L0_attn_wk, L0_attn_wv, L0_attn_wo,
           L0_ffn_weight, L0_ffn_bias, L1_attn_W_x, L1_attn_W_dt, L1_attn_b_dt,
           L1_attn_A_log, L1_attn_D, L1_ffn_gate_up_packed,
           L1_ffn_gate_up_scales, L1_ffn_down_packed, L1_ffn_down_scales):
    n = NCORE
    xs = [np.ascontiguousarray(L0_attn_x[b]) for b in range(n)]
    wq = np.ascontiguousarray(L0_attn_wq)
    wk = np.ascontiguousarray(L0_attn_wk)
    wv = np.ascontiguousarray(L0_attn_wv)
    wo = np.ascontiguousarray(L0_attn_wo)
    fw = np.ascontiguousarray(L0_ffn_weight)
    fb = np.ascontiguousarray(L0_ffn_bias).reshape(1, D)
    wx = np.ascontiguousarray(L1_attn_W_x)
    wdt = np.ascontiguousarray(L1_attn_W_dt)
    bdt = np.ascontiguousarray(L1_attn_b_dt).reshape(1, D)
    alog = np.ascontiguousarray(L1_attn_A_log)
    dv = np.ascontiguousarray(L1_attn_D).reshape(1, D)
    gup = np.ascontiguousarray(L1_ffn_gate_up_packed)
    gus = np.ascontiguousarray(L1_ffn_gate_up_scales)
    dnp = np.ascontiguousarray(L1_ffn_down_packed)
    dns = np.ascontiguousarray(L1_ffn_down_scales)

    r = run_launch(_get("ln0", lambda: build_ln(False)),
                   [dict(x=xs[b]) for b in range(n)])
    xn0 = [r[b]["xn_hi"] for b in range(n)]
    r = run_launch(_get("qkv", build_qkv),
                   [dict(xn_hi=xn0[b], wq=wq, wk=wk, wv=wv) for b in range(n)])
    QT = [r[b]["QT"] for b in range(n)]
    KT = [r[b]["KT"] for b in range(n)]
    VT = [r[b]["VT"] for b in range(n)]
    r = run_launch(_get("attn", build_attn),
                   [dict(QT=QT[b], KT=KT[b], VT=VT[b]) for b in range(n)])
    YT = [r[b]["YT"] for b in range(n)]
    r = run_launch(_get("d", build_wo_ffn0),
                   [dict(x=xs[b], xn_hi=xn0[b], YT=YT[b], wo=wo, fw=fw, fb=fb)
                    for b in range(n)])
    x1 = [r[b]["x1"] for b in range(n)]
    r = run_launch(_get("ln1", lambda: build_ln(True)),
                   [dict(x=x1[b]) for b in range(n)])
    xn1h = [r[b]["xn_hi"] for b in range(n)]
    xn1l = [r[b]["xn_lo"] for b in range(n)]
    r = run_launch(_get("xproj", build_xproj),
                   [dict(xn_hi=xn1h[b], W_x=wx, W_dt=wdt, b_dt=bdt)
                    for b in range(n)])
    dtv = [r[b]["dtv"] for b in range(n)]
    bc = {nm: [r[b][nm] for b in range(n)] for nm in ("Bf", "Cf")}
    r = run_launch(_get("deq", build_dequant),
                   [dict(gup=gup, gus=gus, dnp=dnp, dns=dns) for b in range(n)])
    wgu_d = [r[b]["wgu_d"] for b in range(n)]
    wdn_d = [r[b]["wdn_d"] for b in range(n)]
    r = run_launch(_get("scan", build_scan),
                   [dict(dtv=dtv[b], xn_hi=xn1h[b], xn_lo=xn1l[b], A_log=alog,
                         Dv=dv, Bf=bc["Bf"][b], Cf=bc["Cf"][b])
                    for b in range(n)])
    ysh = [r[b]["yssm_hi"] for b in range(n)]
    ysl_ = [r[b]["yssm_lo"] for b in range(n)]
    r = run_launch(_get("gu", build_gateup),
                   [dict(xn_hi=xn1h[b], wgu_d=wgu_d[b]) for b in range(n)])
    hact = [r[b]["hact"] for b in range(n)]
    r = run_launch(_get("down", build_down),
                   [dict(x1=x1[b], hact=hact[b], wdn_d=wdn_d[b],
                         yssm_hi=ysh[b], yssm_lo=ysl_[b]) for b in range(n)])
    out = np.stack([np.asarray(r[b]["out"], dtype=np.float32)
                    for b in range(n)], axis=0)
    return out



def build_qkv():
    p = PB()
    nc = p.nc
    io = {"xn_hi": nc.dram_tensor("xn_hi", [T, D], BF16, kind="ExternalInput").ap(),
          "wq": nc.dram_tensor("wq", [D, D], F32, kind="ExternalInput").ap(),
          "wk": nc.dram_tensor("wk", [D, 512], F32, kind="ExternalInput").ap(),
          "wv": nc.dram_tensor("wv", [D, 512], F32, kind="ExternalInput").ap(),
          "QT": nc.dram_tensor("QT", [D, T], BF16, kind="ExternalOutput").ap(),
          "KT": nc.dram_tensor("KT", [512, T], BF16, kind="ExternalOutput").ap(),
          "VT": nc.dram_tensor("VT", [T, 512], BF16, kind="ExternalOutput").ap()}
    emit_qkv(p, io)
    return p.finalize()


def emit_qkv(p, io):
    nc = p.nc
    XNH = io["xn_hi"]; WQ = io["wq"]; WK = io["wk"]; WV = io["wv"]
    QT = io["QT"]; KT = io["KT"]; VT = io["VT"]

    wq_b = p.salloc("wq_b", [128, DT_ * D], BF16)
    wk_b = p.salloc("wk_b", [128, DT_ * 512], BF16)
    wv_b = p.salloc("wv_b", [128, DT_ * 512], BF16)
    xnsl = [p.salloc(f"xnsl{j}", [128, DT_ * 512], BF16) for j in range(2)]
    cosb = p.salloc("cosb", [128, T], F32)
    sinb = p.salloc("sinb", [128, T], F32)
    ang = p.salloc("ang", [128, 1024], F32)
    jcol = p.salloc("jcol", [128, 2], F32)
    pmat = p.salloc("pmat", [128, 128], F32)
    pmA = p.salloc("pmA", [128, 128], F32)
    qf = p.salloc("qf", [128, 512], F32)
    rot = p.salloc("rot", [128, 512], F32)
    tmp128 = p.salloc("tmp128", [128, 512], F32)
    sqb = p.salloc("sqb", [128, 512], F32)
    sqh = p.salloc("sqh", [128, 512], BF16)
    sql = p.salloc("sql", [128, 512], BF16)
    msrow = p.salloc("msrow", [1, 512], F32)
    rsrow = p.salloc("rsrow", [1, 512], F32)
    rsh = p.salloc("rsh", [1, 512], BF16)
    rsl = p.salloc("rsl", [1, 512], BF16)
    qout = [p.salloc(f"qout{j}", [128, 512], BF16) for j in range(2)]
    vout = [p.salloc(f"vout{j}", [128, 512], BF16) for j in range(2)]
    ones_c = p.salloc("ones_c", [128, 1], BF16)
    ones_r = p.salloc("ones_r", [1, 128], BF16)
    pq = p.palloc("pq", [128, 1024], F32)
    pms = p.palloc("pms", [128, 512], F32)
    pbc = p.palloc("pbc", [128, 512], F32)
    pv = p.palloc("pv", [128, 1024], F32)
    pperm = p.palloc("pperm", [128, 512], F32)

    p.op("vec", lambda e: e.memset(ones_c[:], 1.0), mark="ones_c")
    p.op("vec", lambda e: e.memset(ones_r[:], 1.0), mark="ones_r")
    p.op("gps", lambda e: e.iota(jcol[:, 0:1], pattern=[[0, 1]], base=0,
                                 channel_multiplier=1,
                                 allow_small_or_imprecise_dtypes=True),
         mark="jcol", chain=True)
    p.op("vec", lambda e: e.tensor_scalar(out=jcol[:, 1:2], in0=jcol[:, 0:1],
                                          scalar1=63.5, scalar2=-64.0,
                                          op0=ALU.is_gt, op1=ALU.mult),
         waits=["jcol"], chain=True)
    p.op("vec", lambda e: e.tensor_tensor(out=jcol[:, 0:1], in0=jcol[:, 0:1],
                                          in1=jcol[:, 1:2], op=ALU.add),
         chain=True, mark="jmod")
    p.op("sca", lambda e: e.activation(out=jcol[:, 1:2], in_=jcol[:, 0:1],
                                       func=AF.Exp,
                                       scale=-math.log(ROPE_BASE) / 64.0),
         waits=["jmod"], mark="invf")
    p.op("gps", lambda e: e.memset(pmA[:], 1.0), mark="pm1", chain=True)
    p.op("gps", lambda e: e.affine_select(
        out=pmat[:], in_=pmA[:], pattern=[[1, 128]], compare_op=ALU.is_equal,
        fill=0.0, base=64, channel_multiplier=-1), waits=["pm1"], chain=True,
        mark="pmB")
    p.op("gps", lambda e: e.affine_select(
        out=pmA[:], in_=pmA[:], pattern=[[1, 128]], compare_op=ALU.is_equal,
        fill=0.0, base=-64, channel_multiplier=-1), waits=["pmB"], chain=True,
        mark="pmA2")
    p.op("vec", lambda e: e.tensor_tensor(out=pmat[:], in0=pmat[:], in1=pmA[:],
                                          op=ALU.subtract), waits=["pmA2"],
         chain=True, mark="pmat")

    ang2 = p.salloc("ang2", [128, 1024], F32)

    def sin_of(dst, src, phase, tag):
        TP = 2.0 * math.pi
        p.op("vec", lambda e, dst=dst, src=src, phase=phase: e.tensor_scalar(
            out=dst, in0=src, scalar1=phase, scalar2=None, op0=ALU.add),
            chain=True, waits=["ang"])
        p.op("vec", lambda e, dst=dst: e.tensor_scalar_mul(ang2[:], dst, 1.0 / TP),
             chain=True)
        p.op("vec", lambda e: e.tensor_copy(ang2[:].bitcast(I32), ang2[:]),
             chain=True)
        p.op("vec", lambda e: e.tensor_copy(ang2[:], ang2[:].bitcast(I32)),
             chain=True)
        p.op("vec", lambda e, dst=dst: e.scalar_tensor_tensor(
            out=dst, in0=ang2[:], scalar=-TP, in1=dst, op0=ALU.mult, op1=ALU.add),
            chain=True)
        for sgn in (1.0, -1.0):
            p.op("vec", lambda e, dst=dst, sgn=sgn: e.tensor_scalar(
                out=ang2[:], in0=dst, scalar1=sgn, scalar2=math.pi,
                op0=ALU.mult, op1=ALU.is_gt), chain=True)
            p.op("vec", lambda e, dst=dst, sgn=sgn: e.scalar_tensor_tensor(
                out=dst, in0=ang2[:], scalar=-sgn * TP, in1=dst,
                op0=ALU.mult, op1=ALU.add), chain=True, mark=tag + "_arg")
        p.op("sca", lambda e, dst=dst: e.activation(out=dst, in_=dst, func=AF.Sin),
             waits=[tag + "_arg"], chain=True, mark=tag)

    for tc4 in range(4):
        c0 = tc4 * 1024
        p.op("gps", lambda e, c0=c0: e.iota(
            ang[:], pattern=[[1, 1024]], base=c0, channel_multiplier=0,
            allow_small_or_imprecise_dtypes=True), chain=True,
            waits=[f"tblc{tc4-1}"] if tc4 >= 1 else [], mark=f"trow{tc4}")
        p.op("vec", lambda e: e.tensor_scalar_mul(ang[:], ang[:], jcol[:, 1:2]),
             waits=[f"trow{tc4}", "invf"], chain=True, mark="ang")
        sin_of(sinb[:, c0:c0+1024], ang[:], 0.0, f"sinb{tc4}")
        sin_of(cosb[:, c0:c0+1024], ang[:], math.pi / 2.0, f"cosb{tc4}")
        p.mark_here("sca", f"tblc{tc4}")
    p.marks[p.pre + "sinb"] = p.marks[p.pre + "tblc3"]
    p.marks[p.pre + "cosb"] = p.marks[p.pre + "tblc3"]

    for k in range(DT_):
        p.dma("gps", "wld", (lambda e, k=k: e.dma_start(
            out=wq_b[:, k*D:(k+1)*D], in_=WQ[k*128:(k+1)*128, :])), rot=1)
        p.dma("gps", "wld", (lambda e, k=k: e.dma_start(
            out=wk_b[:, k*512:(k+1)*512], in_=WK[k*128:(k+1)*128, :])), rot=1)
        p.dma("gps", "wld", (lambda e, k=k: e.dma_start(
            out=wv_b[:, k*512:(k+1)*512], in_=WV[k*128:(k+1)*128, :])),
            mark=("wld_all" if k == DT_ - 1 else f"wld{k}"), rot=1)

    qeps_ap, qeps_mk = p.constcol(QK_EPS)

    def rope_rms(jc, head, src_bank_ap, out_tile, tag, out_war=()):
        c0 = jc * 512
        p.op("sca", lambda e: e.copy(out=qf[:], in_=src_bank_ap), chain=True,
             waits=[f"{tag}_mmdone"], mark=f"{tag}_evac")
        p.op("ten", lambda e: e.matmul(pperm[:], lhsT=pmat[:], rhs=qf[:],
                                       start=True, stop=True), chain=True,
             waits=[f"{tag}_evac", "pmat", f"{tag}_permfree"],
             mark=f"{tag}_perm")
        p.op("vec", lambda e: e.tensor_tensor(
            out=rot[:], in0=qf[:], in1=cosb[:, c0:c0+512],
            op=ALU.mult), chain=True, waits=[f"{tag}_evac", "cosb", "sinb"])
        p.op("vec", lambda e: e.tensor_tensor(
            out=tmp128[:], in0=pperm[:], in1=sinb[:, c0:c0+512],
            op=ALU.mult), chain=True, waits=[f"{tag}_perm"],
            mark=f"{tag}_permread")
        p.op("vec", lambda e: e.tensor_tensor(
            out=rot[:], in0=rot[:], in1=tmp128[:], op=ALU.add),
            chain=True, mark=f"{tag}_rot")
        p.op("sca", lambda e: e.activation(out=sqb[:], in_=rot[:], func=AF.Square),
             waits=[f"{tag}_rot"], chain=True, mark=f"{tag}_sq")
        p.op("vec", lambda e: e.tensor_copy(sqh[:], sqb[:]), chain=True,
             waits=[f"{tag}_sq"])
        p.op("vec", lambda e: e.tensor_tensor(out=sql[:], in0=sqb[:], in1=sqh[:],
                                              op=ALU.subtract), chain=True,
             mark=f"{tag}_sqhl")
        p.op("ten", lambda e: e.matmul(pms[0:1, :], lhsT=ones_c[:], rhs=sqh[:],
                                       start=True, stop=False),
             waits=[f"{tag}_sqhl", "ones_c", f"{tag}_msfree"], chain=True)
        p.op("ten", lambda e: e.matmul(pms[0:1, :], lhsT=ones_c[:], rhs=sql[:],
                                       start=False, stop=True), chain=True,
             mark=f"{tag}_ms")
        p.op("sca", lambda e: e.copy(out=msrow[:], in_=pms[0:1, :]), chain=True,
             waits=[f"{tag}_ms"], mark=f"{tag}_msev")
        p.op("sca", lambda e: e.activation(out=msrow[:], in_=msrow[:], func=AF.Ln,
                                           scale=1.0 / 128.0, bias=qeps_ap[0:1, :]),
             chain=True, waits=[qeps_mk])
        p.op("sca", lambda e: e.activation(out=rsrow[:], in_=msrow[:], func=AF.Exp,
                                           scale=-0.5), chain=True,
             mark=f"{tag}_rs")
        p.op("vec", lambda e: e.tensor_copy(rsh[:], rsrow[:]), chain=True,
             waits=[f"{tag}_rs"])
        p.op("vec", lambda e: e.tensor_tensor(out=rsl[:], in0=rsrow[:], in1=rsh[:],
                                              op=ALU.subtract), chain=True,
             mark=f"{tag}_rshl")
        p.op("ten", lambda e: e.matmul(pbc[:], lhsT=ones_r[:], rhs=rsh[:],
                                       start=True, stop=False),
             waits=[f"{tag}_rshl", "ones_r", f"{tag}_bcfree"], chain=True)
        p.op("ten", lambda e: e.matmul(pbc[:], lhsT=ones_r[:], rhs=rsl[:],
                                       start=False, stop=True), chain=True,
             mark=f"{tag}_bc")
        p.op("vec", lambda e, out_tile=out_tile: e.tensor_tensor(
            out=out_tile[:], in0=rot[:], in1=pbc[:], op=ALU.mult), chain=True,
            waits=[f"{tag}_bc"] + list(out_war), mark=f"{tag}_out")
        p.mark_here("vec", f"{tag}_bcfree")
        p.mark_here("sca", f"{tag}_msfree_s")

    p.marks[p.pre + "_none"] = ("kon", 0)

    qk_iter = 0
    for jc in range(TC):
        q0 = jc * 512
        sl = xnsl[jc % 2]
        for k in range(DT_):
            lw = [f"slab_free{jc-2}"] if jc >= 2 else []
            p.dma("sync", "xl", (lambda e, k=k, q0=q0, sl=sl: e.dma_start(
                out=sl[:, k*512:(k+1)*512],
                in_=XNH[q0:q0+512, k*128:(k+1)*128], transpose=True)),
                mark=f"slab{jc}_{k}", waits=lw, rot=2, bucket=jc)
        slab_ready = [f"slab{jc}_{DT_-1}"]

        for head in range(NH + NKV):
            isq = head < NH
            wb, hh = (wq_b, head) if isq else (wk_b, head - NH)
            wstride = D if isq else 512
            tag = f"j{jc}h{head}"
            bank = pq[:, (qk_iter % 2) * 512:(qk_iter % 2) * 512 + 512]
            ot = qout[qk_iter % 2]
            prev = qk_iter - 2
            bw = ([f"j{(prev)//(NH+NKV)}h{prev%(NH+NKV)}_evac"] if prev >= 0 else [])
            ow = ([f"st_{prev}"] if prev >= 0 else [])
            for k in range(DT_):
                w = (["wld_all"] + slab_ready + bw) if k == 0 else []
                p.op("ten", lambda e, k=k, wb=wb, hh=hh, sl=sl, bank=bank,
                     wstride=wstride: e.matmul(
                         bank, lhsT=wb[:, k*wstride + hh*128: k*wstride + hh*128 + 128],
                         rhs=sl[:, k*512:(k+1)*512],
                         start=(k == 0), stop=(k == DT_ - 1)),
                     chain=(k == 0), waits=w, inc=(k == DT_ - 1),
                     mark=(f"{tag}_mmdone" if k == DT_ - 1 else None))
            prevtag = (f"j{(qk_iter-1)//(NH+NKV)}h{(qk_iter-1)%(NH+NKV)}"
                       if qk_iter >= 1 else None)
            p.marks[p.pre + f"{tag}_msfree"] = (p.marks[p.pre + f"{prevtag}_msev"]
                                        if prevtag else ("kon", 0))
            p.marks[p.pre + f"{tag}_bcfree"] = (p.marks[p.pre + f"{prevtag}_out"]
                                        if prevtag else ("kon", 0))
            p.marks[p.pre + f"{tag}_permfree"] = (p.marks[p.pre + f"{prevtag}_permread"]
                                          if prevtag else ("kon", 0))
            rope_rms(jc, head, bank, ot, tag, out_war=ow)
            dst = QT if isq else KT
            p.dma("sca", "qst", (lambda e, dst=dst, hh=hh, q0=q0, ot=ot: e.dma_start(
                out=dst[hh*128:(hh+1)*128, q0:q0+512], in_=ot[:])),
                mark=f"st_{qk_iter}", waits=[f"{tag}_out"], rot=2)
            qk_iter += 1

        for tt in range(4):
            vtag = f"vj{jc}t{tt}"
            vbank = pv[:, (tt % 2) * 512:(tt % 2) * 512 + 512]
            vo = vout[tt % 2]
            pv_prev = f"vj{jc}t{tt-2}_evac" if tt >= 2 else (
                f"vj{jc-1}t{tt+2}_evac" if jc >= 1 else None)
            vo_prev = f"vst_{jc}_{tt-2}" if tt >= 2 else (
                f"vst_{jc-1}_{tt+2}" if jc >= 1 else None)
            for k in range(DT_):
                w = []
                if k == 0:
                    w = ["wld_all"] + slab_ready
                    if pv_prev:
                        w.append(pv_prev)
                p.op("ten", lambda e, k=k, sl=sl, tt=tt, vbank=vbank: e.matmul(
                    vbank, lhsT=sl[:, k*512 + tt*128: k*512 + tt*128 + 128],
                    rhs=wv_b[:, k*512:(k+1)*512],
                    start=(k == 0), stop=(k == DT_ - 1)),
                    chain=(k == 0), waits=w, inc=(k == DT_ - 1),
                    mark=(f"{vtag}_mmdone" if k == DT_ - 1 else None))
            ww = [f"{vtag}_mmdone"] + ([vo_prev] if vo_prev else [])
            p.op("sca", lambda e, vo=vo, vbank=vbank: e.copy(out=vo[:], in_=vbank),
                 chain=True, waits=ww, mark=f"{vtag}_evac")
            p.dma("sca", "vst", (lambda e, vo=vo, q0=q0, tt=tt: e.dma_start(
                out=VT[q0 + tt*128: q0 + tt*128 + 128, :], in_=vo[:])),
                mark=f"vst_{jc}_{tt}", waits=[f"{vtag}_evac"], rot=2)
        p.mark_here("ten", f"slab_free{jc}")

    fin = [f"st_{qk_iter-1}", f"st_{qk_iter-2}"]
    fin += [f"vst_{TC-1}_{tt}" for tt in (2, 3)]
    p.barrier_wait("sync", fin)



def build_attn(nh_=NH, tc_=TC, nkv_=NKV):
    p = PB()
    nc = p.nc
    io = {"QT": nc.dram_tensor("QT", [D, T], BF16, kind="ExternalInput").ap(),
          "KT": nc.dram_tensor("KT", [512, T], BF16, kind="ExternalInput").ap(),
          "VT": nc.dram_tensor("VT", [T, 512], BF16, kind="ExternalInput").ap(),
          "YT": nc.dram_tensor("YT", [D, T], BF16, kind="ExternalOutput").ap()}
    emit_attn(p, io, nh_, tc_, nkv_)
    return p.finalize()


def emit_attn(p, io, nh_, tc_, nkv_):
    for _ in gen_attn(p, io, nh_, tc_, nkv_):
        pass


def gen_attn(p, io, nh_, tc_, nkv_):
    nc = p.nc
    QT = io["QT"]; KT = io["KT"]; VT = io["VT"]; YT = io["YT"]

    kbuf = p.salloc("kbuf", [128, T], BF16)
    vbuf = p.salloc("vbuf", [128, T], BF16)
    qch = [p.salloc(f"qch{j}", [128, 512], BF16) for j in range(2)]
    pt = p.salloc("ptslab", [128, 32 * 512], BF16)
    drow = p.salloc("drow", [1, 512], F32)
    rrow = p.salloc("rrow", [1, 512], F32)
    rh = p.salloc("rh", [1, 512], BF16)
    rl = p.salloc("rl", [1, 512], BF16)
    bcf = p.salloc("bcf", [128, 1024], F32)
    yout = [p.salloc(f"yout{j}", [128, 512], BF16) for j in range(2)]
    ones_c = p.salloc("ones_c", [128, 128], BF16)
    ones_r = p.salloc("ones_r", [1, 128], BF16)
    psA = p.palloc("psA", [128, 2048], F32)
    pden = p.palloc("pden", [128, 512], F32)
    pY = p.palloc("pY", [128, 1024], F32)
    pbc = pden

    p.op("vec", lambda e: e.memset(ones_c[:], 1.0), mark="ones_c")
    p.op("vec", lambda e: e.memset(ones_r[:], 1.0), mark="ones_r")
    sh_ap, sh_mk = p.constcol(ATT_SHIFT)
    for mk in ("kon0", "prevexp0"):
        p.marks[p.pre + mk] = ("kon", 0)

    it = 0
    all_tags = [f"a{h}_{jc}" for h in range(nh_) for jc in range(tc_)]
    for g in range(nkv_):
        kw = [f"attdone_g{g-1}"] if g >= 1 else []
        p.dma("sync", "kld", (lambda e, g=g: e.dma_start(
            out=kbuf[:], in_=KT[g*128:(g+1)*128, :])), mark="kld", waits=kw,
            rot=2, bucket=g)
        for part in range(8):
            p.dma("sync", "vld", (lambda e, g=g, part=part: e.dma_start(
                out=vbuf[:].rearrange("p (kk d) -> p kk d", d=128)[:, part*4:(part+1)*4, :],
                in_=VT[:, g*128:(g+1)*128].rearrange("(kk p) d -> p kk d", p=128)[:, part*4:(part+1)*4, :])),
                mark="vld", waits=kw, rot=2, bucket=g)
        for hh in range(min(GQ, nh_ - g * GQ)):
            h = g * GQ + hh
            for jc in range(tc_):
                q0 = jc * 512
                nk = 4 * (jc + 1)
                tag = f"a{h}_{jc}"
                qb = qch[it % 2]
                qw = ([all_tags[it - 2] + "_smm"] if it >= 2 else [])
                p.dma("sync", "qld", (lambda e, h=h, q0=q0, qb=qb: e.dma_start(
                    out=qb[:], in_=QT[h*128:(h+1)*128, q0:q0+512])),
                    mark=f"{tag}_q", waits=qw, rot=2, bucket=it)
                for grp in range(nk // 2):
                    half = (grp % 2) * 1024
                    gw = ["kld", f"{tag}_q", sh_mk]
                    if (grp, it) != (0, 0):
                        gw.append(f"{tag}_e{grp-2}" if grp >= 2 else f"prevexp{it}")
                    for m in range(2):
                        kkt = grp * 2 + m
                        p.op("ten", lambda e, kkt=kkt, m=m, qb=qb, half=half:
                             e.matmul(
                            psA[:, half + m*512: half + m*512 + 512],
                            lhsT=kbuf[:, kkt*128:(kkt+1)*128], rhs=qb[:],
                            start=True, stop=True), chain=(m == 0),
                            waits=gw if m == 0 else [], inc=(m == 1),
                            mark=f"{tag}_smm" if kkt == nk - 1 else None)
                    p.mark_here("ten", f"{tag}_sg{grp}")
                    p.op("sca", lambda e, grp=grp, half=half: e.activation(
                        out=pt[:, grp*1024:(grp+1)*1024],
                        in_=psA[:, half:half+1024],
                        func=AF.Exp, scale=ATT_SCALE, bias=sh_ap),
                        chain=True, waits=[f"{tag}_sg{grp}"],
                        mark=f"{tag}_e{grp}")
                p.marks[p.pre + f"prevexp{it+1}"] = p.marks[p.pre + f"{tag}_e{nk//2-1}"]
                for m in range(4):
                    kkt = 4 * jc + m
                    p.op("gps", lambda e, kkt=kkt, m=m: e.affine_select(
                        out=pt[:, kkt*512:(kkt+1)*512],
                        in_=pt[:, kkt*512:(kkt+1)*512],
                        pattern=[[1, 512]], compare_op=ALU.is_ge, fill=0.0,
                        base=-128*m, channel_multiplier=-1), chain=True,
                        waits=[f"{tag}_e{(4*jc+m)//2}"], mark=f"{tag}_mask{m}")
                denw = [(all_tags[it-1] + "_bcev") if it >= 1 else "kon0",
                        "ones_c"]
                pvw = [(all_tags[it-2] + "_ynorm") if it >= 2 else "kon0",
                       "vld"]
                pYh = pY[:, (it % 2)*512:(it % 2)*512 + 512]
                bcfh = bcf[:, (it % 2)*512:(it % 2)*512 + 512]
                for kkt in range(nk):
                    w = [f"{tag}_e{kkt//2}"]
                    if kkt >= 4 * jc:
                        w.append(f"{tag}_mask{kkt-4*jc}")
                    p.op("ten", lambda e, kkt=kkt, nk=nk: e.matmul(
                        pden[:], lhsT=ones_c[:],
                        rhs=pt[:, kkt*512:(kkt+1)*512],
                        start=(kkt == 0), stop=(kkt == nk - 1)),
                        chain=(kkt == 0),
                        waits=w + (denw if kkt == 0 else []),
                        inc=(kkt == nk - 1))
                p.mark_here("ten", f"{tag}_den")
                for kkt in range(nk):
                    p.op("ten", lambda e, kkt=kkt, nk=nk, pYh=pYh: e.matmul(
                        pYh, lhsT=vbuf[:, kkt*128:(kkt+1)*128],
                        rhs=pt[:, kkt*512:(kkt+1)*512],
                        start=(kkt == 0), stop=(kkt == nk - 1)),
                        chain=(kkt == 0),
                        waits=(pvw if kkt == 0 else []), inc=(kkt == nk - 1),
                        mark=f"{tag}_pv" if kkt == nk - 1 else None)
                p.op("sca", lambda e: e.copy(out=drow[:], in_=pden[0:1, :]),
                     chain=True, waits=[f"{tag}_den"], mark=f"{tag}_drowev")
                p.op("sca", lambda e: e.activation(out=drow[:], in_=drow[:],
                                                   func=AF.Ln), chain=True)
                p.op("sca", lambda e: e.activation(out=rrow[:], in_=drow[:],
                                                   func=AF.Exp, scale=-1.0),
                     chain=True, mark=f"{tag}_recip")
                rhw = [(all_tags[it-1] + "_bc") if it >= 1 else "kon0"]
                p.op("vec", lambda e: e.tensor_copy(rh[:], rrow[:]), chain=True,
                     waits=[f"{tag}_recip"] + rhw)
                p.op("vec", lambda e: e.tensor_tensor(out=rl[:], in0=rrow[:],
                                                      in1=rh[:], op=ALU.subtract),
                     chain=True, mark=f"{tag}_rhl")
                bcw = [(all_tags[it-2] + "_ynorm") if it >= 2 else "kon0"]
                p.op("ten", lambda e: e.matmul(pbc[:], lhsT=ones_r[:], rhs=rh[:],
                                               start=True, stop=False),
                     chain=True, waits=[f"{tag}_rhl", "ones_r"])
                p.op("ten", lambda e: e.matmul(pbc[:], lhsT=ones_r[:], rhs=rl[:],
                                               start=False, stop=True),
                     chain=True, mark=f"{tag}_bc")
                p.op("sca", lambda e, bcfh=bcfh: e.copy(out=bcfh, in_=pbc[:]),
                     chain=True, waits=[f"{tag}_bc"] + bcw, mark=f"{tag}_bcev")
                ystw = [(all_tags[it-2] + "_yst") if it >= 2 else "kon0"]
                p.op("vec", lambda e, yo=yout[it % 2], pYh=pYh, bcfh=bcfh:
                     e.tensor_tensor(
                    out=yo[:], in0=pYh, in1=bcfh, op=ALU.mult), chain=True,
                    waits=[f"{tag}_bcev", f"{tag}_pv"] + ystw,
                    mark=f"{tag}_ynorm")
                p.dma("sca", "yst", (lambda e, yo=yout[it % 2], h=h, q0=q0:
                                     e.dma_start(out=YT[h*128:(h+1)*128, q0:q0+512],
                                                 in_=yo[:])),
                    mark=f"{tag}_yst", waits=[f"{tag}_ynorm"], rot=2, bucket=it)
                it += 1
                yield
        p.mark_here("ten", f"attdone_g{g}")

    la = f"a{nh_-1}_{tc_-1}"
    p.barrier_wait("sync", [f"{la}_yst"] +
                   ([f"a{nh_-1}_{tc_-2}_yst"] if tc_ >= 2 else
                    ([f"a{nh_-2}_{tc_-1}_yst"] if nh_ >= 2 else [])))



def build_wo_ffn0():
    p = PB()
    nc = p.nc
    io = {"x": nc.dram_tensor("x", [T, D], F32, kind="ExternalInput").ap(),
          "xn_hi": nc.dram_tensor("xn_hi", [T, D], BF16, kind="ExternalInput").ap(),
          "YT": nc.dram_tensor("YT", [D, T], BF16, kind="ExternalInput").ap(),
          "wo": nc.dram_tensor("wo", [D, D], F32, kind="ExternalInput").ap(),
          "fw": nc.dram_tensor("fw", [D, D], F32, kind="ExternalInput").ap(),
          "fb": nc.dram_tensor("fb", [1, D], F32, kind="ExternalInput").ap(),
          "x1": nc.dram_tensor("x1", [T, D], F32, kind="ExternalOutput").ap()}
    emit_wo_ffn0(p, io)
    return p.finalize()


def emit_wo_ffn0(p, io):
    nc = p.nc
    X = io["x"]; XNH = io["xn_hi"]; YT = io["YT"]; WO = io["wo"]
    FW = io["fw"]; FB = io["fb"]; X1 = io["x1"]

    wo_b = p.salloc("wo_b", [128, DT_ * D], BF16)
    qT = p.salloc("qT", [128, DT_ * D], BF16)
    scr = p.salloc("scr", [128, 9216], F32)
    fwt = [scr[:, j*2048:(j+1)*2048] for j in range(2)]
    aw = scr[:, 4096:6144]
    sg = scr[:, 6144:8192]
    qb_ = scr[:, 8192:9216].bitcast(BF16)
    ysl = scr[:, 0:4096].bitcast(BF16)
    xsl = scr[:, 4096:8192].bitcast(BF16)
    combo = scr[:, 8192:8704]
    accum = [p.salloc(f"acc{j}", [128, D], F32) for j in range(4)]
    fbs = p.salloc("fbs", [128, 16], F32)
    fbr = p.salloc("fbr", [16, 128], F32)
    thr = p.salloc("thr", [128, 2], F32)
    idf = p.salloc("idf", [128, 128], F32)
    idb = p.salloc("idb", [128, 128], BF16)
    pA = p.palloc("pA", [128, 512], F32)
    pB = p.palloc("pB", [128, 512], F32)
    pC = p.palloc("pC", [128, 1024], F32)
    pQ = p.palloc("pQ", [128, 2048], BF16)

    p.op("gps", lambda e: e.memset(idf[:], 1.0), mark="id0")
    p.op("gps", lambda e: e.affine_select(
        out=idf[:], in_=idf[:], pattern=[[1, 128]], compare_op=ALU.is_equal,
        fill=0.0, base=0, channel_multiplier=-1), chain=True, waits=["id0"],
        mark="idf")
    p.op("vec", lambda e: e.tensor_copy(idb[:], idf[:]), waits=["idf"],
         mark="idb")
    p.dma("sync", "fbld", (lambda e: e.dma_start(
        out=fbr[:], in_=FB.rearrange("x (a b) -> (x a) b", b=128))),
        mark="fbld", rot=1)
    p.op("ten", lambda e: e.transpose(out=pC[:, 0:16], in_=fbr[:],
                                      identity=idf[0:16, 0:16]),
         waits=["fbld", "idf"], chain=True, mark="fbtr")
    p.op("sca", lambda e: e.copy(out=fbs[:], in_=pC[:, 0:16]), waits=["fbtr"],
         chain=True, mark="fbs")

    for ot in range(DT_):
        fb_ = fwt[ot % 2]
        lw = [f"tern{ot-2}"] if ot >= 2 else []
        p.dma("sync", "fwld", (lambda e, ot=ot, fb_=fb_: e.dma_start(
            out=fb_, in_=FW[ot*128:(ot+1)*128, :])), mark=f"fw{ot}",
            waits=lw, rot=2, bucket=ot)
        p.op("sca", lambda e, fb_=fb_: e.activation(
            out=aw, in_=fb_, func=AF.Abs, accum_out=thr[:, 0:1]),
            chain=True, waits=[f"fw{ot}", f"qdone{ot-1}"] if ot >= 1
            else [f"fw{ot}"], mark=f"aw{ot}")
        p.op("vec", lambda e: e.tensor_scalar_mul(thr[:, 1:2], thr[:, 0:1],
                                                  0.7 / D),
             chain=True, waits=[f"aw{ot}"], mark=f"thr{ot}")
        p.op("sca", lambda e, fb_=fb_: e.activation(out=sg, in_=fb_, func=AF.Sign),
             chain=True, mark=f"sg{ot}")
        p.op("vec", lambda e: e.tensor_scalar(out=aw, in0=aw, scalar1=thr[:, 1:2],
                                              scalar2=None, op0=ALU.is_gt),
             chain=True, waits=[f"thr{ot}"], mark=f"b01_{ot}")
        p.op("vec", lambda e: e.tensor_tensor(out=qb_[:, 0:2048], in0=aw, in1=sg,
                                              op=ALU.mult), chain=True,
             waits=[f"sg{ot}", f"b01_{ot}"], mark=f"tern{ot}")
        for k in range(DT_):
            p.op("ten", lambda e, k=k: e.transpose(
                out=pQ[:, (k % 2)*1024:(k % 2)*1024+128], in_=qb_[:, k*128:(k+1)*128],
                identity=idb[:]), chain=True,
                waits=[f"tern{ot}", "idb"] + ([f"qev{ot}_{k-2}"] if k >= 2 else
                      ([f"qev{ot-1}_{DT_-2+k}"] if ot >= 1 else [])),
                mark=f"qtr{ot}_{k}")
            p.op("sca", lambda e, k=k, ot=ot: e.copy(
                out=qT[:, k*D + ot*128: k*D + ot*128 + 128],
                in_=pQ[:, (k % 2)*1024:(k % 2)*1024+128]), chain=True,
                waits=[f"qtr{ot}_{k}"], mark=f"qev{ot}_{k}")
        p.mark_here("sca", f"qdone{ot}")

    for k in range(DT_):
        p.dma("gps", "wold", (lambda e, k=k: e.dma_start(
            out=wo_b[:, k*D:(k+1)*D], in_=WO[k*128:(k+1)*128, :])),
            mark="wold", rot=1)

    for jc in range(TC):
        q0 = jc * 512
        yw = [f"mmdone{jc-1}"] if jc >= 1 else [f"tern{DT_-1}", f"qdone{DT_-1}"]
        for k in range(DT_):
            p.dma("sync", "ysl", (lambda e, k=k, q0=q0: e.dma_start(
                out=ysl[:, k*512:(k+1)*512], in_=YT[k*128:(k+1)*128, q0:q0+512])),
                mark=f"ysl{jc}", waits=yw, rot=2, bucket=jc)
        for k in range(DT_):
            p.dma("sync", "xsl", (lambda e, k=k, q0=q0: e.dma_start(
                out=xsl[:, k*512:(k+1)*512],
                in_=XNH[q0:q0+512, k*128:(k+1)*128], transpose=True)),
                mark=f"xsl{jc}", waits=yw, rot=2, bucket=jc)
        for tt in range(4):
            p.dma("sync", "xld", (lambda e, tt=tt, q0=q0: e.dma_start(
                out=accum[tt][:], in_=X[q0 + tt*128: q0 + tt*128 + 128, :])),
                mark=f"xld{jc}", waits=[f"acst{jc-1}"] if jc >= 1 else [],
                rot=2, bucket=jc)
        for ot in range(DT_):
            tg = f"d{jc}_{ot}"
            prv = f"d{jc}_{ot-1}" if ot >= 1 else (
                f"d{jc-1}_{DT_-1}" if jc >= 1 else None)
            for suf, alias in (("_pAfree", "_pAnext"), ("_pBfree", "_pBnext"),
                               ("_cfree", "_cfree2")):
                p.marks[p.pre + f"{tg}{suf}"] = (p.marks[p.pre + f"{prv}{alias}"]
                                         if prv else ("kon", 0))
            for tt in range(4):
                p.marks[p.pre + f"{tg}_accfree{tt}"] = (p.marks[p.pre + f"{prv}_acc{tt}"]
                                                if prv else ("kon", 0))
            wow = [f"ysl{jc}", "wold", f"{tg}_pAfree"]
            for k in range(DT_):
                p.op("ten", lambda e, k=k, ot=ot: e.matmul(
                    pA[:], lhsT=wo_b[:, k*D + ot*128: k*D + ot*128 + 128],
                    rhs=ysl[:, k*512:(k+1)*512],
                    start=(k == 0), stop=(k == DT_ - 1)), chain=(k == 0),
                    waits=wow if k == 0 else [], inc=(k == DT_ - 1))
            p.mark_here("ten", f"{tg}_pA")
            fww = [f"xsl{jc}", f"{tg}_pBfree"]
            for k in range(DT_):
                p.op("ten", lambda e, k=k, ot=ot: e.matmul(
                    pB[:], lhsT=qT[:, k*D + ot*128: k*D + ot*128 + 128],
                    rhs=xsl[:, k*512:(k+1)*512],
                    start=(k == 0), stop=(k == DT_ - 1)), chain=(k == 0),
                    waits=fww if k == 0 else [], inc=(k == DT_ - 1))
            p.mark_here("ten", f"{tg}_pB")
            p.op("sca", lambda e, ot=ot: e.activation(
                out=combo, in_=pA[:], func=AF.Identity, bias=fbs[:, ot:ot+1]),
                chain=True, waits=[f"{tg}_pA", "fbs", f"{tg}_cfree"],
                mark=f"{tg}_ev1")
            p.op("vec", lambda e: e.tensor_tensor(out=combo, in0=combo, in1=pB[:],
                                                  op=ALU.add), chain=True,
                 waits=[f"{tg}_ev1", f"{tg}_pB"], mark=f"{tg}_ev2")
            p.marks[p.pre + f"{tg}_pAnext"] = p.marks[p.pre + f"{tg}_ev1"]
            p.marks[p.pre + f"{tg}_pBnext"] = p.marks[p.pre + f"{tg}_ev2"]
            for tt in range(4):
                slot = (tt % 2) * 512 + (tt // 2) * 128
                bw = ([f"{tg}_acc{tt-2}"] if tt >= 2 else
                      ([f"{prv}_acc{tt+2}"] if prv else []))
                p.op("ten", lambda e, tt=tt, slot=slot: e.transpose(
                    out=pC[:, slot:slot+128], in_=combo[:, tt*128:(tt+1)*128],
                    identity=idf[:]), chain=True,
                    waits=[f"{tg}_ev2"] + bw,
                    mark=f"{tg}_tr{tt}")
                p.op("vec", lambda e, tt=tt, ot=ot, slot=slot: e.tensor_tensor(
                    out=accum[tt][:, ot*128:(ot+1)*128],
                    in0=accum[tt][:, ot*128:(ot+1)*128],
                    in1=pC[:, slot:slot+128], op=ALU.add), chain=True,
                    waits=[f"{tg}_tr{tt}", f"xld{jc}"], mark=f"{tg}_acc{tt}")
            p.mark_here("vec", f"{tg}_cfree2")
        p.mark_here("ten", f"mmdone{jc}")
        for tt in range(4):
            p.dma("sca", "acst", (lambda e, tt=tt, q0=q0: e.dma_start(
                out=X1[q0 + tt*128: q0 + tt*128 + 128, :], in_=accum[tt][:])),
                mark=f"acst{jc}", waits=[f"d{jc}_{DT_-1}_acc{tt}"],
                rot=2, bucket=jc)

    p.barrier_wait("sync", [f"acst{TC-1}", f"acst{TC-2}"])



def build_xproj():
    p = PB()
    nc = p.nc
    io = {"xn_hi": nc.dram_tensor("xn_hi", [T, D], BF16, kind="ExternalInput").ap(),
          "W_x": nc.dram_tensor("W_x", [96, D], F32, kind="ExternalInput").ap(),
          "W_dt": nc.dram_tensor("W_dt", [D, 64], F32, kind="ExternalInput").ap(),
          "b_dt": nc.dram_tensor("b_dt", [1, D], F32, kind="ExternalInput").ap(),
          "dtv": nc.dram_tensor("dtv", [D, T], F32, kind="ExternalOutput").ap()}
    for nm in ("Bf", "Cf"):
        io[nm] = nc.dram_tensor(nm, [16, T], F32, kind="ExternalOutput").ap()
    emit_xproj(p, io)
    return p.finalize()


def emit_xproj(p, io):
    nc = p.nc
    XNH = io["xn_hi"]; WX = io["W_x"]; WDT = io["W_dt"]; BDT = io["b_dt"]
    DTV = io["dtv"]
    OB = {nm: io[nm] for nm in ("Bf", "Cf")}

    wxf = p.salloc("wxf", [96, D], F32)
    wxb = p.salloc("wxb", [96, D], BF16)
    wxT = p.salloc("wxT", [128, DT_ * 96], BF16)
    wdtf = p.salloc("wdtf", [128, DT_ * 64], F32)
    wdtb = p.salloc("wdtb", [128, DT_ * 64], BF16)
    wdtT = p.salloc("wdtT", [64, DT_ * 128], BF16)
    bdtr = p.salloc("bdtr", [16, 128], F32)
    bdts = p.salloc("bdts", [128, 16], F32)
    idb = p.salloc("idb", [128, 128], BF16)
    idf = p.salloc("idf", [128, 128], F32)
    xsl = [p.salloc(f"xsl{j}", [128, DT_ * 512], BF16) for j in range(2)]
    dtxb = p.salloc("dtxb", [64, 512], BF16)
    t_a = p.salloc("t_a", [128, 512], F32)
    t_r = p.salloc("t_r", [128, 512], F32)
    dtt = [p.salloc(f"dtt{j}", [128, 512], F32) for j in range(2)]
    bcf = p.salloc("bcf_", [16, 512], F32)
    pXd = p.palloc("pXd", [64, 512], F32)
    pXB = p.palloc("pXB", [16, 512], F32)
    pXC = p.palloc("pXC", [16, 512], F32)
    pdt = p.palloc("pdt", [128, 1024], F32)
    ptr = p.palloc("ptr", [128, 512], F32)
    ptrb = p.palloc("ptrb", [128, 2048], BF16)

    p.op("gps", lambda e: e.memset(idf[:], 1.0), mark="id0")
    p.op("gps", lambda e: e.affine_select(
        out=idf[:], in_=idf[:], pattern=[[1, 128]], compare_op=ALU.is_equal,
        fill=0.0, base=0, channel_multiplier=-1), chain=True, waits=["id0"],
        mark="idf")
    p.op("vec", lambda e: e.tensor_copy(idb[:], idf[:]), waits=["idf"],
         mark="idb")
    one_ap, one_mk = p.constcol(1.0)

    p.dma("sync", "wxl", (lambda e: e.dma_start(out=wxf[:], in_=WX)),
          mark="wxl", rot=1)
    p.op("sca", lambda e: e.copy(out=wxb[:], in_=wxf[:]), waits=["wxl"],
         chain=True, mark="wxb")
    for k in range(DT_):
        p.op("ten", lambda e, k=k: e.transpose(
            out=ptrb[:, (k % 2)*1024:(k % 2)*1024+96],
            in_=wxb[:, k*128:(k+1)*128], identity=idb[0:96, 0:96]),
            chain=True, waits=["wxb", "idb"] + ([f"wxev{k-2}"] if k >= 2 else []),
            mark=f"wxtr{k}")
        p.op("sca", lambda e, k=k: e.copy(
            out=wxT[:, k*96:(k+1)*96], in_=ptrb[:, (k % 2)*1024:(k % 2)*1024+96]),
            chain=True, waits=[f"wxtr{k}"], mark=f"wxev{k}")
    p.mark_here("sca", "wxT")
    for k in range(DT_):
        p.dma("sync", "wdl", (lambda e, k=k: e.dma_start(
            out=wdtf[:, k*64:(k+1)*64], in_=WDT[k*128:(k+1)*128, :])),
            mark="wdl", rot=1)
    p.op("sca", lambda e: e.copy(out=wdtb[:], in_=wdtf[:]), waits=["wdl"],
         chain=True, mark="wdtb")
    for k in range(DT_):
        p.op("ten", lambda e, k=k: e.transpose(
            out=ptrb[0:64, (k % 2)*1024:(k % 2)*1024+128],
            in_=wdtb[:, k*64:(k+1)*64], identity=idb[:]),
            chain=True, waits=["wdtb", "idb", f"wxev{DT_-1}"] +
            ([f"wdev{k-2}"] if k >= 2 else []),
            mark=f"wdtr{k}")
        p.op("sca", lambda e, k=k: e.copy(
            out=wdtT[:, k*128:(k+1)*128],
            in_=ptrb[0:64, (k % 2)*1024:(k % 2)*1024+128]), chain=True,
            waits=[f"wdtr{k}"], mark=f"wdev{k}")
    p.mark_here("sca", "wdtT")
    p.dma("sync", "bdl", (lambda e: e.dma_start(
        out=bdtr[:], in_=BDT.rearrange("x (a b) -> (x a) b", b=128))),
        mark="bdl", rot=1)
    p.op("ten", lambda e: e.transpose(out=ptr[:, 0:16], in_=bdtr[:],
                                      identity=idf[0:16, 0:16]),
         waits=["bdl", "idf", f"wdtr{DT_-1}"], chain=True, mark="bdtr2")
    p.op("sca", lambda e: e.copy(out=bdts[:], in_=ptr[:, 0:16]),
         waits=["bdtr2"], chain=True, mark="bdts")

    for jc in range(TC):
        q0 = jc * 512
        sl = xsl[jc % 2]
        lw = [f"xp_done{jc-2}"] if jc >= 2 else []
        for k in range(DT_):
            p.dma("sync", "xsl", (lambda e, k=k, q0=q0, sl=sl: e.dma_start(
                out=sl[:, k*512:(k+1)*512],
                in_=XNH[q0:q0+512, k*128:(k+1)*128], transpose=True)),
                mark=f"xsl{jc}", waits=lw, rot=2, bucket=jc)
        for gname, pdst, c0, cn in (("d", pXd, 0, 64), ("B", pXB, 64, 16),
                                    ("C", pXC, 80, 16)):
            if jc == 0:
                p.marks[p.pre + f"xp{gname}free{jc}"] = ("kon", 0)
            else:
                prevmk = {"d": f"dtxb{jc-1}", "B": f"Bf{jc-1}",
                          "C": f"Cf{jc-1}"}[gname]
                p.marks[p.pre + f"xp{gname}free{jc}"] = p.marks[p.pre + prevmk]
            gw = [f"xsl{jc}", "wxT", f"xp{gname}free{jc}"]
            for k in range(DT_):
                p.op("ten", lambda e, k=k, pdst=pdst, c0=c0, cn=cn, sl=sl:
                     e.matmul(pdst[:],
                              lhsT=wxT[:, k*96 + c0: k*96 + c0 + cn],
                              rhs=sl[:, k*512:(k+1)*512],
                              start=(k == 0), stop=(k == DT_ - 1)),
                     chain=(k == 0),
                     waits=gw if k == 0 else [], inc=(k == DT_ - 1))
            p.mark_here("ten", f"xp{gname}{jc}")
        p.op("sca", lambda e: e.copy(out=dtxb[:], in_=pXd[:]), chain=True,
             waits=[f"xpd{jc}"], mark=f"dtxb{jc}")
        for gname, psrc in (("B", pXB), ("C", pXC)):
            prevst = (f"Bst{jc}" if gname == "C" else
                      (f"Cst{jc-1}" if jc >= 1 else one_mk))
            p.op("sca", lambda e, psrc=psrc: e.copy(out=bcf[:], in_=psrc[:]),
                 chain=True, waits=[f"xp{gname}{jc}", prevst],
                 mark=f"{gname}f{jc}")
            p.dma("sca", "bst", (lambda e, gname=gname, q0=q0: e.dma_start(
                out=OB[gname + "f"][:, q0:q0+512], in_=bcf[:])),
                mark=f"{gname}st{jc}", waits=[f"{gname}f{jc}"], rot=2, bucket=jc)
        for ot in range(DT_):
            bank = pdt[:, (ot % 2)*512:(ot % 2)*512+512]
            dw = [f"dtxb{jc}", "wdtT", "bdts", f"dtev{jc}_{ot-2}" if ot >= 2 else
                  (f"dtev{jc-1}_{DT_-2+ot}" if jc >= 1 else one_mk)]
            p.op("ten", lambda e, ot=ot, bank=bank: e.matmul(
                bank, lhsT=wdtT[:, ot*128:(ot+1)*128], rhs=dtxb[:],
                start=True, stop=True), chain=True, waits=dw,
                mark=f"dtmm{jc}_{ot}")
            p.op("sca", lambda e, ot=ot, bank=bank: e.activation(
                out=t_r[:], in_=bank, func=AF.Relu, bias=bdts[:, ot:ot+1]),
                chain=True, waits=[f"dtmm{jc}_{ot}", f"dtst{jc}_{ot-2}"
                                   if ot >= 2 else (f"dtst{jc-1}_{DT_-2+ot}"
                                                    if jc >= 1 else one_mk)])
            p.op("sca", lambda e, ot=ot, bank=bank: e.activation(
                out=t_a[:], in_=bank, func=AF.Abs, bias=bdts[:, ot:ot+1]),
                chain=True, mark=f"dtev{jc}_{ot}")
            p.op("sca", lambda e: e.activation(out=t_a[:], in_=t_a[:],
                                               func=AF.Exp, scale=-1.0),
                 chain=True)
            p.op("sca", lambda e: e.activation(out=t_a[:], in_=t_a[:],
                                               func=AF.Ln, bias=one_ap),
                 chain=True, waits=[one_mk], mark=f"dtln{jc}_{ot}")
            p.op("vec", lambda e, ot=ot: e.tensor_tensor(
                out=dtt[ot % 2][:], in0=t_r[:], in1=t_a[:], op=ALU.add),
                chain=True, waits=[f"dtln{jc}_{ot}"], mark=f"dtadd{jc}_{ot}")
            p.dma("sca", "dtst", (lambda e, ot=ot, q0=q0: e.dma_start(
                out=DTV[ot*128:(ot+1)*128, q0:q0+512], in_=dtt[ot % 2][:])),
                mark=f"dtst{jc}_{ot}", waits=[f"dtadd{jc}_{ot}"], rot=2,
                bucket=ot)
        p.mark_here("ten", f"xp_done{jc}")

    fin = [f"dtst{TC-1}_{DT_-1}", f"dtst{TC-1}_{DT_-2}",
           f"Bst{TC-1}", f"Cst{TC-1}", f"Bst{TC-2}", f"Cst{TC-2}"]
    p.barrier_wait("sync", fin)



def build_dequant():
    p = PB()
    nc = p.nc
    io = {"gup": nc.dram_tensor("gup", [2 * DI, D // 8], I32, kind="ExternalInput").ap(),
          "gus": nc.dram_tensor("gus", [16, 2 * DI], F16, kind="ExternalInput").ap(),
          "dnp": nc.dram_tensor("dnp", [D, DI // 8], I32, kind="ExternalInput").ap(),
          "dns": nc.dram_tensor("dns", [44, D], F16, kind="ExternalInput").ap(),
          "wgu_d": nc.dram_tensor("wgu_d", [2 * DI, D], BF16, kind="ExternalOutput").ap(),
          "wdn_d": nc.dram_tensor("wdn_d", [D, DI], BF16, kind="ExternalOutput").ap()}
    emit_dequant(p, io)
    return p.finalize()


def emit_dequant(p, io):
    for _ in gen_dequant(p, io):
        pass


def gen_dequant(p, io):
    nc = p.nc
    GUP = io["gup"]; GUS = io["gus"]; DNP = io["dnp"]; DNS = io["dns"]
    WGU = io["wgu_d"]; WDN = io["wdn_d"]

    NT_GU = 2 * DI // 128
    NT_DN = D // 128

    gusf = p.salloc("gusf", [16, 2 * DI], F16)
    dnsf = p.salloc("dnsf", [44, D], F16)
    scnv = p.salloc("scnv", [44, 128], F32)
    sT = p.salloc("sT", [128, NT_GU * 16], F32)
    sTd = p.salloc("sTd", [128, NT_DN * 44], F32)
    idf = p.salloc("idf", [128, 128], F32)
    pk = [p.salloc(f"pk{j}", [128, 1408], I32) for j in range(2)]
    d_nib = p.salloc("d_nib", [128, 5632], I16)
    d_u = p.salloc("d_u", [128, 5632], I16)
    d_t = p.salloc("d_t", [128, 5632], I16)
    d_s = p.salloc("d_s", [128, 5632], I16)
    wout = [p.salloc(f"wout{j}", [128, 5632], BF16) for j in range(2)]
    ptr = p.palloc("ptr", [128, 512], F32)

    p.op("gps", lambda e: e.memset(idf[:], 1.0), mark="id0")
    p.op("gps", lambda e: e.affine_select(
        out=idf[:], in_=idf[:], pattern=[[1, 128]], compare_op=ALU.is_equal,
        fill=0.0, base=0, channel_multiplier=-1), chain=True, waits=["id0"],
        mark="idf")
    p.dma("sync", "sld", (lambda e: e.dma_start(out=gusf[:], in_=GUS)),
          mark="gusl", rot=1)
    p.dma("sync", "sld", (lambda e: e.dma_start(out=dnsf[:], in_=DNS)),
          mark="dnsl", rot=1)

    for i, (nt, srcf, dstT, rows) in enumerate(
            ((NT_GU, gusf, sT, 16), (NT_DN, dnsf, sTd, 44))):
        for k in range(nt):
            tg = f"s{i}_{k}"
            pw = [f"s{i}_{k-1}_ev"] if k >= 1 else (
                [f"s0_{NT_GU-1}_ev"] if i == 1 else [])
            p.op("sca", lambda e, k=k, srcf=srcf, rows=rows: e.copy(
                out=scnv[0:rows, :], in_=srcf[0:rows, k*128:(k+1)*128]),
                chain=True, waits=["dnsl"], mark=f"{tg}_cv")
            p.op("ten", lambda e, k=k, rows=rows: e.transpose(
                out=ptr[:, (k % 2)*256:(k % 2)*256+rows], in_=scnv[0:rows, :],
                identity=idf[0:rows, 0:rows]), chain=True,
                waits=[f"{tg}_cv", "idf"] + pw, mark=f"{tg}_tr")
            p.op("sca", lambda e, k=k, rows=rows, dstT=dstT: e.copy(
                out=dstT[:, k*rows:(k+1)*rows],
                in_=ptr[:, (k % 2)*256:(k % 2)*256+rows]), chain=True,
                waits=[f"{tg}_tr"], mark=f"{tg}_ev")
            if k % 8 == 7:
                yield
    p.mark_here("sca", "scales")

    def decode(i, nt, PKD, ncols_i32, sTx, srows, WOUT, hdim):
        for k in range(nt):
            tg = f"q{i}_{k}"
            buf = pk[k % 2]
            lw = [f"q{i}_{k-2}_nib"] if k >= 2 else (
                [f"q0_{NT_GU-2+k}_nib"] if i == 1 else [])
            p.dma("sync", "pkl", (lambda e, k=k, buf=buf, PKD=PKD,
                                  ncols_i32=ncols_i32: e.dma_start(
                out=buf[:, 0:ncols_i32], in_=PKD[k*128:(k+1)*128, :])),
                mark=f"{tg}_ld", waits=lw, rot=2, bucket=k)
            p16 = buf[:, 0:ncols_i32].bitcast(I16)
            nibv = d_nib[:, 0:hdim].rearrange("p (w f) -> p w f", f=4)
            for f4 in range(4):
                p.op("vec", lambda e, f4=f4, p16=p16, nibv=nibv: e.tensor_scalar(
                    out=nibv[:, :, f4], in0=p16, scalar1=4 * f4, scalar2=15,
                    op0=ALU.logical_shift_right, op1=ALU.bitwise_and),
                    chain=True, waits=[f"{tg}_ld"] if f4 == 0 else [])
            p.mark_here("vec", f"{tg}_nib")
            nib = d_nib[:, 0:hdim]
            u = d_u[:, 0:hdim]
            t_ = d_t[:, 0:hdim]
            s_ = d_s[:, 0:hdim]
            p.op("vec", lambda e, nib=nib, u=u: e.tensor_scalar(
                out=u, in0=nib, scalar1=7, scalar2=None, op0=ALU.bitwise_and),
                chain=True)
            p.op("vec", lambda e, nib=nib, s_=s_: e.tensor_scalar(
                out=s_, in0=nib, scalar1=8, scalar2=12, op0=ALU.bitwise_and,
                op1=ALU.logical_shift_left), chain=True)
            p.op("vec", lambda e, u=u, t_=t_: e.tensor_scalar(
                out=t_, in0=u, scalar1=2, scalar2=None, op0=ALU.is_ge),
                chain=True)
            p.op("vec", lambda e, u=u, t_=t_: e.tensor_tensor(
                out=t_, in0=t_, in1=u, op=ALU.add), chain=True)
            p.op("vec", lambda e, u=u, nib=nib: e.tensor_scalar(
                out=nib, in0=u, scalar1=1, scalar2=251, op0=ALU.is_ge,
                op1=ALU.mult), chain=True)
            p.op("vec", lambda e, nib=nib, t_=t_: e.tensor_tensor(
                out=t_, in0=t_, in1=nib, op=ALU.add), chain=True)
            p.op("vec", lambda e, t_=t_, s_=s_: e.tensor_scalar(
                out=t_, in0=t_, scalar1=6, scalar2=None,
                op0=ALU.logical_shift_left), chain=True)
            p.op("vec", lambda e, t_=t_, s_=s_: e.tensor_tensor(
                out=t_, in0=t_, in1=s_, op=ALU.bitwise_or), chain=True,
                mark=f"{tg}_bits")
            wo_ = wout[k % 2]
            ng = hdim // 128
            sw = [f"{tg}_bits", "scales"] + ([f"q{i}_{k-2}_st"] if k >= 2 else
                  ([f"q0_{NT_GU-2+k}_st"] if i == 1 else []))
            p.op("vec", lambda e, t_=t_, wo_=wo_, k=k, sTx=sTx, srows=srows,
                 ng=ng, hdim=hdim: e.tensor_tensor(
                out=wo_[:, 0:hdim].rearrange("p (g i) -> p g i", i=128),
                in0=t_.bitcast(BF16).rearrange("p (g i) -> p g i", i=128),
                in1=sTx[:, k*srows:k*srows+ng].unsqueeze(-1).broadcast_to(
                    [128, ng, 128]),
                op=ALU.mult), chain=True, waits=sw, mark=f"{tg}_w")
            p.dma("sca", "wst", (lambda e, k=k, wo_=wo_, WOUT=WOUT,
                                 hdim=hdim: e.dma_start(
                out=WOUT[k*128:(k+1)*128, :], in_=wo_[:, 0:hdim])),
                mark=f"{tg}_st", waits=[f"{tg}_w"], rot=2, bucket=k)
            yield

    yield from decode(0, NT_GU, GUP, D // 8, sT, 16, WGU, D)
    yield from decode(1, NT_DN, DNP, DI // 8, sTd, 44, WDN, DI)
    p.barrier_wait("sync", [f"q1_{NT_DN-1}_st", f"q1_{NT_DN-2}_st"])



def build_scan():
    p = PB()
    build_scan.last_pb = p
    nc = p.nc
    io = {"dtv": nc.dram_tensor("dtv", [D, T], F32, kind="ExternalInput").ap(),
          "xn_hi": nc.dram_tensor("xn_hi", [T, D], BF16, kind="ExternalInput").ap(),
          "xn_lo": nc.dram_tensor("xn_lo", [T, D], BF16, kind="ExternalInput").ap(),
          "A_log": nc.dram_tensor("A_log", [D, NST], F32, kind="ExternalInput").ap(),
          "Dv": nc.dram_tensor("Dv", [1, D], F32, kind="ExternalInput").ap(),
          "yssm_hi": nc.dram_tensor("yssm_hi", [D, T], BF16, kind="ExternalOutput").ap(),
          "yssm_lo": nc.dram_tensor("yssm_lo", [D, T], BF16, kind="ExternalOutput").ap()}
    for nm in ("Bf", "Cf"):
        io[nm] = nc.dram_tensor(nm, [16, T], F32, kind="ExternalInput").ap()
    emit_scan(p, io)
    return p.finalize()


def emit_scan(p, io):
    nc = p.nc
    DTV = io["dtv"]; XNH = io["xn_hi"]; XNL = io["xn_lo"]
    AL = io["A_log"]; DV = io["Dv"]
    Bf = io["Bf"]; Cf = io["Cf"]
    YH = io["yssm_hi"]; YL = io["yssm_lo"]

    CSZ = 2048
    NCH = T // CSZ
    JG = 4

    dtsl = p.salloc("dtsl", [128, JG * CSZ], F32)
    zsl = p.salloc("zsl", [128, JG * CSZ], F32)
    ysl = p.salloc("ysl", [128, JG * CSZ], F32)
    xnh_t = p.salloc("xnh_t", [128, CSZ], BF16)
    xnl_t = p.salloc("xnl_t", [128, CSZ], BF16)
    xnf = p.salloc("xnf", [128, CSZ], F32)
    Asb = p.salloc("Asb", [128, DT_ * NST], F32)
    Dr = p.salloc("Dr", [16, 128], F32)
    Ds = p.salloc("Ds", [128, 16], F32)
    Bbc = p.salloc("Bbc", [128, CSZ], F32)
    Cbc = p.salloc("Cbc", [128, CSZ], F32)
    dA = [p.salloc(f"dA{i}", [128, CSZ], F32) for i in range(2)]
    ub2 = [p.salloc(f"ub{i}", [128, CSZ], F32) for i in range(2)]
    hb = [p.salloc(f"hb{i}", [128, CSZ], F32) for i in range(2)]
    hc = [p.salloc(f"hc{i}", [128, CSZ], F32) for i in range(2)]
    carry = p.salloc("carry", [128, DT_ * NST], F32)
    yh = p.salloc("yh", [128, CSZ], BF16)
    yl = p.salloc("yl", [128, CSZ], BF16)
    idf = p.salloc("idf", [128, 128], F32)
    ptr = p.palloc("ptr", [128, 512], F32)

    p.op("gps", lambda e: e.memset(idf[:], 1.0), mark="id0")
    p.op("gps", lambda e: e.affine_select(
        out=idf[:], in_=idf[:], pattern=[[1, 128]], compare_op=ALU.is_equal,
        fill=0.0, base=0, channel_multiplier=-1), chain=True, waits=["id0"],
        mark="idf")
    p.op("gps", lambda e: e.memset(carry[:], 0.0), chain=True, mark="carry0")
    p.alias("setup", "carry0")
    for j in range(DT_):
        p.dma("sync", "ald", (lambda e, j=j: e.dma_start(
            out=Asb[:, j*NST:(j+1)*NST], in_=AL[j*128:(j+1)*128, :])),
            mark="ald", rot=1)
    p.dma("sync", "dld", (lambda e: e.dma_start(
        out=Dr[:], in_=DV.rearrange("x (a b) -> (x a) b", b=128))),
        mark="dld", rot=1)
    p.op("sca", lambda e: e.activation(out=Asb[:], in_=Asb[:], func=AF.Exp),
         waits=["ald"], chain=True, mark="aexp")
    p.op("vec", lambda e: e.tensor_scalar_mul(Asb[:], Asb[:], -1.0),
         waits=["aexp"], chain=True, mark="Asb")
    p.op("ten", lambda e: e.transpose(out=ptr[:, 0:16], in_=Dr[:],
                                      identity=idf[0:16, 0:16]),
         waits=["dld", "idf"], chain=True, mark="dtr")
    p.op("sca", lambda e: e.copy(out=Ds[:], in_=ptr[:, 0:16]), waits=["dtr"],
         chain=True, mark="Ds")

    all_it = []
    for c_ in range(NCH):
        for jg_ in range(DT_ // JG):
            for n_ in range(NST):
                for jj_ in range(JG):
                    all_it.append(f"s{c_}_{jg_}_{n_}_{jj_}")

    pend_cy = []
    pend_hcm = []
    state = {"last_hcm": None}

    def emit_hcm_yadd(p_, rec):
        hit, htg, hgtag, hjj, hhbb = rec
        hcb = hc[hit % 2]
        heng = "vec" if hit % 4 == 0 else "gps"
        p_.op(heng, lambda e, hhbb=hhbb, hcb=hcb: e.tensor_tensor(
            out=hcb[:], in0=hhbb[:], in1=Cbc[:], op=ALU.mult), chain=True,
            waits=[f"sc{htg}", f"Cb{htg[1:htg.rindex('_')]}"] +
            ([f"yy{all_it[hit-2]}"] if hit >= 2 else []),
            mark=f"hcm{htg}")
        p_.alias(f"hh{htg}", f"hcm{htg}")
        state["last_hcm"] = f"hcm{htg}"
        p_.op("vec", lambda e, hjj=hjj, hcb=hcb: e.tensor_tensor(
            out=ysl[:, hjj*CSZ:(hjj+1)*CSZ],
            in0=ysl[:, hjj*CSZ:(hjj+1)*CSZ], in1=hcb[:],
            op=ALU.add), chain=True,
            waits=[f"hcm{htg}", f"zb{hgtag}_{hjj}"], mark=f"yy{htg}")

    def emit_cy(p_, rec):
        ctg, cj, cn, chbb = rec
        p_.op("sca", lambda e, cj=cj, cn=cn, chbb=chbb: e.copy(
            out=carry[:, cj*NST+cn:cj*NST+cn+1], in_=chbb[:, CSZ-1:CSZ]),
            chain=True, waits=[f"sc{ctg}"], mark=f"cy{ctg}")

    it = 0
    for c in range(NCH):
        c0 = c * CSZ
        for jg in range(DT_ // JG):
            gtag = f"g{c}_{jg}"
            for jj in range(JG):
                j = jg * JG + jj
                jw = ([f"uu{c}_{jg-1}_15_{JG-1}", f"das{c}_{jg-1}_15_{JG-1}"]
                      if jg >= 1 else
                      ([f"uu{c-1}_{DT_//JG-1}_15_{JG-1}",
                        f"das{c-1}_{DT_//JG-1}_15_{JG-1}"] if c >= 1
                       else ["setup"]))
                p.dma("sync", "dtl", (lambda e, j=j, jj=jj, c0=c0: e.dma_start(
                    out=dtsl[:, jj*CSZ:(jj+1)*CSZ],
                    in_=DTV[j*128:(j+1)*128, c0:c0+CSZ])),
                    mark=f"dtl{gtag}", waits=jw, rot=2,
                    bucket=c * 4 + jg)
            for jj in range(JG):
                j = jg * JG + jj
                xw = [f"xnf{gtag}_{jj-1}"] if jj >= 1 else (
                    [f"xnfg{c}_{jg-1}_{JG-1}"] if jg >= 1 else (
                        [f"xnfg{c-1}_{DT_//JG-1}_{JG-1}"] if c >= 1 else ["setup"]))
                p.dma("sync", "xh", (lambda e, j=j, c0=c0: e.dma_start(
                    out=xnh_t[:], in_=XNH[c0:c0+CSZ, j*128:(j+1)*128],
                    transpose=True)), mark=f"xh{gtag}_{jj}", waits=xw, rot=2,
                    bucket=it + jj)
                p.dma("sync", "xl2", (lambda e, j=j, c0=c0: e.dma_start(
                    out=xnl_t[:], in_=XNL[c0:c0+CSZ, j*128:(j+1)*128],
                    transpose=True)), mark=f"xl{gtag}_{jj}", waits=xw, rot=2,
                    bucket=it + jj)
                p.op("vec", lambda e: e.tensor_tensor(out=xnf[:], in0=xnh_t[:],
                                                      in1=xnl_t[:], op=ALU.add),
                     chain=True, waits=[f"xh{gtag}_{jj}", f"xl{gtag}_{jj}"],
                     mark=f"xnf{gtag}_{jj}")
                p.op("vec", lambda e, jj=jj: e.tensor_tensor(
                    out=zsl[:, jj*CSZ:(jj+1)*CSZ],
                    in0=dtsl[:, jj*CSZ:(jj+1)*CSZ],
                    in1=xnf[:], op=ALU.mult), chain=True,
                    waits=[f"dtl{gtag}", f"uu{c-1}_{jg}_15_{jj}" if c >= 1
                           else "setup"])
                p.op("vec", lambda e, jj=jj, j=j: e.tensor_scalar_mul(
                    ysl[:, jj*CSZ:(jj+1)*CSZ], xnf[:],
                    Ds[:, j:j+1]), chain=True,
                    waits=["Ds", f"ystl{c-1}_{jg}_{jj}" if c >= 1 else "setup"],
                    mark=f"zb{gtag}_{jj}")
            for n in range(NST):
                ntag = f"{gtag}_{n}"
                pbw = ([f"uu{c}_{jg}_{n-1}_{JG-1}"] if n >= 1 else
                       ([f"uu{c}_{jg-1}_15_{JG-1}"] if jg >= 1 else
                        ([f"uu{c-1}_{DT_//JG-1}_15_{JG-1}"] if c >= 1 else
                         ["setup"])))
                p.dma("sync", "bbc", (lambda e, n=n, c0=c0: e.dma_start(
                    out=Bbc[:], in_=Bf[n:n+1, c0:c0+CSZ].partition_broadcast(128))),
                    mark=f"Bbc{ntag}", waits=pbw, rot=2, bucket=it)
                for jj in range(JG):
                    j = jg * JG + jj
                    tg = f"s{c}_{jg}_{n}_{jj}"
                    da = dA[it % 2]
                    hbb = hb[it % 2]
                    ubb = ub2[it % 2]
                    p.op("sca", lambda e, jj=jj, j=j, n=n, da=da: e.activation(
                        out=da[:], in_=dtsl[:, jj*CSZ:(jj+1)*CSZ],
                        func=AF.Exp, scale=Asb[:, j*NST+n:j*NST+n+1]),
                        chain=True,
                        waits=[f"dtl{gtag}", "Asb"] +
                        ([f"sc{all_it[it-2]}"] if it >= 2 else []),
                        mark=f"da{tg}")
                    if pend_cy:
                        emit_cy(p, pend_cy.pop(0))
                    p.op("gps", lambda e, jj=jj, ubb=ubb: e.tensor_tensor(
                        out=ubb[:], in0=zsl[:, jj*CSZ:(jj+1)*CSZ],
                        in1=Bbc[:], op=ALU.mult), chain=True,
                        waits=[f"Bbc{ntag}", f"zb{gtag}_{jj}", "carry0"] +
                        ([f"sc{all_it[it-2]}"] if it >= 2 else []),
                        mark=f"uu{c}_{jg}_{n}_{jj}")
                    p.op("vec", lambda e, da=da, hbb=hbb, ubb=ubb, j=j, n=n:
                         e.tensor_tensor_scan(
                        out=hbb[:], data0=da[:], data1=ubb[:],
                        initial=carry[:, j*NST+n:j*NST+n+1],
                        op0=ALU.mult, op1=ALU.add), chain=True,
                        waits=[f"da{tg}", f"uu{c}_{jg}_{n}_{jj}"] +
                        ([f"hh{all_it[it-2]}", f"cy{all_it[it-2]}"]
                         if it >= 2 else []) +
                        ([f"cys{c-1}_{jg}_{n}_{jj}"] if c >= 1 else []),
                        mark=f"sc{tg}")
                    pend_cy.append((tg, j, n, hbb))
                    if pend_hcm:
                        emit_hcm_yadd(p, pend_hcm.pop(0))
                    if jj == 0:
                        p.dma("sync", "cbc", (lambda e, n=n, c0=c0: e.dma_start(
                            out=Cbc[:],
                            in_=Cf[n:n+1, c0:c0+CSZ].partition_broadcast(128))),
                            mark=f"Cb{c}_{jg}_{n}",
                            waits=[state["last_hcm"] or "setup"],
                            rot=2, bucket=it)
                    pend_hcm.append((it, tg, gtag, jj, hbb))
                    it += 1
            while pend_hcm:
                emit_hcm_yadd(p, pend_hcm.pop(0))
            while pend_cy:
                emit_cy(p, pend_cy.pop(0))
            for jj in range(JG):
                j = jg * JG + jj
                pst = ((c, jg, jj - 1) if jj >= 1 else
                       ((c, jg - 1, JG - 1) if jg >= 1 else
                        ((c - 1, DT_//JG - 1, JG - 1) if c >= 1 else None)))
                sw = [f"yys{c}_{jg}_{NST-1}_{jj}"]
                if pst:
                    sw.append(f"ysth{pst[0]}_{pst[1]}_{pst[2]}")
                p.op("vec", lambda e, jj=jj: e.tensor_copy(
                    yh[:], ysl[:, jj*CSZ:(jj+1)*CSZ]),
                     chain=True, waits=sw, mark=f"yhi{gtag}_{jj}")
                p.op("vec", lambda e, jj=jj: e.tensor_tensor(
                    out=yl[:], in0=ysl[:, jj*CSZ:(jj+1)*CSZ], in1=yh[:],
                    op=ALU.subtract), chain=True,
                    waits=[f"ystl{pst[0]}_{pst[1]}_{pst[2]}"] if pst else [],
                    mark=f"ylo{gtag}_{jj}")
                p.dma("sca", "ysh", (lambda e, j=j, c0=c0: e.dma_start(
                    out=YH[j*128:(j+1)*128, c0:c0+CSZ], in_=yh[:])),
                    mark=f"ysth{c}_{jg}_{jj}", waits=[f"yhi{gtag}_{jj}"],
                    rot=2, bucket=it + jj)
                p.dma("sca", "ysl2", (lambda e, j=j, c0=c0: e.dma_start(
                    out=YL[j*128:(j+1)*128, c0:c0+CSZ], in_=yl[:])),
                    mark=f"ystl{c}_{jg}_{jj}", waits=[f"ylo{gtag}_{jj}"],
                    rot=2, bucket=it + jj)

    fin = [f"ysth{NCH-1}_{DT_//JG-1}_{JG-1}", f"ystl{NCH-1}_{DT_//JG-1}_{JG-1}",
           f"ysth{NCH-1}_{DT_//JG-1}_{JG-2}", f"ystl{NCH-1}_{DT_//JG-1}_{JG-2}"]
    p.barrier_wait("sync", fin)

def build_gateup():
    p = PB()
    nc = p.nc
    io = {"xn_hi": nc.dram_tensor("xn_hi", [T, D], BF16, kind="ExternalInput").ap(),
          "wgu_d": nc.dram_tensor("wgu_d", [2 * DI, D], BF16, kind="ExternalInput").ap(),
          "hact": nc.dram_tensor("hact", [DI, T], BF16, kind="ExternalOutput").ap()}
    emit_gateup(p, io)
    return p.finalize()


def emit_gateup(p, io):
    nc = p.nc
    XNH = io["xn_hi"]; WGU = io["wgu_d"]; HACT = io["hact"]

    NSP = DI // 512

    xnf = p.salloc("xnf", [128, DT_ * T], BF16)
    wg = [p.salloc(f"wg{j}", [128, DT_ * 512], BF16) for j in range(2)]
    wu = [p.salloc(f"wu{j}", [128, DT_ * 512], BF16) for j in range(2)]
    sg = p.salloc("sg", [128, 512], F32)
    t1 = p.salloc("t1", [128, 512], F32)
    ht = [p.salloc(f"ht{j}", [128, 512], BF16) for j in range(2)]
    pG = p.palloc("pG", [128, 1024], F32)
    pU = p.palloc("pU", [128, 1024], F32)

    for k in range(DT_):
        p.dma("sync", "xnl", (lambda e, k=k: e.dma_start(
            out=xnf[:, k*T:(k+1)*T], in_=XNH[:, k*128:(k+1)*128],
            transpose=True)), mark="xnl", rot=1)

    it = 0
    all_h = [f"h{sp}_{o4}_{jc}" for sp in range(NSP) for o4 in range(4)
             for jc in range(TC)]
    for sp in range(NSP):
        wgb, wub = wg[sp % 2], wu[sp % 2]
        lw = [f"sp_done{sp-2}"] if sp >= 2 else []
        for k in range(DT_):
            p.dma("sync", "wgl", (lambda e, k=k, sp=sp, wgb=wgb: e.dma_start(
                out=wgb[:, k*512:(k+1)*512],
                in_=WGU[sp*512:(sp+1)*512, k*128:(k+1)*128], transpose=True)),
                mark=f"wgl{sp}", waits=lw, rot=2, bucket=sp)
            p.dma("sync", "wul", (lambda e, k=k, sp=sp, wub=wub: e.dma_start(
                out=wub[:, k*512:(k+1)*512],
                in_=WGU[DI + sp*512: DI + sp*512 + 512, k*128:(k+1)*128],
                transpose=True)), mark=f"wul{sp}", waits=lw, rot=2, bucket=sp)
        for o4 in range(4):
            for jc in range(TC):
                tg = f"h{sp}_{o4}_{jc}"
                gb = pG[:, (it % 2)*512:(it % 2)*512+512]
                ub_ = pU[:, (it % 2)*512:(it % 2)*512+512]
                prev = it - 2
                gw = [f"wgl{sp}", f"wul{sp}", "xnl"]
                if prev >= 0:
                    gw += [all_h[prev] + "_sg"]
                uw = [all_h[prev] + "_h"] if prev >= 0 else []
                for k in range(DT_):
                    p.op("ten", lambda e, k=k, o4=o4, jc=jc, gb=gb, wgb=wgb:
                         e.matmul(gb, lhsT=wgb[:, k*512 + o4*128: k*512 + o4*128 + 128],
                                  rhs=xnf[:, k*T + jc*512: k*T + jc*512 + 512],
                                  start=(k == 0), stop=(k == DT_ - 1)),
                         chain=(k == 0), waits=gw if k == 0 else [],
                         inc=(k == DT_ - 1))
                p.mark_here("ten", f"{tg}_g")
                for k in range(DT_):
                    p.op("ten", lambda e, k=k, o4=o4, jc=jc, ub_=ub_, wub=wub:
                         e.matmul(ub_, lhsT=wub[:, k*512 + o4*128: k*512 + o4*128 + 128],
                                  rhs=xnf[:, k*T + jc*512: k*T + jc*512 + 512],
                                  start=(k == 0), stop=(k == DT_ - 1)),
                         chain=(k == 0), waits=uw if k == 0 else [],
                         inc=(k == DT_ - 1))
                p.mark_here("ten", f"{tg}_u")
                p.marks[p.pre + f"{tg}_t1free"] = (p.marks[p.pre + all_h[it - 1] + "_sg"]
                                           if it >= 1 else ("kon", 0))
                p.op("sca", lambda e, gb=gb: e.activation(
                    out=sg[:], in_=gb, func=AF.Sigmoid), chain=True,
                    waits=[f"{tg}_g", f"{tg}_t1free"], mark=f"{tg}_sg0")
                p.op("vec", lambda e, gb=gb: e.tensor_tensor(
                    out=t1[:], in0=gb, in1=sg[:], op=ALU.mult), chain=True,
                    waits=[f"{tg}_sg0"], mark=f"{tg}_sg")
                hb_ = ht[it % 2]
                hw = [all_h[prev] + "_st"] if prev >= 0 else []
                p.op("vec", lambda e, ub_=ub_, hb_=hb_: e.tensor_tensor(
                    out=hb_[:], in0=t1[:], in1=ub_, op=ALU.mult), chain=True,
                    waits=[f"{tg}_u"] + hw, mark=f"{tg}_h")
                p.dma("sca", "hst", (lambda e, sp=sp, o4=o4, jc=jc, hb_=hb_:
                                     e.dma_start(
                    out=HACT[sp*512 + o4*128: sp*512 + o4*128 + 128,
                             jc*512:(jc+1)*512], in_=hb_[:])),
                    mark=f"{tg}_st", waits=[f"{tg}_h"], rot=2, bucket=it)
                it += 1
        p.mark_here("ten", f"sp_done{sp}")

    p.barrier_wait("sync", [all_h[-1] + "_st", all_h[-2] + "_st"])



def build_down():
    p = PB()
    nc = p.nc
    io = {"x1": nc.dram_tensor("x1", [T, D], F32, kind="ExternalInput").ap(),
          "hact": nc.dram_tensor("hact", [DI, T], BF16, kind="ExternalInput").ap(),
          "wdn_d": nc.dram_tensor("wdn_d", [D, DI], BF16, kind="ExternalInput").ap(),
          "yssm_hi": nc.dram_tensor("yssm_hi", [D, T], BF16, kind="ExternalInput").ap(),
          "yssm_lo": nc.dram_tensor("yssm_lo", [D, T], BF16, kind="ExternalInput").ap(),
          "out": nc.dram_tensor("out", [T, D], F32, kind="ExternalOutput").ap()}
    emit_down(p, io)
    return p.finalize()


def emit_down(p, io):
    nc = p.nc
    X1 = io["x1"]; HACT = io["hact"]; WDN = io["wdn_d"]
    YH = io["yssm_hi"]; YL = io["yssm_lo"]; OUT = io["out"]

    NI = DI // 128

    hsl = p.salloc("hsl", [128, NI * 512], BF16)
    wdn0 = p.salloc("wdn0", [128, NI * 1024], BF16)
    accy = [p.salloc(f"accy{j}", [128, D], F32) for j in range(4)]
    accr = [p.salloc(f"accr{j}", [128, D], F32) for j in range(4)]
    ytmp = p.salloc("ytmp", [128, D], BF16)
    cmb = p.salloc("cmb", [128, 512], F32)
    st_ = p.salloc("st_", [128, 4], F32)
    idf = p.salloc("idf", [128, 128], F32)
    pD = p.palloc("pD", [128, 1024], F32)
    pT = p.palloc("pT", [128, 1024], F32)
    sqd = p.palloc("sqd", [128, 2048], F32)

    p.op("gps", lambda e: e.memset(idf[:], 1.0), mark="id0")
    p.op("gps", lambda e: e.affine_select(
        out=idf[:], in_=idf[:], pattern=[[1, 128]], compare_op=ALU.is_equal,
        fill=0.0, base=0, channel_multiplier=-1), chain=True, waits=["id0"],
        mark="idf")
    eps_ap, eps_mk = p.constcol(RMS_EPS)

    for jc in range(TC):
        q0 = jc * 512
        lw = [f"ffdone{jc-1}"] if jc >= 1 else []
        for it_ in range(NI):
            p.dma("sync", "hl", (lambda e, it_=it_, q0=q0: e.dma_start(
                out=hsl[:, it_*512:(it_+1)*512],
                in_=HACT[it_*128:(it_+1)*128, q0:q0+512])),
                mark=f"hl{jc}", waits=lw, rot=2, bucket=jc)
        for tt in range(4):
            aw = [f"outst{jc-1}"] if jc >= 1 else []
            p.dma("sync", "x1l", (lambda e, tt=tt, q0=q0: e.dma_start(
                out=accr[tt][:], in_=X1[q0 + tt*128: q0 + tt*128 + 128, :])),
                mark=f"x1l{jc}_{tt}", waits=aw, rot=2, bucket=jc * 4 + tt)
            p.dma("sync", "yhl", (lambda e, tt=tt, q0=q0: e.dma_start(
                out=ytmp[:], in_=YH[:, q0 + tt*128: q0 + tt*128 + 128],
                transpose=True)), mark=f"yhl{jc}_{tt}",
                waits=[f"accr{jc}_{tt-1}" if tt >= 1 else
                       (f"accr{jc-1}_3" if jc >= 1 else "idf")],
                rot=2, bucket=jc * 4 + tt)
            p.op("vec", lambda e, tt=tt: e.tensor_tensor(
                out=accr[tt][:], in0=accr[tt][:], in1=ytmp[:], op=ALU.add),
                chain=True, waits=[f"x1l{jc}_{tt}", f"yhl{jc}_{tt}"],
                mark=f"yhread{jc}_{tt}")
            p.dma("sync", "yll", (lambda e, tt=tt, q0=q0: e.dma_start(
                out=ytmp[:], in_=YL[:, q0 + tt*128: q0 + tt*128 + 128],
                transpose=True)), mark=f"yll{jc}_{tt}",
                waits=[f"yhread{jc}_{tt}"], rot=2, bucket=jc * 4 + tt)
            p.op("vec", lambda e, tt=tt: e.tensor_tensor(
                out=accr[tt][:], in0=accr[tt][:], in1=ytmp[:], op=ALU.add),
                chain=True, waits=[f"yll{jc}_{tt}"], mark=f"accr{jc}_{tt}")
        it2 = 0
        for hsp in range(4):
            wb = wdn0
            if hsp % 2 == 0:
                hp = hsp // 2
                wlw = [f"mmdone{jc}_{hsp-1}"] if hsp >= 1 else (
                    [f"mmdone{jc-1}_3"] if jc >= 1 else [])
                for it_ in range(NI):
                    p.dma("sync", "wdl", (lambda e, it_=it_, hp=hp:
                                          e.dma_start(
                        out=wdn0[:, it_*1024:(it_+1)*1024],
                        in_=WDN[hp*1024:(hp+1)*1024, it_*128:(it_+1)*128],
                        transpose=True)), mark=f"wdl{jc}_{hp}", waits=wlw,
                        rot=2, bucket=jc * 2 + hp)
            for o4 in range(4):
                tg = f"dn{jc}_{hsp}_{o4}"
                bank = pD[:, (it2 % 2)*512:(it2 % 2)*512+512]
                gidx = jc * 16 + hsp * 4 + o4
                all_dn = [f"dn{g//16}_{(g%16)//4}_{g%4}" for g in range(128)]
                p.marks[p.pre + f"{tg}_evfree"] = (
                    p.marks[p.pre + all_dn[gidx-2] + "_ev"]
                    if gidx >= 2 else ("kon", 0))
                p.marks[p.pre + f"{tg}_cfree"] = (
                    p.marks[p.pre + all_dn[gidx-1] + "_acc3"]
                    if gidx >= 1 else ("kon", 0))
                for tt in range(4):
                    p.marks[p.pre + f"{tg}_trfree{tt}"] = (
                        p.marks[p.pre + all_dn[gidx-1] + f"_acc{tt}"]
                        if gidx >= 1 else ("kon", 0))
                dw = [f"hl{jc}", f"wdl{jc}_{hsp//2}"]
                if gidx >= 2:
                    dw.append(f"{tg}_evfree")
                for k in range(NI):
                    p.op("ten", lambda e, k=k, o4=o4, hsp=hsp, bank=bank:
                         e.matmul(bank,
                                  lhsT=wdn0[:, k*1024 + (hsp % 2)*512 + o4*128:
                                            k*1024 + (hsp % 2)*512 + o4*128 + 128],
                                  rhs=hsl[:, k*512:(k+1)*512],
                                  start=(k == 0), stop=(k == NI - 1)),
                         chain=(k == 0), waits=dw if k == 0 else [],
                         inc=(k == NI - 1))
                p.mark_here("ten", f"{tg}_mm")
                p.op("sca", lambda e, bank=bank: e.copy(out=cmb[:], in_=bank),
                     chain=True, waits=[f"{tg}_mm", f"{tg}_cfree"],
                     mark=f"{tg}_ev")
                for tt in range(4):
                    slot = (tt % 2) * 512 + (tt // 2) * 128
                    bw2 = [f"{tg}_trfree{tt}"]
                    p.op("ten", lambda e, tt=tt, slot=slot: e.transpose(
                        out=pT[:, slot:slot+128], in_=cmb[:, tt*128:(tt+1)*128],
                        identity=idf[:]), chain=True,
                        waits=[f"{tg}_ev", "idf"] + bw2, mark=f"{tg}_tr{tt}")
                    oc = hsp * 512 + o4 * 128
                    p.op("vec", lambda e, tt=tt, oc=oc, slot=slot: e.tensor_copy(
                        accy[tt][:, oc:oc+128], pT[:, slot:slot+128]),
                        chain=True, waits=[f"{tg}_tr{tt}"], mark=f"{tg}_acc{tt}")
                it2 += 1
            p.mark_here("ten", f"mmdone{jc}_{hsp}")
        for tt in range(4):
            fw = [f"dn{jc}_3_{o4}_acc{tt}" for o4 in range(4)]
            p.op("sca", lambda e, tt=tt: e.activation(
                out=sqd[:], in_=accy[tt][:], func=AF.Square,
                accum_out=st_[:, 0:1]), chain=True,
                waits=fw + [f"accr{jc}_{tt}"], mark=f"rs{jc}_{tt}_sq")
            p.op("sca", lambda e: e.activation(
                out=st_[:, 1:2], in_=st_[:, 0:1], func=AF.Ln, scale=1.0 / D,
                bias=eps_ap), chain=True, waits=[f"rs{jc}_{tt}_sq", eps_mk])
            p.op("sca", lambda e: e.activation(
                out=st_[:, 2:3], in_=st_[:, 1:2], func=AF.Exp, scale=-0.5),
                chain=True, mark=f"rs{jc}_{tt}")
            p.op("vec", lambda e, tt=tt: e.scalar_tensor_tensor(
                out=accr[tt][:], in0=accy[tt][:], scalar=st_[:, 2:3],
                in1=accr[tt][:], op0=ALU.mult, op1=ALU.add), chain=True,
                waits=[f"rs{jc}_{tt}"], mark=f"fin{jc}_{tt}")
            p.dma("sca", "ost", (lambda e, tt=tt, q0=q0: e.dma_start(
                out=OUT[q0 + tt*128: q0 + tt*128 + 128, :], in_=accr[tt][:])),
                mark=f"outst{jc}", waits=[f"fin{jc}_{tt}"], rot=2, bucket=jc)
        p.mark_here("ten", f"ffdone{jc}")

    p.barrier_wait("sync", [f"outst{TC-1}", f"outst{TC-2}"])



def build_fused(debug=False):
    p = PB()
    nc = p.nc
    p.init_arena()
    _dt0 = nc.dram_tensor
    kindI = "ExternalOutput" if debug else "Internal"
    def dt_(nm, shape, dtyp, kind="Internal"):
        if kind == "Internal":
            kind = kindI
        return _dt0(nm, shape, dtyp, kind=kind)
    ext_in = {
        "x": dt_("x", [T, D], F32, kind="ExternalInput").ap(),
        "wq": dt_("wq", [D, D], F32, kind="ExternalInput").ap(),
        "wk": dt_("wk", [D, 512], F32, kind="ExternalInput").ap(),
        "wv": dt_("wv", [D, 512], F32, kind="ExternalInput").ap(),
        "wo": dt_("wo", [D, D], F32, kind="ExternalInput").ap(),
        "fw": dt_("fw", [D, D], F32, kind="ExternalInput").ap(),
        "fb": dt_("fb", [1, D], F32, kind="ExternalInput").ap(),
        "W_x": dt_("W_x", [96, D], F32, kind="ExternalInput").ap(),
        "W_dt": dt_("W_dt", [D, 64], F32, kind="ExternalInput").ap(),
        "b_dt": dt_("b_dt", [1, D], F32, kind="ExternalInput").ap(),
        "A_log": dt_("A_log", [D, NST], F32, kind="ExternalInput").ap(),
        "Dv": dt_("Dv", [1, D], F32, kind="ExternalInput").ap(),
        "gup": dt_("gup", [2 * DI, D // 8], I32, kind="ExternalInput").ap(),
        "gus": dt_("gus", [16, 2 * DI], F16, kind="ExternalInput").ap(),
        "dnp": dt_("dnp", [D, DI // 8], I32, kind="ExternalInput").ap(),
        "dns": dt_("dns", [44, D], F16, kind="ExternalInput").ap(),
    }
    OUT = dt_("out", [T, D], F32, kind="ExternalOutput").ap()
    mid = {
        "xn0": dt_("i_xn0", [T, D], BF16, kind="Internal").ap(),
        "QT": dt_("i_QT", [D, T], BF16, kind="Internal").ap(),
        "KT": dt_("i_KT", [512, T], BF16, kind="Internal").ap(),
        "VT": dt_("i_VT", [T, 512], BF16, kind="Internal").ap(),
        "YT": dt_("i_YT", [D, T], BF16, kind="Internal").ap(),
        "x1": dt_("i_x1", [T, D], F32, kind="Internal").ap(),
        "xn1h": dt_("i_xn1h", [T, D], BF16, kind="Internal").ap(),
        "xn1l": dt_("i_xn1l", [T, D], BF16, kind="Internal").ap(),
        "dtv": dt_("i_dtv", [D, T], F32, kind="Internal").ap(),
        "Bf": dt_("i_Bf", [16, T], F32, kind="Internal").ap(),
        "Cf": dt_("i_Cf", [16, T], F32, kind="Internal").ap(),
        "wgu_d": dt_("i_wgu", [2 * DI, D], BF16, kind="Internal").ap(),
        "wdn_d": dt_("i_wdn", [D, DI], BF16, kind="Internal").ap(),
        "ysh": dt_("i_ysh", [D, T], BF16, kind="Internal").ap(),
        "ysl": dt_("i_ysl", [D, T], BF16, kind="Internal").ap(),
        "hact": dt_("i_hact", [DI, T], BF16, kind="Internal").ap(),
    }

    build_fused.phase_bounds = bounds = []

    def phase(pre, emitfn, io, *args):
        p.pre = pre
        p.open_frame()
        emitfn(p, io, *args)
        p.global_barrier()
        bounds.append((pre, {e: len(p.ops[e]) for e in ENG_ORDER}))

    phase("A_", emit_ln, {"x": ext_in["x"], "xn_hi": mid["xn0"],
                          "xn_lo": None}, False)
    phase("B_", emit_qkv, {"xn_hi": mid["xn0"], "wq": ext_in["wq"],
                           "wk": ext_in["wk"], "wv": ext_in["wv"],
                           "QT": mid["QT"], "KT": mid["KT"], "VT": mid["VT"]})
    p.pre = "C_"
    p.open_frame()
    _ga = gen_attn(p, {"QT": mid["QT"], "KT": mid["KT"],
                       "VT": mid["VT"], "YT": mid["YT"]}, NH, TC, NKV)
    _gd = gen_dequant(p, {"gup": ext_in["gup"], "gus": ext_in["gus"],
                          "dnp": ext_in["dnp"], "dns": ext_in["dns"],
                          "wgu_d": mid["wgu_d"], "wdn_d": mid["wdn_d"]})
    _acts = [_ga, _gd]
    while _acts:
        for _g in list(_acts):
            try:
                next(_g)
            except StopIteration:
                _acts.remove(_g)
    p.global_barrier()
    bounds.append(("C_", {e: len(p.ops[e]) for e in ENG_ORDER}))
    phase("D_", emit_wo_ffn0, {"x": ext_in["x"], "xn_hi": mid["xn0"],
                               "YT": mid["YT"], "wo": ext_in["wo"],
                               "fw": ext_in["fw"], "fb": ext_in["fb"],
                               "x1": mid["x1"]})
    phase("E_", emit_ln, {"x": mid["x1"], "xn_hi": mid["xn1h"],
                          "xn_lo": mid["xn1l"]}, True)
    phase("F_", emit_xproj, {"xn_hi": mid["xn1h"], "W_x": ext_in["W_x"],
                             "W_dt": ext_in["W_dt"], "b_dt": ext_in["b_dt"],
                             "dtv": mid["dtv"], "Bf": mid["Bf"],
                             "Cf": mid["Cf"]})
    phase("H_", emit_scan, {"dtv": mid["dtv"], "xn_hi": mid["xn1h"],
                            "xn_lo": mid["xn1l"], "A_log": ext_in["A_log"],
                            "Dv": ext_in["Dv"], "Bf": mid["Bf"],
                            "Cf": mid["Cf"],
                            "yssm_hi": mid["ysh"], "yssm_lo": mid["ysl"]})
    phase("I_", emit_gateup, {"xn_hi": mid["xn1h"], "wgu_d": mid["wgu_d"],
                              "hact": mid["hact"]})
    phase("J_", emit_down, {"x1": mid["x1"], "hact": mid["hact"],
                            "wdn_d": mid["wdn_d"], "yssm_hi": mid["ysh"],
                            "yssm_lo": mid["ysl"], "out": OUT})
    return p.finalize()


def kernel_fused(L0_attn_x, L0_attn_wq, L0_attn_wk, L0_attn_wv, L0_attn_wo,
                 L0_ffn_weight, L0_ffn_bias, L1_attn_W_x, L1_attn_W_dt,
                 L1_attn_b_dt, L1_attn_A_log, L1_attn_D, L1_ffn_gate_up_packed,
                 L1_ffn_gate_up_scales, L1_ffn_down_packed, L1_ffn_down_scales):
    n = NCORE
    shared = dict(
        wq=np.ascontiguousarray(L0_attn_wq),
        wk=np.ascontiguousarray(L0_attn_wk),
        wv=np.ascontiguousarray(L0_attn_wv),
        wo=np.ascontiguousarray(L0_attn_wo),
        fw=np.ascontiguousarray(L0_ffn_weight),
        fb=np.ascontiguousarray(L0_ffn_bias).reshape(1, D),
        W_x=np.ascontiguousarray(L1_attn_W_x),
        W_dt=np.ascontiguousarray(L1_attn_W_dt),
        b_dt=np.ascontiguousarray(L1_attn_b_dt).reshape(1, D),
        A_log=np.ascontiguousarray(L1_attn_A_log),
        Dv=np.ascontiguousarray(L1_attn_D).reshape(1, D),
        gup=np.ascontiguousarray(L1_ffn_gate_up_packed),
        gus=np.ascontiguousarray(L1_ffn_gate_up_scales),
        dnp=np.ascontiguousarray(L1_ffn_down_packed),
        dns=np.ascontiguousarray(L1_ffn_down_scales),
    )
    in_maps = [dict(shared, x=np.ascontiguousarray(L0_attn_x[b]))
               for b in range(n)]
    r = run_launch(_get("fused", build_fused), in_maps)
    return np.stack([np.asarray(r[b]["out"], dtype=np.float32)
                     for b in range(n)], axis=0)


# version 16 (final)
# speedup vs baseline: 1.0567x; 1.0567x over previous
import math
import numpy as np
import ml_dtypes
import concourse.bass as bass
import concourse.mybir as mybir
from concourse.bass_utils import run_bass_kernel_spmd

F32 = mybir.dt.float32
BF16 = mybir.dt.bfloat16
F16 = mybir.dt.float16
I32 = mybir.dt.int32
I16 = mybir.dt.int16
AF = mybir.ActivationFunctionType
ALU = mybir.AluOpType
AX = mybir.AxisListType.X

NCORE = 8
T = 4096
D = 2048
NH, NKV, HD = 16, 4, 128
GQ = NH // NKV
DI = 5632
DTR, NST = 64, 16
ROPE_BASE = 100000.0
LN_EPS = 1e-5
QK_EPS = 1e-6
RMS_EPS = 1e-6
ATT_SCALE = float(np.float32(1.0) / np.sqrt(np.float32(HD)))
ATT_SHIFT = -11.3137085

TT = T // 128
DT_ = D // 128
TC = T // 512

ENG_ORDER = ["sync", "sca", "vec", "gps", "ten"]


class PB:

    def __init__(self):
        self.nc = bass.Bass("TRN2", target_bir_lowering=False)
        self.sem = {}
        self.cnt = {}
        self.marks = {}
        self.ops = {e: [] for e in ENG_ORDER}
        self.dma_rot = {}
        self._consts = {}
        self._ctile = None
        self.pre = ""
        self.sb_arena = None

    ESZ = {F32: 4, BF16: 2, F16: 2, I32: 4, I16: 2}

    def init_arena(self, sbytes=210944):
        self.sbytes = sbytes
        self.sb_arena = self.nc.alloc_sbuf_tensor("ARENA", [128, sbytes],
                                                   mybir.dt.uint8)
        self.ps_arena = self.nc.alloc_psum_tensor("PARENA", [128, 4096], F32)
        self.open_frame()

    def open_frame(self):
        self.sb_off = 0
        self.ps_bank = 0

    def salloc(self, name, shape, dtype):
        if self.sb_arena is None:
            self.init_arena()
        esz = self.ESZ[dtype]
        nb = shape[1] * esz
        off = (self.sb_off + 63) & ~63
        self.sb_off = off + nb
        if not hasattr(self, "salloc_map"):
            self.salloc_map = {}
        self.salloc_map[self.pre + name] = (off, shape, dtype)
        assert self.sb_off <= self.sbytes, (
            f"sbuf arena overflow at {name}: {self.sb_off}")
        v = self.sb_arena[:, off:off + nb].bitcast(dtype)
        if shape[0] != 128:
            v = v[0:shape[0], :]
        return v

    def palloc(self, name, shape, dtype=F32):
        esz = self.ESZ[dtype]
        nf32 = (shape[1] * esz + 3) // 4
        banks = (nf32 + 511) // 512
        start = self.ps_bank
        self.ps_bank += banks
        assert self.ps_bank <= 8, f"psum arena overflow at {name}"
        v = self.ps_arena[:, start * 512:(start + banks) * 512]
        if dtype != F32:
            v = v.bitcast(dtype)
        if shape[0] != 128:
            v = v[0:shape[0], :]
        return v

    def global_barrier(self):
        snap = [(q, self.cnt[q]) for q in list(self.cnt) if self.cnt[q] > 0]
        for ename in ENG_ORDER:
            self.ops[ename].append((list(snap), None, None, 0))

    def constcol(self, value):
        if value not in self._consts:
            if self._ctile is None:
                self._ctile = self.nc.alloc_sbuf_tensor("konst", [128, 16], F32)
            k = len(self._consts)
            ap = self._ctile[:, k:k+1]
            self.s("kon")
            self.cnt["kon"] += 1
            self.ops["gps"].append(
                ([], (lambda e, ap=ap, value=value: e.memset(ap, value)),
                 "kon", 1))
            self._consts[value] = (ap, ("kon", self.cnt["kon"]))
        return self._consts[value]

    def alias(self, dst, srcmark):
        self.marks[self.pre + dst] = (self.marks[self.pre + srcmark]
                                      if isinstance(srcmark, str) else srcmark)

    def getmark(self, mk):
        return self.marks[self.pre + mk]

    def s(self, q):
        if q not in self.sem:
            self.sem[q] = self.nc.alloc_semaphore(name="S_" + q)
            self.cnt[q] = 0
        return self.sem[q]

    def _resolve(self, waits):
        out = []
        for wt in waits:
            if isinstance(wt, tuple):
                out.append(wt)
            else:
                q, v = self.marks[self.pre + wt]
                if v > 0:
                    out.append((q, v))
        return out

    def op(self, eng, fn, q=None, mark=None, waits=(), chain=False, n=1,
           inc=True):
        q = q or eng
        self.s(q)
        wl = list(waits)
        if chain and self.cnt[q] > 0:
            wl.append((q, self.cnt[q]))
        resolved = self._resolve(wl)
        if not inc:
            n = 0
        self.cnt[q] += n
        if mark is not None:
            self.marks[self.pre + mark] = (q, self.cnt[q])
        self.ops[eng].append((resolved, fn, q, n))

    def dma(self, eng, stream, fn, mark=None, waits=(), rot=2, bucket=None):
        stream = self.pre + stream
        key = (stream, rot)
        idx = self.dma_rot.get(key, 0)
        self.dma_rot[key] = idx + 1
        b = idx if bucket is None else bucket
        q = f"{stream}{b % rot}"
        self.op(eng, fn, q=q, mark=mark, waits=waits, n=16)

    def barrier_wait(self, eng, waits):
        resolved = self._resolve(list(waits))
        self.ops[eng].append((resolved, None, None, 0))

    def mark_here(self, q, mark):
        self.s(q)
        self.marks[self.pre + mark] = (q, self.cnt[q])

    def finalize(self):
        nc = self.nc
        with nc.Block() as blk:
            def make_section(ename):
                def section(e):
                    for resolved, fn, q, n in self.ops[ename]:
                        for (qq, v) in resolved:
                            e.wait_ge(self.sem[qq], v)
                        if fn is not None:
                            inst = fn(e)
                            if n:
                                inst.then_inc(self.sem[q], n)
                return section
            for ename in ENG_ORDER:
                if self.ops[ename]:
                    getattr(blk, {"sync": "sync", "sca": "scalar", "vec": "vector",
                                  "gps": "gpsimd", "ten": "tensor"}[ename])(
                        make_section(ename))
        return nc


def hilo_split(p, hi, lo, src, tag, waits=()):
    p.op("vec", lambda v: v.tensor_copy(hi, src), chain=True, waits=waits,
         mark=tag + "_hi")
    p.op("vec", lambda v: v.tensor_tensor(out=lo, in0=src, in1=hi, op=ALU.subtract),
         chain=True, mark=tag)



def build_ln(store_pair):
    p = PB()
    nc = p.nc
    io = {"x": nc.dram_tensor("x", [T, D], F32, kind="ExternalInput").ap(),
          "xn_hi": nc.dram_tensor("xn_hi", [T, D], BF16, kind="ExternalOutput").ap(),
          "xn_lo": (nc.dram_tensor("xn_lo", [T, D], BF16,
                                   kind="ExternalOutput").ap()
                    if store_pair else None)}
    emit_ln(p, io, store_pair)
    return p.finalize()


def emit_ln(p, io, store_pair):
    nc = p.nc
    X = io["x"]; XNH = io["xn_hi"]; XNL = io.get("xn_lo")
    xt = [p.salloc(f"xt{j}", [128, D], F32) for j in range(2)]
    sq = p.salloc("sq", [128, D], F32)
    xn = p.salloc("xn", [128, D], F32)
    xh = [p.salloc(f"xh{j}", [128, D], BF16) for j in range(2)]
    xl = [p.salloc(f"xl{j}", [128, D], BF16) for j in range(2)]
    st = p.salloc("st", [128, 8], F32)

    for i in range(TT):
        b = xt[i % 2]
        lw = [f"ln_apply{i-2}", f"sq{i-2}"] if i >= 2 else []
        p.dma("sync", "ld", (lambda e, b=b, i=i: e.dma_start(
            out=b[:], in_=X[i*128:(i+1)*128, :])), mark=f"x{i}", waits=lw)
        p.op("sca", lambda e, b=b: e.activation(out=sq[:], in_=b[:], func=AF.Square,
                                                accum_out=st[:, 2:3]),
             waits=[f"x{i}", f"var{i-1}"] if i >= 1 else [f"x{i}"],
             chain=True, mark=f"sq{i}")
        p.op("vec", lambda e, b=b: e.reduce_sum(st[:, 0:1], b[:], axis=AX),
             waits=[f"x{i}"] + ([f"nmr{i-1}"] if i >= 1 else []), chain=True)
        p.op("vec", lambda e: e.tensor_scalar_mul(st[:, 1:2], st[:, 0:1], 1.0 / D),
             chain=True)
        p.op("vec", lambda e: e.tensor_scalar_mul(st[:, 2:3], st[:, 2:3], 1.0 / D),
             waits=[f"sq{i}"], chain=True)
        p.op("vec", lambda e: e.scalar_tensor_tensor(
            out=st[:, 3:4], in0=st[:, 1:2], scalar=-1.0, in1=st[:, 1:2],
            op0=ALU.mult, op1=ALU.mult), chain=True)
        p.op("vec", lambda e: e.tensor_tensor(out=st[:, 3:4], in0=st[:, 2:3],
                                              in1=st[:, 3:4], op=ALU.add),
             chain=True, mark=f"var{i}")
        eps_ap, eps_mk = p.constcol(LN_EPS)
        p.op("sca", lambda e: e.activation(out=st[:, 4:5], in_=st[:, 3:4],
                                           func=AF.Ln, bias=eps_ap),
             waits=[f"var{i}", eps_mk], chain=True)
        p.op("sca", lambda e: e.activation(out=st[:, 5:6], in_=st[:, 4:5],
                                           func=AF.Exp, scale=-0.5),
             chain=True, mark=f"rstd{i}")
        p.op("vec", lambda e: e.scalar_tensor_tensor(
            out=st[:, 6:7], in0=st[:, 1:2], scalar=-1.0, in1=st[:, 5:6],
            op0=ALU.mult, op1=ALU.mult), waits=[f"rstd{i}"], chain=True,
            mark=f"nmr{i}")
        p.op("vec", lambda e, b=b: e.tensor_scalar(
            out=xn[:], in0=b[:], scalar1=st[:, 5:6], scalar2=st[:, 6:7],
            op0=ALU.mult, op1=ALU.add), chain=True, mark=f"ln_apply{i}")
        hw = [f"st_hi{i-2}"] if i >= 2 else []
        p.op("vec", lambda e, i=i: e.tensor_copy(xh[i % 2][:], xn[:]),
             chain=True, waits=hw, mark=f"hi{i}")
        if store_pair:
            lww = [f"st_lo{i-2}"] if i >= 2 else []
            p.op("vec", lambda e, i=i: e.tensor_tensor(
                out=xl[i % 2][:], in0=xn[:], in1=xh[i % 2][:], op=ALU.subtract),
                 chain=True, waits=lww, mark=f"lo{i}")
        p.dma("sca", "st", (lambda e, i=i: e.dma_start(
            out=XNH[i*128:(i+1)*128, :], in_=xh[i % 2][:])),
            mark=f"st_hi{i}", waits=[f"hi{i}"])
        if store_pair:
            p.dma("sca", "st2", (lambda e, i=i: e.dma_start(
                out=XNL[i*128:(i+1)*128, :], in_=xl[i % 2][:])),
                mark=f"st_lo{i}", waits=[f"lo{i}"])
    final = [f"st_hi{TT-1}", f"st_hi{TT-2}"]
    if store_pair:
        final += [f"st_lo{TT-1}", f"st_lo{TT-2}"]
    p.barrier_wait("sync", final)



_LAUNCH_CACHE = {}


def _get(name, builder):
    if name not in _LAUNCH_CACHE:
        _LAUNCH_CACHE[name] = builder()
    return _LAUNCH_CACHE[name]


def run_launch(nc, in_maps):
    res = run_bass_kernel_spmd(nc, in_maps, core_ids=list(range(len(in_maps))))
    return res.results


def kernel(**inputs):
    return kernel_fused(**inputs)


def kernel_unfused(L0_attn_x, L0_attn_wq, L0_attn_wk, L0_attn_wv, L0_attn_wo,
           L0_ffn_weight, L0_ffn_bias, L1_attn_W_x, L1_attn_W_dt, L1_attn_b_dt,
           L1_attn_A_log, L1_attn_D, L1_ffn_gate_up_packed,
           L1_ffn_gate_up_scales, L1_ffn_down_packed, L1_ffn_down_scales):
    n = NCORE
    xs = [np.ascontiguousarray(L0_attn_x[b]) for b in range(n)]
    wq = np.ascontiguousarray(L0_attn_wq)
    wk = np.ascontiguousarray(L0_attn_wk)
    wv = np.ascontiguousarray(L0_attn_wv)
    wo = np.ascontiguousarray(L0_attn_wo)
    fw = np.ascontiguousarray(L0_ffn_weight)
    fb = np.ascontiguousarray(L0_ffn_bias).reshape(1, D)
    wx = np.ascontiguousarray(L1_attn_W_x)
    wdt = np.ascontiguousarray(L1_attn_W_dt)
    bdt = np.ascontiguousarray(L1_attn_b_dt).reshape(1, D)
    alog = np.ascontiguousarray(L1_attn_A_log)
    dv = np.ascontiguousarray(L1_attn_D).reshape(1, D)
    gup = np.ascontiguousarray(L1_ffn_gate_up_packed)
    gus = np.ascontiguousarray(L1_ffn_gate_up_scales)
    dnp = np.ascontiguousarray(L1_ffn_down_packed)
    dns = np.ascontiguousarray(L1_ffn_down_scales)

    r = run_launch(_get("ln0", lambda: build_ln(False)),
                   [dict(x=xs[b]) for b in range(n)])
    xn0 = [r[b]["xn_hi"] for b in range(n)]
    r = run_launch(_get("qkv", build_qkv),
                   [dict(xn_hi=xn0[b], wq=wq, wk=wk, wv=wv) for b in range(n)])
    QT = [r[b]["QT"] for b in range(n)]
    KT = [r[b]["KT"] for b in range(n)]
    VT = [r[b]["VT"] for b in range(n)]
    r = run_launch(_get("attn", build_attn),
                   [dict(QT=QT[b], KT=KT[b], VT=VT[b]) for b in range(n)])
    YT = [r[b]["YT"] for b in range(n)]
    r = run_launch(_get("d", build_wo_ffn0),
                   [dict(x=xs[b], xn_hi=xn0[b], YT=YT[b], wo=wo, fw=fw, fb=fb)
                    for b in range(n)])
    x1 = [r[b]["x1"] for b in range(n)]
    r = run_launch(_get("ln1", lambda: build_ln(True)),
                   [dict(x=x1[b]) for b in range(n)])
    xn1h = [r[b]["xn_hi"] for b in range(n)]
    xn1l = [r[b]["xn_lo"] for b in range(n)]
    r = run_launch(_get("xproj", build_xproj),
                   [dict(xn_hi=xn1h[b], W_x=wx, W_dt=wdt, b_dt=bdt)
                    for b in range(n)])
    dtv = [r[b]["dtv"] for b in range(n)]
    bc = {nm: [r[b][nm] for b in range(n)] for nm in ("Bf", "Cf")}
    r = run_launch(_get("deq", build_dequant),
                   [dict(gup=gup, gus=gus, dnp=dnp, dns=dns) for b in range(n)])
    wgu_d = [r[b]["wgu_d"] for b in range(n)]
    wdn_d = [r[b]["wdn_d"] for b in range(n)]
    r = run_launch(_get("scan", build_scan),
                   [dict(dtv=dtv[b], xn_hi=xn1h[b], xn_lo=xn1l[b], A_log=alog,
                         Dv=dv, Bf=bc["Bf"][b], Cf=bc["Cf"][b])
                    for b in range(n)])
    ysh = [r[b]["yssm_hi"] for b in range(n)]
    ysl_ = [r[b]["yssm_lo"] for b in range(n)]
    r = run_launch(_get("gu", build_gateup),
                   [dict(xn_hi=xn1h[b], wgu_d=wgu_d[b]) for b in range(n)])
    hact = [r[b]["hact"] for b in range(n)]
    r = run_launch(_get("down", build_down),
                   [dict(x1=x1[b], hact=hact[b], wdn_d=wdn_d[b],
                         yssm_hi=ysh[b], yssm_lo=ysl_[b]) for b in range(n)])
    out = np.stack([np.asarray(r[b]["out"], dtype=np.float32)
                    for b in range(n)], axis=0)
    return out



def build_qkv():
    p = PB()
    nc = p.nc
    io = {"xn_hi": nc.dram_tensor("xn_hi", [T, D], BF16, kind="ExternalInput").ap(),
          "wq": nc.dram_tensor("wq", [D, D], F32, kind="ExternalInput").ap(),
          "wk": nc.dram_tensor("wk", [D, 512], F32, kind="ExternalInput").ap(),
          "wv": nc.dram_tensor("wv", [D, 512], F32, kind="ExternalInput").ap(),
          "QT": nc.dram_tensor("QT", [D, T], BF16, kind="ExternalOutput").ap(),
          "KT": nc.dram_tensor("KT", [512, T], BF16, kind="ExternalOutput").ap(),
          "VT": nc.dram_tensor("VT", [T, 512], BF16, kind="ExternalOutput").ap()}
    emit_qkv(p, io)
    return p.finalize()


def emit_qkv(p, io):
    nc = p.nc
    XNH = io["xn_hi"]; WQ = io["wq"]; WK = io["wk"]; WV = io["wv"]
    QT = io["QT"]; KT = io["KT"]; VT = io["VT"]

    wq_b = p.salloc("wq_b", [128, DT_ * D], BF16)
    wk_b = p.salloc("wk_b", [128, DT_ * 512], BF16)
    wv_b = p.salloc("wv_b", [128, DT_ * 512], BF16)
    xnsl = [p.salloc(f"xnsl{j}", [128, DT_ * 512], BF16) for j in range(2)]
    cosb = p.salloc("cosb", [128, T], F32)
    sinb = p.salloc("sinb", [128, T], F32)
    ang = p.salloc("ang", [128, 1024], F32)
    jcol = p.salloc("jcol", [128, 2], F32)
    pmat = p.salloc("pmat", [128, 128], F32)
    pmA = p.salloc("pmA", [128, 128], F32)
    qf = p.salloc("qf", [128, 512], F32)
    rot = p.salloc("rot", [128, 512], F32)
    tmp128 = p.salloc("tmp128", [128, 512], F32)
    sqb = p.salloc("sqb", [128, 512], F32)
    sqh = p.salloc("sqh", [128, 512], BF16)
    sql = p.salloc("sql", [128, 512], BF16)
    msrow = p.salloc("msrow", [1, 512], F32)
    rsrow = p.salloc("rsrow", [1, 512], F32)
    rsh = p.salloc("rsh", [1, 512], BF16)
    rsl = p.salloc("rsl", [1, 512], BF16)
    qout = [p.salloc(f"qout{j}", [128, 512], BF16) for j in range(2)]
    vout = [p.salloc(f"vout{j}", [128, 512], BF16) for j in range(2)]
    ones_c = p.salloc("ones_c", [128, 1], BF16)
    ones_r = p.salloc("ones_r", [1, 128], BF16)
    pq = p.palloc("pq", [128, 1024], F32)
    pms = p.palloc("pms", [128, 512], F32)
    pbc = p.palloc("pbc", [128, 512], F32)
    pv = p.palloc("pv", [128, 1024], F32)
    pperm = p.palloc("pperm", [128, 512], F32)

    p.op("vec", lambda e: e.memset(ones_c[:], 1.0), mark="ones_c")
    p.op("vec", lambda e: e.memset(ones_r[:], 1.0), mark="ones_r")
    p.op("gps", lambda e: e.iota(jcol[:, 0:1], pattern=[[0, 1]], base=0,
                                 channel_multiplier=1,
                                 allow_small_or_imprecise_dtypes=True),
         mark="jcol", chain=True)
    p.op("vec", lambda e: e.tensor_scalar(out=jcol[:, 1:2], in0=jcol[:, 0:1],
                                          scalar1=63.5, scalar2=-64.0,
                                          op0=ALU.is_gt, op1=ALU.mult),
         waits=["jcol"], chain=True)
    p.op("vec", lambda e: e.tensor_tensor(out=jcol[:, 0:1], in0=jcol[:, 0:1],
                                          in1=jcol[:, 1:2], op=ALU.add),
         chain=True, mark="jmod")
    p.op("sca", lambda e: e.activation(out=jcol[:, 1:2], in_=jcol[:, 0:1],
                                       func=AF.Exp,
                                       scale=-math.log(ROPE_BASE) / 64.0),
         waits=["jmod"], mark="invf")
    p.op("gps", lambda e: e.memset(pmA[:], 1.0), mark="pm1", chain=True)
    p.op("gps", lambda e: e.affine_select(
        out=pmat[:], in_=pmA[:], pattern=[[1, 128]], compare_op=ALU.is_equal,
        fill=0.0, base=64, channel_multiplier=-1), waits=["pm1"], chain=True,
        mark="pmB")
    p.op("gps", lambda e: e.affine_select(
        out=pmA[:], in_=pmA[:], pattern=[[1, 128]], compare_op=ALU.is_equal,
        fill=0.0, base=-64, channel_multiplier=-1), waits=["pmB"], chain=True,
        mark="pmA2")
    p.op("vec", lambda e: e.tensor_tensor(out=pmat[:], in0=pmat[:], in1=pmA[:],
                                          op=ALU.subtract), waits=["pmA2"],
         chain=True, mark="pmat")

    ang2 = p.salloc("ang2", [128, 1024], F32)

    def sin_of(dst, src, phase, tag):
        TP = 2.0 * math.pi
        p.op("vec", lambda e, dst=dst, src=src, phase=phase: e.tensor_scalar(
            out=dst, in0=src, scalar1=phase, scalar2=None, op0=ALU.add),
            chain=True, waits=["ang"])
        p.op("vec", lambda e, dst=dst: e.tensor_scalar_mul(ang2[:], dst, 1.0 / TP),
             chain=True)
        p.op("vec", lambda e: e.tensor_copy(ang2[:].bitcast(I32), ang2[:]),
             chain=True)
        p.op("vec", lambda e: e.tensor_copy(ang2[:], ang2[:].bitcast(I32)),
             chain=True)
        p.op("vec", lambda e, dst=dst: e.scalar_tensor_tensor(
            out=dst, in0=ang2[:], scalar=-TP, in1=dst, op0=ALU.mult, op1=ALU.add),
            chain=True)
        for sgn in (1.0, -1.0):
            p.op("vec", lambda e, dst=dst, sgn=sgn: e.tensor_scalar(
                out=ang2[:], in0=dst, scalar1=sgn, scalar2=math.pi,
                op0=ALU.mult, op1=ALU.is_gt), chain=True)
            p.op("vec", lambda e, dst=dst, sgn=sgn: e.scalar_tensor_tensor(
                out=dst, in0=ang2[:], scalar=-sgn * TP, in1=dst,
                op0=ALU.mult, op1=ALU.add), chain=True, mark=tag + "_arg")
        p.op("sca", lambda e, dst=dst: e.activation(out=dst, in_=dst, func=AF.Sin),
             waits=[tag + "_arg"], chain=True, mark=tag)

    for tc4 in range(4):
        c0 = tc4 * 1024
        p.op("gps", lambda e, c0=c0: e.iota(
            ang[:], pattern=[[1, 1024]], base=c0, channel_multiplier=0,
            allow_small_or_imprecise_dtypes=True), chain=True,
            waits=[f"tblc{tc4-1}"] if tc4 >= 1 else [], mark=f"trow{tc4}")
        p.op("vec", lambda e: e.tensor_scalar_mul(ang[:], ang[:], jcol[:, 1:2]),
             waits=[f"trow{tc4}", "invf"], chain=True, mark="ang")
        sin_of(sinb[:, c0:c0+1024], ang[:], 0.0, f"sinb{tc4}")
        sin_of(cosb[:, c0:c0+1024], ang[:], math.pi / 2.0, f"cosb{tc4}")
        p.mark_here("sca", f"tblc{tc4}")
    p.marks[p.pre + "sinb"] = p.marks[p.pre + "tblc3"]
    p.marks[p.pre + "cosb"] = p.marks[p.pre + "tblc3"]

    for k in range(DT_):
        p.dma("gps", "wld", (lambda e, k=k: e.dma_start(
            out=wq_b[:, k*D:(k+1)*D], in_=WQ[k*128:(k+1)*128, :])), rot=1)
        p.dma("gps", "wld", (lambda e, k=k: e.dma_start(
            out=wk_b[:, k*512:(k+1)*512], in_=WK[k*128:(k+1)*128, :])), rot=1)
        p.dma("gps", "wld", (lambda e, k=k: e.dma_start(
            out=wv_b[:, k*512:(k+1)*512], in_=WV[k*128:(k+1)*128, :])),
            mark=("wld_all" if k == DT_ - 1 else f"wld{k}"), rot=1)

    qeps_ap, qeps_mk = p.constcol(QK_EPS)

    def rope_rms(jc, head, src_bank_ap, out_tile, tag, out_war=()):
        c0 = jc * 512
        p.op("sca", lambda e: e.copy(out=qf[:], in_=src_bank_ap), chain=True,
             waits=[f"{tag}_mmdone"], mark=f"{tag}_evac")
        p.op("ten", lambda e: e.matmul(pperm[:], lhsT=pmat[:], rhs=qf[:],
                                       start=True, stop=True), chain=True,
             waits=[f"{tag}_evac", "pmat", f"{tag}_permfree"],
             mark=f"{tag}_perm")
        p.op("vec", lambda e: e.tensor_tensor(
            out=rot[:], in0=qf[:], in1=cosb[:, c0:c0+512],
            op=ALU.mult), chain=True, waits=[f"{tag}_evac", "cosb", "sinb"])
        p.op("vec", lambda e: e.tensor_tensor(
            out=tmp128[:], in0=pperm[:], in1=sinb[:, c0:c0+512],
            op=ALU.mult), chain=True, waits=[f"{tag}_perm"],
            mark=f"{tag}_permread")
        p.op("vec", lambda e: e.tensor_tensor(
            out=rot[:], in0=rot[:], in1=tmp128[:], op=ALU.add),
            chain=True, mark=f"{tag}_rot")
        p.op("sca", lambda e: e.activation(out=sqb[:], in_=rot[:], func=AF.Square),
             waits=[f"{tag}_rot"], chain=True, mark=f"{tag}_sq")
        p.op("vec", lambda e: e.tensor_copy(sqh[:], sqb[:]), chain=True,
             waits=[f"{tag}_sq"])
        p.op("vec", lambda e: e.tensor_tensor(out=sql[:], in0=sqb[:], in1=sqh[:],
                                              op=ALU.subtract), chain=True,
             mark=f"{tag}_sqhl")
        p.op("ten", lambda e: e.matmul(pms[0:1, :], lhsT=ones_c[:], rhs=sqh[:],
                                       start=True, stop=False),
             waits=[f"{tag}_sqhl", "ones_c", f"{tag}_msfree"], chain=True)
        p.op("ten", lambda e: e.matmul(pms[0:1, :], lhsT=ones_c[:], rhs=sql[:],
                                       start=False, stop=True), chain=True,
             mark=f"{tag}_ms")
        p.op("sca", lambda e: e.copy(out=msrow[:], in_=pms[0:1, :]), chain=True,
             waits=[f"{tag}_ms"], mark=f"{tag}_msev")
        p.op("sca", lambda e: e.activation(out=msrow[:], in_=msrow[:], func=AF.Ln,
                                           scale=1.0 / 128.0, bias=qeps_ap[0:1, :]),
             chain=True, waits=[qeps_mk])
        p.op("sca", lambda e: e.activation(out=rsrow[:], in_=msrow[:], func=AF.Exp,
                                           scale=-0.5), chain=True,
             mark=f"{tag}_rs")
        p.op("vec", lambda e: e.tensor_copy(rsh[:], rsrow[:]), chain=True,
             waits=[f"{tag}_rs"])
        p.op("vec", lambda e: e.tensor_tensor(out=rsl[:], in0=rsrow[:], in1=rsh[:],
                                              op=ALU.subtract), chain=True,
             mark=f"{tag}_rshl")
        p.op("ten", lambda e: e.matmul(pbc[:], lhsT=ones_r[:], rhs=rsh[:],
                                       start=True, stop=False),
             waits=[f"{tag}_rshl", "ones_r", f"{tag}_bcfree"], chain=True)
        p.op("ten", lambda e: e.matmul(pbc[:], lhsT=ones_r[:], rhs=rsl[:],
                                       start=False, stop=True), chain=True,
             mark=f"{tag}_bc")
        p.op("vec", lambda e, out_tile=out_tile: e.tensor_tensor(
            out=out_tile[:], in0=rot[:], in1=pbc[:], op=ALU.mult), chain=True,
            waits=[f"{tag}_bc"] + list(out_war), mark=f"{tag}_out")
        p.mark_here("vec", f"{tag}_bcfree")
        p.mark_here("sca", f"{tag}_msfree_s")

    p.marks[p.pre + "_none"] = ("kon", 0)

    qk_iter = 0
    for jc in range(TC):
        q0 = jc * 512
        sl = xnsl[jc % 2]
        for k in range(DT_):
            lw = [f"slab_free{jc-2}"] if jc >= 2 else []
            p.dma("sync", "xl", (lambda e, k=k, q0=q0, sl=sl: e.dma_start(
                out=sl[:, k*512:(k+1)*512],
                in_=XNH[q0:q0+512, k*128:(k+1)*128], transpose=True)),
                mark=f"slab{jc}_{k}", waits=lw, rot=2, bucket=jc)
        slab_ready = [f"slab{jc}_{DT_-1}"]

        for head in range(NH + NKV):
            isq = head < NH
            wb, hh = (wq_b, head) if isq else (wk_b, head - NH)
            wstride = D if isq else 512
            tag = f"j{jc}h{head}"
            bank = pq[:, (qk_iter % 2) * 512:(qk_iter % 2) * 512 + 512]
            ot = qout[qk_iter % 2]
            prev = qk_iter - 2
            bw = ([f"j{(prev)//(NH+NKV)}h{prev%(NH+NKV)}_evac"] if prev >= 0 else [])
            ow = ([f"st_{prev}"] if prev >= 0 else [])
            for k in range(DT_):
                w = (["wld_all"] + slab_ready + bw) if k == 0 else []
                p.op("ten", lambda e, k=k, wb=wb, hh=hh, sl=sl, bank=bank,
                     wstride=wstride: e.matmul(
                         bank, lhsT=wb[:, k*wstride + hh*128: k*wstride + hh*128 + 128],
                         rhs=sl[:, k*512:(k+1)*512],
                         start=(k == 0), stop=(k == DT_ - 1)),
                     chain=(k == 0), waits=w, inc=(k == DT_ - 1),
                     mark=(f"{tag}_mmdone" if k == DT_ - 1 else None))
            prevtag = (f"j{(qk_iter-1)//(NH+NKV)}h{(qk_iter-1)%(NH+NKV)}"
                       if qk_iter >= 1 else None)
            p.marks[p.pre + f"{tag}_msfree"] = (p.marks[p.pre + f"{prevtag}_msev"]
                                        if prevtag else ("kon", 0))
            p.marks[p.pre + f"{tag}_bcfree"] = (p.marks[p.pre + f"{prevtag}_out"]
                                        if prevtag else ("kon", 0))
            p.marks[p.pre + f"{tag}_permfree"] = (p.marks[p.pre + f"{prevtag}_permread"]
                                          if prevtag else ("kon", 0))
            rope_rms(jc, head, bank, ot, tag, out_war=ow)
            dst = QT if isq else KT
            p.dma("sca", "qst", (lambda e, dst=dst, hh=hh, q0=q0, ot=ot: e.dma_start(
                out=dst[hh*128:(hh+1)*128, q0:q0+512], in_=ot[:])),
                mark=f"st_{qk_iter}", waits=[f"{tag}_out"], rot=2)
            qk_iter += 1

        for tt in range(4):
            vtag = f"vj{jc}t{tt}"
            vbank = pv[:, (tt % 2) * 512:(tt % 2) * 512 + 512]
            vo = vout[tt % 2]
            pv_prev = f"vj{jc}t{tt-2}_evac" if tt >= 2 else (
                f"vj{jc-1}t{tt+2}_evac" if jc >= 1 else None)
            vo_prev = f"vst_{jc}_{tt-2}" if tt >= 2 else (
                f"vst_{jc-1}_{tt+2}" if jc >= 1 else None)
            for k in range(DT_):
                w = []
                if k == 0:
                    w = ["wld_all"] + slab_ready
                    if pv_prev:
                        w.append(pv_prev)
                p.op("ten", lambda e, k=k, sl=sl, tt=tt, vbank=vbank: e.matmul(
                    vbank, lhsT=sl[:, k*512 + tt*128: k*512 + tt*128 + 128],
                    rhs=wv_b[:, k*512:(k+1)*512],
                    start=(k == 0), stop=(k == DT_ - 1)),
                    chain=(k == 0), waits=w, inc=(k == DT_ - 1),
                    mark=(f"{vtag}_mmdone" if k == DT_ - 1 else None))
            ww = [f"{vtag}_mmdone"] + ([vo_prev] if vo_prev else [])
            p.op("sca", lambda e, vo=vo, vbank=vbank: e.copy(out=vo[:], in_=vbank),
                 chain=True, waits=ww, mark=f"{vtag}_evac")
            p.dma("sca", "vst", (lambda e, vo=vo, q0=q0, tt=tt: e.dma_start(
                out=VT[q0 + tt*128: q0 + tt*128 + 128, :], in_=vo[:])),
                mark=f"vst_{jc}_{tt}", waits=[f"{vtag}_evac"], rot=2)
        p.mark_here("ten", f"slab_free{jc}")

    fin = [f"st_{qk_iter-1}", f"st_{qk_iter-2}"]
    fin += [f"vst_{TC-1}_{tt}" for tt in (2, 3)]
    p.barrier_wait("sync", fin)



def build_attn(nh_=NH, tc_=TC, nkv_=NKV):
    p = PB()
    nc = p.nc
    io = {"QT": nc.dram_tensor("QT", [D, T], BF16, kind="ExternalInput").ap(),
          "KT": nc.dram_tensor("KT", [512, T], BF16, kind="ExternalInput").ap(),
          "VT": nc.dram_tensor("VT", [T, 512], BF16, kind="ExternalInput").ap(),
          "YT": nc.dram_tensor("YT", [D, T], BF16, kind="ExternalOutput").ap()}
    emit_attn(p, io, nh_, tc_, nkv_)
    return p.finalize()


def emit_attn(p, io, nh_, tc_, nkv_):
    for _ in gen_attn(p, io, nh_, tc_, nkv_):
        pass


def gen_attn(p, io, nh_, tc_, nkv_):
    nc = p.nc
    QT = io["QT"]; KT = io["KT"]; VT = io["VT"]; YT = io["YT"]

    kbuf = p.salloc("kbuf", [128, T], BF16)
    vbuf = p.salloc("vbuf", [128, T], BF16)
    qch = [p.salloc(f"qch{j}", [128, 512], BF16) for j in range(2)]
    pt = p.salloc("ptslab", [128, 32 * 512], BF16)
    drow = p.salloc("drow", [1, 512], F32)
    rrow = p.salloc("rrow", [1, 512], F32)
    rh = p.salloc("rh", [1, 512], BF16)
    rl = p.salloc("rl", [1, 512], BF16)
    bcf = p.salloc("bcf", [128, 1024], F32)
    yout = [p.salloc(f"yout{j}", [128, 512], BF16) for j in range(2)]
    ones_c = p.salloc("ones_c", [128, 128], BF16)
    ones_r = p.salloc("ones_r", [1, 128], BF16)
    psA = p.palloc("psA", [128, 2048], F32)
    pden = p.palloc("pden", [128, 512], F32)
    pY = p.palloc("pY", [128, 1024], F32)
    pbc = pden

    p.op("vec", lambda e: e.memset(ones_c[:], 1.0), mark="ones_c")
    p.op("vec", lambda e: e.memset(ones_r[:], 1.0), mark="ones_r")
    sh_ap, sh_mk = p.constcol(ATT_SHIFT)
    for mk in ("kon0", "prevexp0"):
        p.marks[p.pre + mk] = ("kon", 0)

    it = 0
    all_tags = [f"a{h}_{jc}" for h in range(nh_) for jc in range(tc_)]
    for g in range(nkv_):
        kw = [f"attdone_g{g-1}"] if g >= 1 else []
        p.dma("sync", "kld", (lambda e, g=g: e.dma_start(
            out=kbuf[:], in_=KT[g*128:(g+1)*128, :])), mark="kld", waits=kw,
            rot=2, bucket=g)
        for part in range(8):
            p.dma("sync", "vld", (lambda e, g=g, part=part: e.dma_start(
                out=vbuf[:].rearrange("p (kk d) -> p kk d", d=128)[:, part*4:(part+1)*4, :],
                in_=VT[:, g*128:(g+1)*128].rearrange("(kk p) d -> p kk d", p=128)[:, part*4:(part+1)*4, :])),
                mark="vld", waits=kw, rot=2, bucket=g)
        for hh in range(min(GQ, nh_ - g * GQ)):
            h = g * GQ + hh
            for jc in range(tc_):
                q0 = jc * 512
                nk = 4 * (jc + 1)
                tag = f"a{h}_{jc}"
                qb = qch[it % 2]
                qw = ([all_tags[it - 2] + "_smm"] if it >= 2 else [])
                p.dma("sync", "qld", (lambda e, h=h, q0=q0, qb=qb: e.dma_start(
                    out=qb[:], in_=QT[h*128:(h+1)*128, q0:q0+512])),
                    mark=f"{tag}_q", waits=qw, rot=2, bucket=it)
                for grp in range(nk // 2):
                    half = (grp % 2) * 1024
                    gw = ["kld", f"{tag}_q", sh_mk]
                    if (grp, it) != (0, 0):
                        gw.append(f"{tag}_e{grp-2}" if grp >= 2 else f"prevexp{it}")
                    for m in range(2):
                        kkt = grp * 2 + m
                        p.op("ten", lambda e, kkt=kkt, m=m, qb=qb, half=half:
                             e.matmul(
                            psA[:, half + m*512: half + m*512 + 512],
                            lhsT=kbuf[:, kkt*128:(kkt+1)*128], rhs=qb[:],
                            start=True, stop=True), chain=(m == 0),
                            waits=gw if m == 0 else [], inc=(m == 1),
                            mark=f"{tag}_smm" if kkt == nk - 1 else None)
                    p.mark_here("ten", f"{tag}_sg{grp}")
                    p.op("sca", lambda e, grp=grp, half=half: e.activation(
                        out=pt[:, grp*1024:(grp+1)*1024],
                        in_=psA[:, half:half+1024],
                        func=AF.Exp, scale=ATT_SCALE, bias=sh_ap),
                        chain=True, waits=[f"{tag}_sg{grp}"],
                        mark=f"{tag}_e{grp}")
                p.marks[p.pre + f"prevexp{it+1}"] = p.marks[p.pre + f"{tag}_e{nk//2-1}"]
                for m in range(4):
                    kkt = 4 * jc + m
                    p.op("gps", lambda e, kkt=kkt, m=m: e.affine_select(
                        out=pt[:, kkt*512:(kkt+1)*512],
                        in_=pt[:, kkt*512:(kkt+1)*512],
                        pattern=[[1, 512]], compare_op=ALU.is_ge, fill=0.0,
                        base=-128*m, channel_multiplier=-1), chain=True,
                        waits=[f"{tag}_e{(4*jc+m)//2}"], mark=f"{tag}_mask{m}")
                denw = [(all_tags[it-1] + "_bcev") if it >= 1 else "kon0",
                        "ones_c"]
                pvw = [(all_tags[it-2] + "_ynorm") if it >= 2 else "kon0",
                       "vld"]
                pYh = pY[:, (it % 2)*512:(it % 2)*512 + 512]
                bcfh = bcf[:, (it % 2)*512:(it % 2)*512 + 512]
                for kkt in range(nk):
                    w = [f"{tag}_e{kkt//2}"]
                    if kkt >= 4 * jc:
                        w.append(f"{tag}_mask{kkt-4*jc}")
                    p.op("ten", lambda e, kkt=kkt, nk=nk: e.matmul(
                        pden[:], lhsT=ones_c[:],
                        rhs=pt[:, kkt*512:(kkt+1)*512],
                        start=(kkt == 0), stop=(kkt == nk - 1)),
                        chain=(kkt == 0),
                        waits=w + (denw if kkt == 0 else []),
                        inc=(kkt == nk - 1))
                p.mark_here("ten", f"{tag}_den")
                for kkt in range(nk):
                    p.op("ten", lambda e, kkt=kkt, nk=nk, pYh=pYh: e.matmul(
                        pYh, lhsT=vbuf[:, kkt*128:(kkt+1)*128],
                        rhs=pt[:, kkt*512:(kkt+1)*512],
                        start=(kkt == 0), stop=(kkt == nk - 1)),
                        chain=(kkt == 0),
                        waits=(pvw if kkt == 0 else []), inc=(kkt == nk - 1),
                        mark=f"{tag}_pv" if kkt == nk - 1 else None)
                p.op("sca", lambda e: e.copy(out=drow[:], in_=pden[0:1, :]),
                     chain=True, waits=[f"{tag}_den"], mark=f"{tag}_drowev")
                p.op("sca", lambda e: e.activation(out=drow[:], in_=drow[:],
                                                   func=AF.Ln), chain=True)
                p.op("sca", lambda e: e.activation(out=rrow[:], in_=drow[:],
                                                   func=AF.Exp, scale=-1.0),
                     chain=True, mark=f"{tag}_recip")
                rhw = [(all_tags[it-1] + "_bc") if it >= 1 else "kon0"]
                p.op("vec", lambda e: e.tensor_copy(rh[:], rrow[:]), chain=True,
                     waits=[f"{tag}_recip"] + rhw)
                p.op("vec", lambda e: e.tensor_tensor(out=rl[:], in0=rrow[:],
                                                      in1=rh[:], op=ALU.subtract),
                     chain=True, mark=f"{tag}_rhl")
                bcw = [(all_tags[it-2] + "_ynorm") if it >= 2 else "kon0"]
                p.op("ten", lambda e: e.matmul(pbc[:], lhsT=ones_r[:], rhs=rh[:],
                                               start=True, stop=False),
                     chain=True, waits=[f"{tag}_rhl", "ones_r"])
                p.op("ten", lambda e: e.matmul(pbc[:], lhsT=ones_r[:], rhs=rl[:],
                                               start=False, stop=True),
                     chain=True, mark=f"{tag}_bc")
                p.op("sca", lambda e, bcfh=bcfh: e.copy(out=bcfh, in_=pbc[:]),
                     chain=True, waits=[f"{tag}_bc"] + bcw, mark=f"{tag}_bcev")
                ystw = [(all_tags[it-2] + "_yst") if it >= 2 else "kon0"]
                p.op("vec", lambda e, yo=yout[it % 2], pYh=pYh, bcfh=bcfh:
                     e.tensor_tensor(
                    out=yo[:], in0=pYh, in1=bcfh, op=ALU.mult), chain=True,
                    waits=[f"{tag}_bcev", f"{tag}_pv"] + ystw,
                    mark=f"{tag}_ynorm")
                p.dma("sca", "yst", (lambda e, yo=yout[it % 2], h=h, q0=q0:
                                     e.dma_start(out=YT[h*128:(h+1)*128, q0:q0+512],
                                                 in_=yo[:])),
                    mark=f"{tag}_yst", waits=[f"{tag}_ynorm"], rot=2, bucket=it)
                it += 1
                yield
        p.mark_here("ten", f"attdone_g{g}")

    la = f"a{nh_-1}_{tc_-1}"
    p.barrier_wait("sync", [f"{la}_yst"] +
                   ([f"a{nh_-1}_{tc_-2}_yst"] if tc_ >= 2 else
                    ([f"a{nh_-2}_{tc_-1}_yst"] if nh_ >= 2 else [])))



def build_wo_ffn0():
    p = PB()
    nc = p.nc
    io = {"x": nc.dram_tensor("x", [T, D], F32, kind="ExternalInput").ap(),
          "xn_hi": nc.dram_tensor("xn_hi", [T, D], BF16, kind="ExternalInput").ap(),
          "YT": nc.dram_tensor("YT", [D, T], BF16, kind="ExternalInput").ap(),
          "wo": nc.dram_tensor("wo", [D, D], F32, kind="ExternalInput").ap(),
          "fw": nc.dram_tensor("fw", [D, D], F32, kind="ExternalInput").ap(),
          "fb": nc.dram_tensor("fb", [1, D], F32, kind="ExternalInput").ap(),
          "x1": nc.dram_tensor("x1", [T, D], F32, kind="ExternalOutput").ap()}
    emit_wo_ffn0(p, io)
    return p.finalize()


def emit_wo_ffn0(p, io):
    nc = p.nc
    X = io["x"]; XNH = io["xn_hi"]; YT = io["YT"]; WO = io["wo"]
    FW = io["fw"]; FB = io["fb"]; X1 = io["x1"]

    wo_b = p.salloc("wo_b", [128, DT_ * D], BF16)
    qT = p.salloc("qT", [128, DT_ * D], BF16)
    scr = p.salloc("scr", [128, 9216], F32)
    fwt = [scr[:, j*2048:(j+1)*2048] for j in range(2)]
    aw = scr[:, 4096:6144]
    sg = scr[:, 6144:8192]
    qb_ = scr[:, 8192:9216].bitcast(BF16)
    ysl = scr[:, 0:4096].bitcast(BF16)
    xsl = scr[:, 4096:8192].bitcast(BF16)
    combo = scr[:, 8192:8704]
    accum = [p.salloc(f"acc{j}", [128, D], F32) for j in range(4)]
    fbs = p.salloc("fbs", [128, 16], F32)
    fbr = p.salloc("fbr", [16, 128], F32)
    thr = p.salloc("thr", [128, 2], F32)
    idf = p.salloc("idf", [128, 128], F32)
    idb = p.salloc("idb", [128, 128], BF16)
    pA = p.palloc("pA", [128, 512], F32)
    pB = p.palloc("pB", [128, 512], F32)
    pC = p.palloc("pC", [128, 1024], F32)
    pQ = p.palloc("pQ", [128, 2048], BF16)

    p.op("gps", lambda e: e.memset(idf[:], 1.0), mark="id0")
    p.op("gps", lambda e: e.affine_select(
        out=idf[:], in_=idf[:], pattern=[[1, 128]], compare_op=ALU.is_equal,
        fill=0.0, base=0, channel_multiplier=-1), chain=True, waits=["id0"],
        mark="idf")
    p.op("vec", lambda e: e.tensor_copy(idb[:], idf[:]), waits=["idf"],
         mark="idb")
    p.dma("sync", "fbld", (lambda e: e.dma_start(
        out=fbr[:], in_=FB.rearrange("x (a b) -> (x a) b", b=128))),
        mark="fbld", rot=1)
    p.op("ten", lambda e: e.transpose(out=pC[:, 0:16], in_=fbr[:],
                                      identity=idf[0:16, 0:16]),
         waits=["fbld", "idf"], chain=True, mark="fbtr")
    p.op("sca", lambda e: e.copy(out=fbs[:], in_=pC[:, 0:16]), waits=["fbtr"],
         chain=True, mark="fbs")

    for ot in range(DT_):
        fb_ = fwt[ot % 2]
        lw = [f"tern{ot-2}"] if ot >= 2 else []
        p.dma("sync", "fwld", (lambda e, ot=ot, fb_=fb_: e.dma_start(
            out=fb_, in_=FW[ot*128:(ot+1)*128, :])), mark=f"fw{ot}",
            waits=lw, rot=2, bucket=ot)
        p.op("sca", lambda e, fb_=fb_: e.activation(
            out=aw, in_=fb_, func=AF.Abs, accum_out=thr[:, 0:1]),
            chain=True, waits=[f"fw{ot}", f"qdone{ot-1}"] if ot >= 1
            else [f"fw{ot}"], mark=f"aw{ot}")
        p.op("vec", lambda e: e.tensor_scalar_mul(thr[:, 1:2], thr[:, 0:1],
                                                  0.7 / D),
             chain=True, waits=[f"aw{ot}"], mark=f"thr{ot}")
        p.op("sca", lambda e, fb_=fb_: e.activation(out=sg, in_=fb_, func=AF.Sign),
             chain=True, mark=f"sg{ot}")
        p.op("vec", lambda e: e.tensor_scalar(out=aw, in0=aw, scalar1=thr[:, 1:2],
                                              scalar2=None, op0=ALU.is_gt),
             chain=True, waits=[f"thr{ot}"], mark=f"b01_{ot}")
        p.op("vec", lambda e: e.tensor_tensor(out=qb_[:, 0:2048], in0=aw, in1=sg,
                                              op=ALU.mult), chain=True,
             waits=[f"sg{ot}", f"b01_{ot}"], mark=f"tern{ot}")
        for k in range(DT_):
            p.op("ten", lambda e, k=k: e.transpose(
                out=pQ[:, (k % 2)*1024:(k % 2)*1024+128], in_=qb_[:, k*128:(k+1)*128],
                identity=idb[:]), chain=True,
                waits=[f"tern{ot}", "idb"] + ([f"qev{ot}_{k-2}"] if k >= 2 else
                      ([f"qev{ot-1}_{DT_-2+k}"] if ot >= 1 else [])),
                mark=f"qtr{ot}_{k}")
            p.op("sca", lambda e, k=k, ot=ot: e.copy(
                out=qT[:, k*D + ot*128: k*D + ot*128 + 128],
                in_=pQ[:, (k % 2)*1024:(k % 2)*1024+128]), chain=True,
                waits=[f"qtr{ot}_{k}"], mark=f"qev{ot}_{k}")
        p.mark_here("sca", f"qdone{ot}")

    for k in range(DT_):
        p.dma("gps", "wold", (lambda e, k=k: e.dma_start(
            out=wo_b[:, k*D:(k+1)*D], in_=WO[k*128:(k+1)*128, :])),
            mark="wold", rot=1)

    for jc in range(TC):
        q0 = jc * 512
        yw = [f"mmdone{jc-1}"] if jc >= 1 else [f"tern{DT_-1}", f"qdone{DT_-1}"]
        for k in range(DT_):
            p.dma("sync", "ysl", (lambda e, k=k, q0=q0: e.dma_start(
                out=ysl[:, k*512:(k+1)*512], in_=YT[k*128:(k+1)*128, q0:q0+512])),
                mark=f"ysl{jc}", waits=yw, rot=2, bucket=jc)
        for k in range(DT_):
            p.dma("sync", "xsl", (lambda e, k=k, q0=q0: e.dma_start(
                out=xsl[:, k*512:(k+1)*512],
                in_=XNH[q0:q0+512, k*128:(k+1)*128], transpose=True)),
                mark=f"xsl{jc}", waits=yw, rot=2, bucket=jc)
        for tt in range(4):
            p.dma("sync", "xld", (lambda e, tt=tt, q0=q0: e.dma_start(
                out=accum[tt][:], in_=X[q0 + tt*128: q0 + tt*128 + 128, :])),
                mark=f"xld{jc}", waits=[f"acst{jc-1}"] if jc >= 1 else [],
                rot=2, bucket=jc)
        for ot in range(DT_):
            tg = f"d{jc}_{ot}"
            prv = f"d{jc}_{ot-1}" if ot >= 1 else (
                f"d{jc-1}_{DT_-1}" if jc >= 1 else None)
            for suf, alias in (("_pAfree", "_pAnext"), ("_pBfree", "_pBnext"),
                               ("_cfree", "_cfree2")):
                p.marks[p.pre + f"{tg}{suf}"] = (p.marks[p.pre + f"{prv}{alias}"]
                                         if prv else ("kon", 0))
            for tt in range(4):
                p.marks[p.pre + f"{tg}_accfree{tt}"] = (p.marks[p.pre + f"{prv}_acc{tt}"]
                                                if prv else ("kon", 0))
            wow = [f"ysl{jc}", "wold", f"{tg}_pAfree"]
            for k in range(DT_):
                p.op("ten", lambda e, k=k, ot=ot: e.matmul(
                    pA[:], lhsT=wo_b[:, k*D + ot*128: k*D + ot*128 + 128],
                    rhs=ysl[:, k*512:(k+1)*512],
                    start=(k == 0), stop=(k == DT_ - 1)), chain=(k == 0),
                    waits=wow if k == 0 else [], inc=(k == DT_ - 1))
            p.mark_here("ten", f"{tg}_pA")
            fww = [f"xsl{jc}", f"{tg}_pBfree"]
            for k in range(DT_):
                p.op("ten", lambda e, k=k, ot=ot: e.matmul(
                    pB[:], lhsT=qT[:, k*D + ot*128: k*D + ot*128 + 128],
                    rhs=xsl[:, k*512:(k+1)*512],
                    start=(k == 0), stop=(k == DT_ - 1)), chain=(k == 0),
                    waits=fww if k == 0 else [], inc=(k == DT_ - 1))
            p.mark_here("ten", f"{tg}_pB")
            p.op("sca", lambda e, ot=ot: e.activation(
                out=combo, in_=pA[:], func=AF.Identity, bias=fbs[:, ot:ot+1]),
                chain=True, waits=[f"{tg}_pA", "fbs", f"{tg}_cfree"],
                mark=f"{tg}_ev1")
            p.op("vec", lambda e: e.tensor_tensor(out=combo, in0=combo, in1=pB[:],
                                                  op=ALU.add), chain=True,
                 waits=[f"{tg}_ev1", f"{tg}_pB"], mark=f"{tg}_ev2")
            p.marks[p.pre + f"{tg}_pAnext"] = p.marks[p.pre + f"{tg}_ev1"]
            p.marks[p.pre + f"{tg}_pBnext"] = p.marks[p.pre + f"{tg}_ev2"]
            for tt in range(4):
                slot = (tt % 2) * 512 + (tt // 2) * 128
                bw = ([f"{tg}_acc{tt-2}"] if tt >= 2 else
                      ([f"{prv}_acc{tt+2}"] if prv else []))
                p.op("ten", lambda e, tt=tt, slot=slot: e.transpose(
                    out=pC[:, slot:slot+128], in_=combo[:, tt*128:(tt+1)*128],
                    identity=idf[:]), chain=True,
                    waits=[f"{tg}_ev2"] + bw,
                    mark=f"{tg}_tr{tt}")
                p.op("vec", lambda e, tt=tt, ot=ot, slot=slot: e.tensor_tensor(
                    out=accum[tt][:, ot*128:(ot+1)*128],
                    in0=accum[tt][:, ot*128:(ot+1)*128],
                    in1=pC[:, slot:slot+128], op=ALU.add), chain=True,
                    waits=[f"{tg}_tr{tt}", f"xld{jc}"], mark=f"{tg}_acc{tt}")
            p.mark_here("vec", f"{tg}_cfree2")
        p.mark_here("ten", f"mmdone{jc}")
        for tt in range(4):
            p.dma("sca", "acst", (lambda e, tt=tt, q0=q0: e.dma_start(
                out=X1[q0 + tt*128: q0 + tt*128 + 128, :], in_=accum[tt][:])),
                mark=f"acst{jc}", waits=[f"d{jc}_{DT_-1}_acc{tt}"],
                rot=2, bucket=jc)

    p.barrier_wait("sync", [f"acst{TC-1}", f"acst{TC-2}"])



def build_xproj():
    p = PB()
    nc = p.nc
    io = {"xn_hi": nc.dram_tensor("xn_hi", [T, D], BF16, kind="ExternalInput").ap(),
          "W_x": nc.dram_tensor("W_x", [96, D], F32, kind="ExternalInput").ap(),
          "W_dt": nc.dram_tensor("W_dt", [D, 64], F32, kind="ExternalInput").ap(),
          "b_dt": nc.dram_tensor("b_dt", [1, D], F32, kind="ExternalInput").ap(),
          "dtv": nc.dram_tensor("dtv", [D, T], F32, kind="ExternalOutput").ap()}
    for nm in ("Bf", "Cf"):
        io[nm] = nc.dram_tensor(nm, [16, T], F32, kind="ExternalOutput").ap()
    emit_xproj(p, io)
    return p.finalize()


def emit_xproj(p, io):
    nc = p.nc
    XNH = io["xn_hi"]; WX = io["W_x"]; WDT = io["W_dt"]; BDT = io["b_dt"]
    DTV = io["dtv"]
    OB = {nm: io[nm] for nm in ("Bf", "Cf")}

    wxf = p.salloc("wxf", [96, D], F32)
    wxb = p.salloc("wxb", [96, D], BF16)
    wxT = p.salloc("wxT", [128, DT_ * 96], BF16)
    wdtf = p.salloc("wdtf", [128, DT_ * 64], F32)
    wdtb = p.salloc("wdtb", [128, DT_ * 64], BF16)
    wdtT = p.salloc("wdtT", [64, DT_ * 128], BF16)
    bdtr = p.salloc("bdtr", [16, 128], F32)
    bdts = p.salloc("bdts", [128, 16], F32)
    idb = p.salloc("idb", [128, 128], BF16)
    idf = p.salloc("idf", [128, 128], F32)
    xsl = [p.salloc(f"xsl{j}", [128, DT_ * 512], BF16) for j in range(2)]
    dtxb = p.salloc("dtxb", [64, 512], BF16)
    t_a = p.salloc("t_a", [128, 512], F32)
    t_r = p.salloc("t_r", [128, 512], F32)
    dtt = [p.salloc(f"dtt{j}", [128, 512], F32) for j in range(2)]
    bcf = p.salloc("bcf_", [16, 512], F32)
    pXd = p.palloc("pXd", [64, 512], F32)
    pXB = p.palloc("pXB", [16, 512], F32)
    pXC = p.palloc("pXC", [16, 512], F32)
    pdt = p.palloc("pdt", [128, 1024], F32)
    ptr = p.palloc("ptr", [128, 512], F32)
    ptrb = p.palloc("ptrb", [128, 2048], BF16)

    p.op("gps", lambda e: e.memset(idf[:], 1.0), mark="id0")
    p.op("gps", lambda e: e.affine_select(
        out=idf[:], in_=idf[:], pattern=[[1, 128]], compare_op=ALU.is_equal,
        fill=0.0, base=0, channel_multiplier=-1), chain=True, waits=["id0"],
        mark="idf")
    p.op("vec", lambda e: e.tensor_copy(idb[:], idf[:]), waits=["idf"],
         mark="idb")
    one_ap, one_mk = p.constcol(1.0)

    p.dma("sync", "wxl", (lambda e: e.dma_start(out=wxf[:], in_=WX)),
          mark="wxl", rot=1)
    p.op("sca", lambda e: e.copy(out=wxb[:], in_=wxf[:]), waits=["wxl"],
         chain=True, mark="wxb")
    for k in range(DT_):
        p.op("ten", lambda e, k=k: e.transpose(
            out=ptrb[:, (k % 2)*1024:(k % 2)*1024+96],
            in_=wxb[:, k*128:(k+1)*128], identity=idb[0:96, 0:96]),
            chain=True, waits=["wxb", "idb"] + ([f"wxev{k-2}"] if k >= 2 else []),
            mark=f"wxtr{k}")
        p.op("sca", lambda e, k=k: e.copy(
            out=wxT[:, k*96:(k+1)*96], in_=ptrb[:, (k % 2)*1024:(k % 2)*1024+96]),
            chain=True, waits=[f"wxtr{k}"], mark=f"wxev{k}")
    p.mark_here("sca", "wxT")
    for k in range(DT_):
        p.dma("sync", "wdl", (lambda e, k=k: e.dma_start(
            out=wdtf[:, k*64:(k+1)*64], in_=WDT[k*128:(k+1)*128, :])),
            mark="wdl", rot=1)
    p.op("sca", lambda e: e.copy(out=wdtb[:], in_=wdtf[:]), waits=["wdl"],
         chain=True, mark="wdtb")
    for k in range(DT_):
        p.op("ten", lambda e, k=k: e.transpose(
            out=ptrb[0:64, (k % 2)*1024:(k % 2)*1024+128],
            in_=wdtb[:, k*64:(k+1)*64], identity=idb[:]),
            chain=True, waits=["wdtb", "idb", f"wxev{DT_-1}"] +
            ([f"wdev{k-2}"] if k >= 2 else []),
            mark=f"wdtr{k}")
        p.op("sca", lambda e, k=k: e.copy(
            out=wdtT[:, k*128:(k+1)*128],
            in_=ptrb[0:64, (k % 2)*1024:(k % 2)*1024+128]), chain=True,
            waits=[f"wdtr{k}"], mark=f"wdev{k}")
    p.mark_here("sca", "wdtT")
    p.dma("sync", "bdl", (lambda e: e.dma_start(
        out=bdtr[:], in_=BDT.rearrange("x (a b) -> (x a) b", b=128))),
        mark="bdl", rot=1)
    p.op("ten", lambda e: e.transpose(out=ptr[:, 0:16], in_=bdtr[:],
                                      identity=idf[0:16, 0:16]),
         waits=["bdl", "idf", f"wdtr{DT_-1}"], chain=True, mark="bdtr2")
    p.op("sca", lambda e: e.copy(out=bdts[:], in_=ptr[:, 0:16]),
         waits=["bdtr2"], chain=True, mark="bdts")

    for jc in range(TC):
        q0 = jc * 512
        sl = xsl[jc % 2]
        lw = [f"xp_done{jc-2}"] if jc >= 2 else []
        for k in range(DT_):
            p.dma("sync", "xsl", (lambda e, k=k, q0=q0, sl=sl: e.dma_start(
                out=sl[:, k*512:(k+1)*512],
                in_=XNH[q0:q0+512, k*128:(k+1)*128], transpose=True)),
                mark=f"xsl{jc}", waits=lw, rot=2, bucket=jc)
        for gname, pdst, c0, cn in (("d", pXd, 0, 64), ("B", pXB, 64, 16),
                                    ("C", pXC, 80, 16)):
            if jc == 0:
                p.marks[p.pre + f"xp{gname}free{jc}"] = ("kon", 0)
            else:
                prevmk = {"d": f"dtxb{jc-1}", "B": f"Bf{jc-1}",
                          "C": f"Cf{jc-1}"}[gname]
                p.marks[p.pre + f"xp{gname}free{jc}"] = p.marks[p.pre + prevmk]
            gw = [f"xsl{jc}", "wxT", f"xp{gname}free{jc}"]
            for k in range(DT_):
                p.op("ten", lambda e, k=k, pdst=pdst, c0=c0, cn=cn, sl=sl:
                     e.matmul(pdst[:],
                              lhsT=wxT[:, k*96 + c0: k*96 + c0 + cn],
                              rhs=sl[:, k*512:(k+1)*512],
                              start=(k == 0), stop=(k == DT_ - 1)),
                     chain=(k == 0),
                     waits=gw if k == 0 else [], inc=(k == DT_ - 1))
            p.mark_here("ten", f"xp{gname}{jc}")
        p.op("sca", lambda e: e.copy(out=dtxb[:], in_=pXd[:]), chain=True,
             waits=[f"xpd{jc}"], mark=f"dtxb{jc}")
        for gname, psrc in (("B", pXB), ("C", pXC)):
            prevst = (f"Bst{jc}" if gname == "C" else
                      (f"Cst{jc-1}" if jc >= 1 else one_mk))
            p.op("sca", lambda e, psrc=psrc: e.copy(out=bcf[:], in_=psrc[:]),
                 chain=True, waits=[f"xp{gname}{jc}", prevst],
                 mark=f"{gname}f{jc}")
            p.dma("sca", "bst", (lambda e, gname=gname, q0=q0: e.dma_start(
                out=OB[gname + "f"][:, q0:q0+512], in_=bcf[:])),
                mark=f"{gname}st{jc}", waits=[f"{gname}f{jc}"], rot=2, bucket=jc)
        for ot in range(DT_):
            bank = pdt[:, (ot % 2)*512:(ot % 2)*512+512]
            dw = [f"dtxb{jc}", "wdtT", "bdts", f"dtev{jc}_{ot-2}" if ot >= 2 else
                  (f"dtev{jc-1}_{DT_-2+ot}" if jc >= 1 else one_mk)]
            p.op("ten", lambda e, ot=ot, bank=bank: e.matmul(
                bank, lhsT=wdtT[:, ot*128:(ot+1)*128], rhs=dtxb[:],
                start=True, stop=True), chain=True, waits=dw,
                mark=f"dtmm{jc}_{ot}")
            p.op("sca", lambda e, ot=ot, bank=bank: e.activation(
                out=t_r[:], in_=bank, func=AF.Relu, bias=bdts[:, ot:ot+1]),
                chain=True, waits=[f"dtmm{jc}_{ot}", f"dtst{jc}_{ot-2}"
                                   if ot >= 2 else (f"dtst{jc-1}_{DT_-2+ot}"
                                                    if jc >= 1 else one_mk)])
            p.op("sca", lambda e, ot=ot, bank=bank: e.activation(
                out=t_a[:], in_=bank, func=AF.Abs, bias=bdts[:, ot:ot+1]),
                chain=True, mark=f"dtev{jc}_{ot}")
            p.op("sca", lambda e: e.activation(out=t_a[:], in_=t_a[:],
                                               func=AF.Exp, scale=-1.0),
                 chain=True)
            p.op("sca", lambda e: e.activation(out=t_a[:], in_=t_a[:],
                                               func=AF.Ln, bias=one_ap),
                 chain=True, waits=[one_mk], mark=f"dtln{jc}_{ot}")
            p.op("vec", lambda e, ot=ot: e.tensor_tensor(
                out=dtt[ot % 2][:], in0=t_r[:], in1=t_a[:], op=ALU.add),
                chain=True, waits=[f"dtln{jc}_{ot}"], mark=f"dtadd{jc}_{ot}")
            p.dma("sca", "dtst", (lambda e, ot=ot, q0=q0: e.dma_start(
                out=DTV[ot*128:(ot+1)*128, q0:q0+512], in_=dtt[ot % 2][:])),
                mark=f"dtst{jc}_{ot}", waits=[f"dtadd{jc}_{ot}"], rot=2,
                bucket=ot)
        p.mark_here("ten", f"xp_done{jc}")

    fin = [f"dtst{TC-1}_{DT_-1}", f"dtst{TC-1}_{DT_-2}",
           f"Bst{TC-1}", f"Cst{TC-1}", f"Bst{TC-2}", f"Cst{TC-2}"]
    p.barrier_wait("sync", fin)



def build_dequant():
    p = PB()
    nc = p.nc
    io = {"gup": nc.dram_tensor("gup", [2 * DI, D // 8], I32, kind="ExternalInput").ap(),
          "gus": nc.dram_tensor("gus", [16, 2 * DI], F16, kind="ExternalInput").ap(),
          "dnp": nc.dram_tensor("dnp", [D, DI // 8], I32, kind="ExternalInput").ap(),
          "dns": nc.dram_tensor("dns", [44, D], F16, kind="ExternalInput").ap(),
          "wgu_d": nc.dram_tensor("wgu_d", [2 * DI, D], BF16, kind="ExternalOutput").ap(),
          "wdn_d": nc.dram_tensor("wdn_d", [D, DI], BF16, kind="ExternalOutput").ap()}
    emit_dequant(p, io)
    return p.finalize()


def emit_dequant(p, io):
    for _ in gen_dequant(p, io):
        pass


def gen_dequant(p, io):
    nc = p.nc
    GUP = io["gup"]; GUS = io["gus"]; DNP = io["dnp"]; DNS = io["dns"]
    WGU = io["wgu_d"]; WDN = io["wdn_d"]

    NT_GU = 2 * DI // 128
    NT_DN = D // 128

    gusf = p.salloc("gusf", [16, 2 * DI], F16)
    dnsf = p.salloc("dnsf", [44, D], F16)
    scnv = p.salloc("scnv", [44, 128], F32)
    sT = p.salloc("sT", [128, NT_GU * 16], F32)
    sTd = p.salloc("sTd", [128, NT_DN * 44], F32)
    idf = p.salloc("idf", [128, 128], F32)
    pk = [p.salloc(f"pk{j}", [128, 1408], I32) for j in range(2)]
    d_nib = p.salloc("d_nib", [128, 5632], I16)
    d_u = p.salloc("d_u", [128, 5632], I16)
    d_t = p.salloc("d_t", [128, 5632], I16)
    d_s = p.salloc("d_s", [128, 5632], I16)
    wout = [p.salloc(f"wout{j}", [128, 5632], BF16) for j in range(2)]
    ptr = p.palloc("ptr", [128, 512], F32)

    p.op("gps", lambda e: e.memset(idf[:], 1.0), mark="id0")
    p.op("gps", lambda e: e.affine_select(
        out=idf[:], in_=idf[:], pattern=[[1, 128]], compare_op=ALU.is_equal,
        fill=0.0, base=0, channel_multiplier=-1), chain=True, waits=["id0"],
        mark="idf")
    p.dma("sync", "sld", (lambda e: e.dma_start(out=gusf[:], in_=GUS)),
          mark="gusl", rot=1)
    p.dma("sync", "sld", (lambda e: e.dma_start(out=dnsf[:], in_=DNS)),
          mark="dnsl", rot=1)

    for i, (nt, srcf, dstT, rows) in enumerate(
            ((NT_GU, gusf, sT, 16), (NT_DN, dnsf, sTd, 44))):
        for k in range(nt):
            tg = f"s{i}_{k}"
            pw = [f"s{i}_{k-1}_ev"] if k >= 1 else (
                [f"s0_{NT_GU-1}_ev"] if i == 1 else [])
            p.op("sca", lambda e, k=k, srcf=srcf, rows=rows: e.copy(
                out=scnv[0:rows, :], in_=srcf[0:rows, k*128:(k+1)*128]),
                chain=True, waits=["dnsl"], mark=f"{tg}_cv")
            p.op("ten", lambda e, k=k, rows=rows: e.transpose(
                out=ptr[:, (k % 2)*256:(k % 2)*256+rows], in_=scnv[0:rows, :],
                identity=idf[0:rows, 0:rows]), chain=True,
                waits=[f"{tg}_cv", "idf"] + pw, mark=f"{tg}_tr")
            p.op("sca", lambda e, k=k, rows=rows, dstT=dstT: e.copy(
                out=dstT[:, k*rows:(k+1)*rows],
                in_=ptr[:, (k % 2)*256:(k % 2)*256+rows]), chain=True,
                waits=[f"{tg}_tr"], mark=f"{tg}_ev")
            if k % 8 == 7:
                yield
    p.mark_here("sca", "scales")

    def decode(i, nt, PKD, ncols_i32, sTx, srows, WOUT, hdim):
        for k in range(nt):
            tg = f"q{i}_{k}"
            buf = pk[k % 2]
            lw = [f"q{i}_{k-2}_nib"] if k >= 2 else (
                [f"q0_{NT_GU-2+k}_nib"] if i == 1 else [])
            p.dma("sync", "pkl", (lambda e, k=k, buf=buf, PKD=PKD,
                                  ncols_i32=ncols_i32: e.dma_start(
                out=buf[:, 0:ncols_i32], in_=PKD[k*128:(k+1)*128, :])),
                mark=f"{tg}_ld", waits=lw, rot=2, bucket=k)
            p16 = buf[:, 0:ncols_i32].bitcast(I16)
            nibv = d_nib[:, 0:hdim].rearrange("p (w f) -> p w f", f=4)
            wprev = ([f"q{i}_{k-1}_w"] if k >= 1 else
                     ([f"q0_{NT_GU-1}_w"] if i == 1 else []))
            for f4 in range(4):
                p.op("vec", lambda e, f4=f4, p16=p16, nibv=nibv: e.tensor_scalar(
                    out=nibv[:, :, f4], in0=p16, scalar1=4 * f4, scalar2=15,
                    op0=ALU.logical_shift_right, op1=ALU.bitwise_and),
                    chain=True,
                    waits=([f"{tg}_ld"] + wprev) if f4 == 0 else [])
            p.mark_here("vec", f"{tg}_nib")
            yield
            nib = d_nib[:, 0:hdim]
            u = d_u[:, 0:hdim]
            t_ = d_t[:, 0:hdim]
            s_ = d_s[:, 0:hdim]
            p.op("vec", lambda e, nib=nib, u=u: e.tensor_scalar(
                out=u, in0=nib, scalar1=7, scalar2=None, op0=ALU.bitwise_and),
                chain=True)
            p.op("vec", lambda e, nib=nib, s_=s_: e.tensor_scalar(
                out=s_, in0=nib, scalar1=8, scalar2=12, op0=ALU.bitwise_and,
                op1=ALU.logical_shift_left), chain=True)
            p.op("vec", lambda e, u=u, t_=t_: e.tensor_scalar(
                out=t_, in0=u, scalar1=2, scalar2=None, op0=ALU.is_ge),
                chain=True)
            p.op("vec", lambda e, u=u, t_=t_: e.tensor_tensor(
                out=t_, in0=t_, in1=u, op=ALU.add), chain=True)
            p.op("vec", lambda e, u=u, nib=nib: e.tensor_scalar(
                out=nib, in0=u, scalar1=1, scalar2=251, op0=ALU.is_ge,
                op1=ALU.mult), chain=True)
            p.op("vec", lambda e, nib=nib, t_=t_: e.tensor_tensor(
                out=t_, in0=t_, in1=nib, op=ALU.add), chain=True)
            p.op("vec", lambda e, t_=t_, s_=s_: e.tensor_scalar(
                out=t_, in0=t_, scalar1=6, scalar2=None,
                op0=ALU.logical_shift_left), chain=True)
            p.op("vec", lambda e, t_=t_, s_=s_: e.tensor_tensor(
                out=t_, in0=t_, in1=s_, op=ALU.bitwise_or), chain=True,
                mark=f"{tg}_bits")
            yield
            wo_ = wout[k % 2]
            ng = hdim // 128
            sw = [f"{tg}_bits", "scales"] + ([f"q{i}_{k-2}_st"] if k >= 2 else
                  ([f"q0_{NT_GU-2+k}_st"] if i == 1 else []))
            p.op("vec", lambda e, t_=t_, wo_=wo_, k=k, sTx=sTx, srows=srows,
                 ng=ng, hdim=hdim: e.tensor_tensor(
                out=wo_[:, 0:hdim].rearrange("p (g i) -> p g i", i=128),
                in0=t_.bitcast(BF16).rearrange("p (g i) -> p g i", i=128),
                in1=sTx[:, k*srows:k*srows+ng].unsqueeze(-1).broadcast_to(
                    [128, ng, 128]),
                op=ALU.mult), chain=True, waits=sw, mark=f"{tg}_w")
            p.dma("sca", "wst", (lambda e, k=k, wo_=wo_, WOUT=WOUT,
                                 hdim=hdim: e.dma_start(
                out=WOUT[k*128:(k+1)*128, :], in_=wo_[:, 0:hdim])),
                mark=f"{tg}_st", waits=[f"{tg}_w"], rot=2, bucket=k)
            yield

    yield from decode(0, NT_GU, GUP, D // 8, sT, 16, WGU, D)
    yield from decode(1, NT_DN, DNP, DI // 8, sTd, 44, WDN, DI)
    p.barrier_wait("sync", [f"q1_{NT_DN-1}_st", f"q1_{NT_DN-2}_st"])



def build_scan():
    p = PB()
    build_scan.last_pb = p
    nc = p.nc
    io = {"dtv": nc.dram_tensor("dtv", [D, T], F32, kind="ExternalInput").ap(),
          "xn_hi": nc.dram_tensor("xn_hi", [T, D], BF16, kind="ExternalInput").ap(),
          "xn_lo": nc.dram_tensor("xn_lo", [T, D], BF16, kind="ExternalInput").ap(),
          "A_log": nc.dram_tensor("A_log", [D, NST], F32, kind="ExternalInput").ap(),
          "Dv": nc.dram_tensor("Dv", [1, D], F32, kind="ExternalInput").ap(),
          "yssm_hi": nc.dram_tensor("yssm_hi", [D, T], BF16, kind="ExternalOutput").ap(),
          "yssm_lo": nc.dram_tensor("yssm_lo", [D, T], BF16, kind="ExternalOutput").ap()}
    for nm in ("Bf", "Cf"):
        io[nm] = nc.dram_tensor(nm, [16, T], F32, kind="ExternalInput").ap()
    emit_scan(p, io)
    return p.finalize()


def emit_scan(p, io):
    nc = p.nc
    DTV = io["dtv"]; XNH = io["xn_hi"]; XNL = io["xn_lo"]
    AL = io["A_log"]; DV = io["Dv"]
    Bf = io["Bf"]; Cf = io["Cf"]
    YH = io["yssm_hi"]; YL = io["yssm_lo"]

    CSZ = 2048
    NCH = T // CSZ
    JG = 4

    dtsl = p.salloc("dtsl", [128, JG * CSZ], F32)
    zsl = p.salloc("zsl", [128, JG * CSZ], F32)
    ysl = p.salloc("ysl", [128, JG * CSZ], F32)
    xnh_t = p.salloc("xnh_t", [128, CSZ], BF16)
    xnl_t = p.salloc("xnl_t", [128, CSZ], BF16)
    xnf = p.salloc("xnf", [128, CSZ], F32)
    Asb = p.salloc("Asb", [128, DT_ * NST], F32)
    Dr = p.salloc("Dr", [16, 128], F32)
    Ds = p.salloc("Ds", [128, 16], F32)
    Bbc = p.salloc("Bbc", [128, CSZ], F32)
    Cbc = p.salloc("Cbc", [128, CSZ], F32)
    dA = [p.salloc(f"dA{i}", [128, CSZ], F32) for i in range(2)]
    ub2 = [p.salloc(f"ub{i}", [128, CSZ], F32) for i in range(2)]
    hb = [p.salloc(f"hb{i}", [128, CSZ], F32) for i in range(2)]
    hc = [p.salloc(f"hc{i}", [128, CSZ], F32) for i in range(2)]
    carry = p.salloc("carry", [128, DT_ * NST], F32)
    yh = p.salloc("yh", [128, CSZ], BF16)
    yl = p.salloc("yl", [128, CSZ], BF16)
    idf = p.salloc("idf", [128, 128], F32)
    ptr = p.palloc("ptr", [128, 512], F32)

    p.op("gps", lambda e: e.memset(idf[:], 1.0), mark="id0")
    p.op("gps", lambda e: e.affine_select(
        out=idf[:], in_=idf[:], pattern=[[1, 128]], compare_op=ALU.is_equal,
        fill=0.0, base=0, channel_multiplier=-1), chain=True, waits=["id0"],
        mark="idf")
    p.op("gps", lambda e: e.memset(carry[:], 0.0), chain=True, mark="carry0")
    p.alias("setup", "carry0")
    for j in range(DT_):
        p.dma("sync", "ald", (lambda e, j=j: e.dma_start(
            out=Asb[:, j*NST:(j+1)*NST], in_=AL[j*128:(j+1)*128, :])),
            mark="ald", rot=1)
    p.dma("sync", "dld", (lambda e: e.dma_start(
        out=Dr[:], in_=DV.rearrange("x (a b) -> (x a) b", b=128))),
        mark="dld", rot=1)
    p.op("sca", lambda e: e.activation(out=Asb[:], in_=Asb[:], func=AF.Exp),
         waits=["ald"], chain=True, mark="aexp")
    p.op("vec", lambda e: e.tensor_scalar_mul(Asb[:], Asb[:], -1.0),
         waits=["aexp"], chain=True, mark="Asb")
    p.op("ten", lambda e: e.transpose(out=ptr[:, 0:16], in_=Dr[:],
                                      identity=idf[0:16, 0:16]),
         waits=["dld", "idf"], chain=True, mark="dtr")
    p.op("sca", lambda e: e.copy(out=Ds[:], in_=ptr[:, 0:16]), waits=["dtr"],
         chain=True, mark="Ds")

    all_it = []
    for c_ in range(NCH):
        for jg_ in range(DT_ // JG):
            for n_ in range(NST):
                for jj_ in range(JG):
                    all_it.append(f"s{c_}_{jg_}_{n_}_{jj_}")

    pend_cy = []
    pend_hcm = []
    state = {"last_hcm": None}

    def emit_hcm_yadd(p_, rec):
        hit, htg, hgtag, hjj, hhbb = rec
        hcb = hc[hit % 2]
        heng = "vec" if hit % 4 == 0 else "gps"
        p_.op(heng, lambda e, hhbb=hhbb, hcb=hcb: e.tensor_tensor(
            out=hcb[:], in0=hhbb[:], in1=Cbc[:], op=ALU.mult), chain=True,
            waits=[f"sc{htg}", f"Cb{htg[1:htg.rindex('_')]}"] +
            ([f"yy{all_it[hit-2]}"] if hit >= 2 else []),
            mark=f"hcm{htg}")
        p_.alias(f"hh{htg}", f"hcm{htg}")
        state["last_hcm"] = f"hcm{htg}"
        p_.op("vec", lambda e, hjj=hjj, hcb=hcb: e.tensor_tensor(
            out=ysl[:, hjj*CSZ:(hjj+1)*CSZ],
            in0=ysl[:, hjj*CSZ:(hjj+1)*CSZ], in1=hcb[:],
            op=ALU.add), chain=True,
            waits=[f"hcm{htg}", f"zb{hgtag}_{hjj}"], mark=f"yy{htg}")

    def emit_cy(p_, rec):
        ctg, cj, cn, chbb = rec
        p_.op("sca", lambda e, cj=cj, cn=cn, chbb=chbb: e.copy(
            out=carry[:, cj*NST+cn:cj*NST+cn+1], in_=chbb[:, CSZ-1:CSZ]),
            chain=True, waits=[f"sc{ctg}"], mark=f"cy{ctg}")

    it = 0
    for c in range(NCH):
        c0 = c * CSZ
        for jg in range(DT_ // JG):
            gtag = f"g{c}_{jg}"
            for jj in range(JG):
                j = jg * JG + jj
                jw = ([f"uu{c}_{jg-1}_15_{JG-1}", f"das{c}_{jg-1}_15_{JG-1}"]
                      if jg >= 1 else
                      ([f"uu{c-1}_{DT_//JG-1}_15_{JG-1}",
                        f"das{c-1}_{DT_//JG-1}_15_{JG-1}"] if c >= 1
                       else ["setup"]))
                p.dma("sync", "dtl", (lambda e, j=j, jj=jj, c0=c0: e.dma_start(
                    out=dtsl[:, jj*CSZ:(jj+1)*CSZ],
                    in_=DTV[j*128:(j+1)*128, c0:c0+CSZ])),
                    mark=f"dtl{gtag}", waits=jw, rot=2,
                    bucket=c * 4 + jg)
            for jj in range(JG):
                j = jg * JG + jj
                xw = [f"xnf{gtag}_{jj-1}"] if jj >= 1 else (
                    [f"xnfg{c}_{jg-1}_{JG-1}"] if jg >= 1 else (
                        [f"xnfg{c-1}_{DT_//JG-1}_{JG-1}"] if c >= 1 else ["setup"]))
                p.dma("sync", "xh", (lambda e, j=j, c0=c0: e.dma_start(
                    out=xnh_t[:], in_=XNH[c0:c0+CSZ, j*128:(j+1)*128],
                    transpose=True)), mark=f"xh{gtag}_{jj}", waits=xw, rot=2,
                    bucket=it + jj)
                p.dma("sync", "xl2", (lambda e, j=j, c0=c0: e.dma_start(
                    out=xnl_t[:], in_=XNL[c0:c0+CSZ, j*128:(j+1)*128],
                    transpose=True)), mark=f"xl{gtag}_{jj}", waits=xw, rot=2,
                    bucket=it + jj)
                p.op("vec", lambda e: e.tensor_tensor(out=xnf[:], in0=xnh_t[:],
                                                      in1=xnl_t[:], op=ALU.add),
                     chain=True, waits=[f"xh{gtag}_{jj}", f"xl{gtag}_{jj}"],
                     mark=f"xnf{gtag}_{jj}")
                p.op("vec", lambda e, jj=jj: e.tensor_tensor(
                    out=zsl[:, jj*CSZ:(jj+1)*CSZ],
                    in0=dtsl[:, jj*CSZ:(jj+1)*CSZ],
                    in1=xnf[:], op=ALU.mult), chain=True,
                    waits=[f"dtl{gtag}", f"uu{c-1}_{jg}_15_{jj}" if c >= 1
                           else "setup"])
                p.op("vec", lambda e, jj=jj, j=j: e.tensor_scalar_mul(
                    ysl[:, jj*CSZ:(jj+1)*CSZ], xnf[:],
                    Ds[:, j:j+1]), chain=True,
                    waits=["Ds", f"ystl{c-1}_{jg}_{jj}" if c >= 1 else "setup"],
                    mark=f"zb{gtag}_{jj}")
            for n in range(NST):
                ntag = f"{gtag}_{n}"
                pbw = ([f"uu{c}_{jg}_{n-1}_{JG-1}"] if n >= 1 else
                       ([f"uu{c}_{jg-1}_15_{JG-1}"] if jg >= 1 else
                        ([f"uu{c-1}_{DT_//JG-1}_15_{JG-1}"] if c >= 1 else
                         ["setup"])))
                p.dma("sync", "bbc", (lambda e, n=n, c0=c0: e.dma_start(
                    out=Bbc[:], in_=Bf[n:n+1, c0:c0+CSZ].partition_broadcast(128))),
                    mark=f"Bbc{ntag}", waits=pbw, rot=2, bucket=it)
                for jj in range(JG):
                    j = jg * JG + jj
                    tg = f"s{c}_{jg}_{n}_{jj}"
                    da = dA[it % 2]
                    hbb = hb[it % 2]
                    ubb = ub2[it % 2]
                    p.op("sca", lambda e, jj=jj, j=j, n=n, da=da: e.activation(
                        out=da[:], in_=dtsl[:, jj*CSZ:(jj+1)*CSZ],
                        func=AF.Exp, scale=Asb[:, j*NST+n:j*NST+n+1]),
                        chain=True,
                        waits=[f"dtl{gtag}", "Asb"] +
                        ([f"sc{all_it[it-2]}"] if it >= 2 else []),
                        mark=f"da{tg}")
                    if pend_cy:
                        emit_cy(p, pend_cy.pop(0))
                    p.op("gps", lambda e, jj=jj, ubb=ubb: e.tensor_tensor(
                        out=ubb[:], in0=zsl[:, jj*CSZ:(jj+1)*CSZ],
                        in1=Bbc[:], op=ALU.mult), chain=True,
                        waits=[f"Bbc{ntag}", f"zb{gtag}_{jj}", "carry0"] +
                        ([f"sc{all_it[it-2]}"] if it >= 2 else []),
                        mark=f"uu{c}_{jg}_{n}_{jj}")
                    p.op("vec", lambda e, da=da, hbb=hbb, ubb=ubb, j=j, n=n:
                         e.tensor_tensor_scan(
                        out=hbb[:], data0=da[:], data1=ubb[:],
                        initial=carry[:, j*NST+n:j*NST+n+1],
                        op0=ALU.mult, op1=ALU.add), chain=True,
                        waits=[f"da{tg}", f"uu{c}_{jg}_{n}_{jj}"] +
                        ([f"hh{all_it[it-2]}", f"cy{all_it[it-2]}"]
                         if it >= 2 else []) +
                        ([f"cys{c-1}_{jg}_{n}_{jj}"] if c >= 1 else []),
                        mark=f"sc{tg}")
                    pend_cy.append((tg, j, n, hbb))
                    if pend_hcm:
                        emit_hcm_yadd(p, pend_hcm.pop(0))
                    if jj == 0:
                        p.dma("sync", "cbc", (lambda e, n=n, c0=c0: e.dma_start(
                            out=Cbc[:],
                            in_=Cf[n:n+1, c0:c0+CSZ].partition_broadcast(128))),
                            mark=f"Cb{c}_{jg}_{n}",
                            waits=[state["last_hcm"] or "setup"],
                            rot=2, bucket=it)
                    pend_hcm.append((it, tg, gtag, jj, hbb))
                    it += 1
            while pend_hcm:
                emit_hcm_yadd(p, pend_hcm.pop(0))
            while pend_cy:
                emit_cy(p, pend_cy.pop(0))
            for jj in range(JG):
                j = jg * JG + jj
                pst = ((c, jg, jj - 1) if jj >= 1 else
                       ((c, jg - 1, JG - 1) if jg >= 1 else
                        ((c - 1, DT_//JG - 1, JG - 1) if c >= 1 else None)))
                sw = [f"yys{c}_{jg}_{NST-1}_{jj}"]
                if pst:
                    sw.append(f"ysth{pst[0]}_{pst[1]}_{pst[2]}")
                p.op("vec", lambda e, jj=jj: e.tensor_copy(
                    yh[:], ysl[:, jj*CSZ:(jj+1)*CSZ]),
                     chain=True, waits=sw, mark=f"yhi{gtag}_{jj}")
                p.op("vec", lambda e, jj=jj: e.tensor_tensor(
                    out=yl[:], in0=ysl[:, jj*CSZ:(jj+1)*CSZ], in1=yh[:],
                    op=ALU.subtract), chain=True,
                    waits=[f"ystl{pst[0]}_{pst[1]}_{pst[2]}"] if pst else [],
                    mark=f"ylo{gtag}_{jj}")
                p.dma("sca", "ysh", (lambda e, j=j, c0=c0: e.dma_start(
                    out=YH[j*128:(j+1)*128, c0:c0+CSZ], in_=yh[:])),
                    mark=f"ysth{c}_{jg}_{jj}", waits=[f"yhi{gtag}_{jj}"],
                    rot=2, bucket=it + jj)
                p.dma("sca", "ysl2", (lambda e, j=j, c0=c0: e.dma_start(
                    out=YL[j*128:(j+1)*128, c0:c0+CSZ], in_=yl[:])),
                    mark=f"ystl{c}_{jg}_{jj}", waits=[f"ylo{gtag}_{jj}"],
                    rot=2, bucket=it + jj)

    fin = [f"ysth{NCH-1}_{DT_//JG-1}_{JG-1}", f"ystl{NCH-1}_{DT_//JG-1}_{JG-1}",
           f"ysth{NCH-1}_{DT_//JG-1}_{JG-2}", f"ystl{NCH-1}_{DT_//JG-1}_{JG-2}"]
    p.barrier_wait("sync", fin)

def build_gateup():
    p = PB()
    nc = p.nc
    io = {"xn_hi": nc.dram_tensor("xn_hi", [T, D], BF16, kind="ExternalInput").ap(),
          "wgu_d": nc.dram_tensor("wgu_d", [2 * DI, D], BF16, kind="ExternalInput").ap(),
          "hact": nc.dram_tensor("hact", [DI, T], BF16, kind="ExternalOutput").ap()}
    emit_gateup(p, io)
    return p.finalize()


def emit_gateup(p, io):
    nc = p.nc
    XNH = io["xn_hi"]; WGU = io["wgu_d"]; HACT = io["hact"]

    NSP = DI // 512

    xnf = p.salloc("xnf", [128, DT_ * T], BF16)
    wg = [p.salloc(f"wg{j}", [128, DT_ * 512], BF16) for j in range(2)]
    wu = [p.salloc(f"wu{j}", [128, DT_ * 512], BF16) for j in range(2)]
    sg = p.salloc("sg", [128, 512], F32)
    t1 = p.salloc("t1", [128, 512], F32)
    ht = [p.salloc(f"ht{j}", [128, 512], BF16) for j in range(2)]
    pG = p.palloc("pG", [128, 1024], F32)
    pU = p.palloc("pU", [128, 1024], F32)

    for k in range(DT_):
        p.dma("sync", "xnl", (lambda e, k=k: e.dma_start(
            out=xnf[:, k*T:(k+1)*T], in_=XNH[:, k*128:(k+1)*128],
            transpose=True)), mark="xnl", rot=1)

    it = 0
    all_h = [f"h{sp}_{o4}_{jc}" for sp in range(NSP) for o4 in range(4)
             for jc in range(TC)]
    for sp in range(NSP):
        wgb, wub = wg[sp % 2], wu[sp % 2]
        lw = [f"sp_done{sp-2}"] if sp >= 2 else []
        for k in range(DT_):
            p.dma("sync", "wgl", (lambda e, k=k, sp=sp, wgb=wgb: e.dma_start(
                out=wgb[:, k*512:(k+1)*512],
                in_=WGU[sp*512:(sp+1)*512, k*128:(k+1)*128], transpose=True)),
                mark=f"wgl{sp}", waits=lw, rot=2, bucket=sp)
            p.dma("sync", "wul", (lambda e, k=k, sp=sp, wub=wub: e.dma_start(
                out=wub[:, k*512:(k+1)*512],
                in_=WGU[DI + sp*512: DI + sp*512 + 512, k*128:(k+1)*128],
                transpose=True)), mark=f"wul{sp}", waits=lw, rot=2, bucket=sp)
        for o4 in range(4):
            for jc in range(TC):
                tg = f"h{sp}_{o4}_{jc}"
                gb = pG[:, (it % 2)*512:(it % 2)*512+512]
                ub_ = pU[:, (it % 2)*512:(it % 2)*512+512]
                prev = it - 2
                gw = [f"wgl{sp}", f"wul{sp}", "xnl"]
                if prev >= 0:
                    gw += [all_h[prev] + "_sg"]
                uw = [all_h[prev] + "_h"] if prev >= 0 else []
                for k in range(DT_):
                    p.op("ten", lambda e, k=k, o4=o4, jc=jc, gb=gb, wgb=wgb:
                         e.matmul(gb, lhsT=wgb[:, k*512 + o4*128: k*512 + o4*128 + 128],
                                  rhs=xnf[:, k*T + jc*512: k*T + jc*512 + 512],
                                  start=(k == 0), stop=(k == DT_ - 1)),
                         chain=(k == 0), waits=gw if k == 0 else [],
                         inc=(k == DT_ - 1))
                p.mark_here("ten", f"{tg}_g")
                for k in range(DT_):
                    p.op("ten", lambda e, k=k, o4=o4, jc=jc, ub_=ub_, wub=wub:
                         e.matmul(ub_, lhsT=wub[:, k*512 + o4*128: k*512 + o4*128 + 128],
                                  rhs=xnf[:, k*T + jc*512: k*T + jc*512 + 512],
                                  start=(k == 0), stop=(k == DT_ - 1)),
                         chain=(k == 0), waits=uw if k == 0 else [],
                         inc=(k == DT_ - 1))
                p.mark_here("ten", f"{tg}_u")
                p.marks[p.pre + f"{tg}_t1free"] = (p.marks[p.pre + all_h[it - 1] + "_sg"]
                                           if it >= 1 else ("kon", 0))
                p.op("sca", lambda e, gb=gb: e.activation(
                    out=sg[:], in_=gb, func=AF.Sigmoid), chain=True,
                    waits=[f"{tg}_g", f"{tg}_t1free"], mark=f"{tg}_sg0")
                p.op("vec", lambda e, gb=gb: e.tensor_tensor(
                    out=t1[:], in0=gb, in1=sg[:], op=ALU.mult), chain=True,
                    waits=[f"{tg}_sg0"], mark=f"{tg}_sg")
                hb_ = ht[it % 2]
                hw = [all_h[prev] + "_st"] if prev >= 0 else []
                p.op("vec", lambda e, ub_=ub_, hb_=hb_: e.tensor_tensor(
                    out=hb_[:], in0=t1[:], in1=ub_, op=ALU.mult), chain=True,
                    waits=[f"{tg}_u"] + hw, mark=f"{tg}_h")
                p.dma("sca", "hst", (lambda e, sp=sp, o4=o4, jc=jc, hb_=hb_:
                                     e.dma_start(
                    out=HACT[sp*512 + o4*128: sp*512 + o4*128 + 128,
                             jc*512:(jc+1)*512], in_=hb_[:])),
                    mark=f"{tg}_st", waits=[f"{tg}_h"], rot=2, bucket=it)
                it += 1
        p.mark_here("ten", f"sp_done{sp}")

    p.barrier_wait("sync", [all_h[-1] + "_st", all_h[-2] + "_st"])



def build_down():
    p = PB()
    nc = p.nc
    io = {"x1": nc.dram_tensor("x1", [T, D], F32, kind="ExternalInput").ap(),
          "hact": nc.dram_tensor("hact", [DI, T], BF16, kind="ExternalInput").ap(),
          "wdn_d": nc.dram_tensor("wdn_d", [D, DI], BF16, kind="ExternalInput").ap(),
          "yssm_hi": nc.dram_tensor("yssm_hi", [D, T], BF16, kind="ExternalInput").ap(),
          "yssm_lo": nc.dram_tensor("yssm_lo", [D, T], BF16, kind="ExternalInput").ap(),
          "out": nc.dram_tensor("out", [T, D], F32, kind="ExternalOutput").ap()}
    emit_down(p, io)
    return p.finalize()


def emit_down(p, io):
    nc = p.nc
    X1 = io["x1"]; HACT = io["hact"]; WDN = io["wdn_d"]
    YH = io["yssm_hi"]; YL = io["yssm_lo"]; OUT = io["out"]

    NI = DI // 128

    hsl = p.salloc("hsl", [128, NI * 512], BF16)
    wdn0 = p.salloc("wdn0", [128, NI * 1024], BF16)
    accy = [p.salloc(f"accy{j}", [128, D], F32) for j in range(4)]
    accr = [p.salloc(f"accr{j}", [128, D], F32) for j in range(4)]
    ytmp = p.salloc("ytmp", [128, D], BF16)
    cmb = p.salloc("cmb", [128, 512], F32)
    st_ = p.salloc("st_", [128, 4], F32)
    idf = p.salloc("idf", [128, 128], F32)
    pD = p.palloc("pD", [128, 1024], F32)
    pT = p.palloc("pT", [128, 1024], F32)
    sqd = p.palloc("sqd", [128, 2048], F32)

    p.op("gps", lambda e: e.memset(idf[:], 1.0), mark="id0")
    p.op("gps", lambda e: e.affine_select(
        out=idf[:], in_=idf[:], pattern=[[1, 128]], compare_op=ALU.is_equal,
        fill=0.0, base=0, channel_multiplier=-1), chain=True, waits=["id0"],
        mark="idf")
    eps_ap, eps_mk = p.constcol(RMS_EPS)

    for jc in range(TC):
        q0 = jc * 512
        lw = [f"ffdone{jc-1}"] if jc >= 1 else []
        for it_ in range(NI):
            p.dma("sync", "hl", (lambda e, it_=it_, q0=q0: e.dma_start(
                out=hsl[:, it_*512:(it_+1)*512],
                in_=HACT[it_*128:(it_+1)*128, q0:q0+512])),
                mark=f"hl{jc}", waits=lw, rot=2, bucket=jc)
        for tt in range(4):
            aw = [f"outst{jc-1}"] if jc >= 1 else []
            p.dma("sync", "x1l", (lambda e, tt=tt, q0=q0: e.dma_start(
                out=accr[tt][:], in_=X1[q0 + tt*128: q0 + tt*128 + 128, :])),
                mark=f"x1l{jc}_{tt}", waits=aw, rot=2, bucket=jc * 4 + tt)
            p.dma("sync", "yhl", (lambda e, tt=tt, q0=q0: e.dma_start(
                out=ytmp[:], in_=YH[:, q0 + tt*128: q0 + tt*128 + 128],
                transpose=True)), mark=f"yhl{jc}_{tt}",
                waits=[f"accr{jc}_{tt-1}" if tt >= 1 else
                       (f"accr{jc-1}_3" if jc >= 1 else "idf")],
                rot=2, bucket=jc * 4 + tt)
            p.op("vec", lambda e, tt=tt: e.tensor_tensor(
                out=accr[tt][:], in0=accr[tt][:], in1=ytmp[:], op=ALU.add),
                chain=True, waits=[f"x1l{jc}_{tt}", f"yhl{jc}_{tt}"],
                mark=f"yhread{jc}_{tt}")
            p.dma("sync", "yll", (lambda e, tt=tt, q0=q0: e.dma_start(
                out=ytmp[:], in_=YL[:, q0 + tt*128: q0 + tt*128 + 128],
                transpose=True)), mark=f"yll{jc}_{tt}",
                waits=[f"yhread{jc}_{tt}"], rot=2, bucket=jc * 4 + tt)
            p.op("vec", lambda e, tt=tt: e.tensor_tensor(
                out=accr[tt][:], in0=accr[tt][:], in1=ytmp[:], op=ALU.add),
                chain=True, waits=[f"yll{jc}_{tt}"], mark=f"accr{jc}_{tt}")
        it2 = 0
        for hsp in range(4):
            wb = wdn0
            if hsp % 2 == 0:
                hp = hsp // 2
                wlw = [f"mmdone{jc}_{hsp-1}"] if hsp >= 1 else (
                    [f"mmdone{jc-1}_3"] if jc >= 1 else [])
                for it_ in range(NI):
                    p.dma("sync", "wdl", (lambda e, it_=it_, hp=hp:
                                          e.dma_start(
                        out=wdn0[:, it_*1024:(it_+1)*1024],
                        in_=WDN[hp*1024:(hp+1)*1024, it_*128:(it_+1)*128],
                        transpose=True)), mark=f"wdl{jc}_{hp}", waits=wlw,
                        rot=2, bucket=jc * 2 + hp)
            for o4 in range(4):
                tg = f"dn{jc}_{hsp}_{o4}"
                bank = pD[:, (it2 % 2)*512:(it2 % 2)*512+512]
                gidx = jc * 16 + hsp * 4 + o4
                all_dn = [f"dn{g//16}_{(g%16)//4}_{g%4}" for g in range(128)]
                p.marks[p.pre + f"{tg}_evfree"] = (
                    p.marks[p.pre + all_dn[gidx-2] + "_ev"]
                    if gidx >= 2 else ("kon", 0))
                p.marks[p.pre + f"{tg}_cfree"] = (
                    p.marks[p.pre + all_dn[gidx-1] + "_acc3"]
                    if gidx >= 1 else ("kon", 0))
                for tt in range(4):
                    p.marks[p.pre + f"{tg}_trfree{tt}"] = (
                        p.marks[p.pre + all_dn[gidx-1] + f"_acc{tt}"]
                        if gidx >= 1 else ("kon", 0))
                dw = [f"hl{jc}", f"wdl{jc}_{hsp//2}"]
                if gidx >= 2:
                    dw.append(f"{tg}_evfree")
                for k in range(NI):
                    p.op("ten", lambda e, k=k, o4=o4, hsp=hsp, bank=bank:
                         e.matmul(bank,
                                  lhsT=wdn0[:, k*1024 + (hsp % 2)*512 + o4*128:
                                            k*1024 + (hsp % 2)*512 + o4*128 + 128],
                                  rhs=hsl[:, k*512:(k+1)*512],
                                  start=(k == 0), stop=(k == NI - 1)),
                         chain=(k == 0), waits=dw if k == 0 else [],
                         inc=(k == NI - 1))
                p.mark_here("ten", f"{tg}_mm")
                p.op("sca", lambda e, bank=bank: e.copy(out=cmb[:], in_=bank),
                     chain=True, waits=[f"{tg}_mm", f"{tg}_cfree"],
                     mark=f"{tg}_ev")
                for tt in range(4):
                    slot = (tt % 2) * 512 + (tt // 2) * 128
                    bw2 = [f"{tg}_trfree{tt}"]
                    p.op("ten", lambda e, tt=tt, slot=slot: e.transpose(
                        out=pT[:, slot:slot+128], in_=cmb[:, tt*128:(tt+1)*128],
                        identity=idf[:]), chain=True,
                        waits=[f"{tg}_ev", "idf"] + bw2, mark=f"{tg}_tr{tt}")
                    oc = hsp * 512 + o4 * 128
                    p.op("vec", lambda e, tt=tt, oc=oc, slot=slot: e.tensor_copy(
                        accy[tt][:, oc:oc+128], pT[:, slot:slot+128]),
                        chain=True, waits=[f"{tg}_tr{tt}"], mark=f"{tg}_acc{tt}")
                it2 += 1
            p.mark_here("ten", f"mmdone{jc}_{hsp}")
        for tt in range(4):
            fw = [f"dn{jc}_3_{o4}_acc{tt}" for o4 in range(4)]
            p.op("sca", lambda e, tt=tt: e.activation(
                out=sqd[:], in_=accy[tt][:], func=AF.Square,
                accum_out=st_[:, 0:1]), chain=True,
                waits=fw + [f"accr{jc}_{tt}"], mark=f"rs{jc}_{tt}_sq")
            p.op("sca", lambda e: e.activation(
                out=st_[:, 1:2], in_=st_[:, 0:1], func=AF.Ln, scale=1.0 / D,
                bias=eps_ap), chain=True, waits=[f"rs{jc}_{tt}_sq", eps_mk])
            p.op("sca", lambda e: e.activation(
                out=st_[:, 2:3], in_=st_[:, 1:2], func=AF.Exp, scale=-0.5),
                chain=True, mark=f"rs{jc}_{tt}")
            p.op("vec", lambda e, tt=tt: e.scalar_tensor_tensor(
                out=accr[tt][:], in0=accy[tt][:], scalar=st_[:, 2:3],
                in1=accr[tt][:], op0=ALU.mult, op1=ALU.add), chain=True,
                waits=[f"rs{jc}_{tt}"], mark=f"fin{jc}_{tt}")
            p.dma("sca", "ost", (lambda e, tt=tt, q0=q0: e.dma_start(
                out=OUT[q0 + tt*128: q0 + tt*128 + 128, :], in_=accr[tt][:])),
                mark=f"outst{jc}", waits=[f"fin{jc}_{tt}"], rot=2, bucket=jc)
        p.mark_here("ten", f"ffdone{jc}")

    p.barrier_wait("sync", [f"outst{TC-1}", f"outst{TC-2}"])



def build_fused(debug=False):
    p = PB()
    nc = p.nc
    p.init_arena()
    _dt0 = nc.dram_tensor
    kindI = "ExternalOutput" if debug else "Internal"
    def dt_(nm, shape, dtyp, kind="Internal"):
        if kind == "Internal":
            kind = kindI
        return _dt0(nm, shape, dtyp, kind=kind)
    ext_in = {
        "x": dt_("x", [T, D], F32, kind="ExternalInput").ap(),
        "wq": dt_("wq", [D, D], F32, kind="ExternalInput").ap(),
        "wk": dt_("wk", [D, 512], F32, kind="ExternalInput").ap(),
        "wv": dt_("wv", [D, 512], F32, kind="ExternalInput").ap(),
        "wo": dt_("wo", [D, D], F32, kind="ExternalInput").ap(),
        "fw": dt_("fw", [D, D], F32, kind="ExternalInput").ap(),
        "fb": dt_("fb", [1, D], F32, kind="ExternalInput").ap(),
        "W_x": dt_("W_x", [96, D], F32, kind="ExternalInput").ap(),
        "W_dt": dt_("W_dt", [D, 64], F32, kind="ExternalInput").ap(),
        "b_dt": dt_("b_dt", [1, D], F32, kind="ExternalInput").ap(),
        "A_log": dt_("A_log", [D, NST], F32, kind="ExternalInput").ap(),
        "Dv": dt_("Dv", [1, D], F32, kind="ExternalInput").ap(),
        "gup": dt_("gup", [2 * DI, D // 8], I32, kind="ExternalInput").ap(),
        "gus": dt_("gus", [16, 2 * DI], F16, kind="ExternalInput").ap(),
        "dnp": dt_("dnp", [D, DI // 8], I32, kind="ExternalInput").ap(),
        "dns": dt_("dns", [44, D], F16, kind="ExternalInput").ap(),
    }
    OUT = dt_("out", [T, D], F32, kind="ExternalOutput").ap()
    mid = {
        "xn0": dt_("i_xn0", [T, D], BF16, kind="Internal").ap(),
        "QT": dt_("i_QT", [D, T], BF16, kind="Internal").ap(),
        "KT": dt_("i_KT", [512, T], BF16, kind="Internal").ap(),
        "VT": dt_("i_VT", [T, 512], BF16, kind="Internal").ap(),
        "YT": dt_("i_YT", [D, T], BF16, kind="Internal").ap(),
        "x1": dt_("i_x1", [T, D], F32, kind="Internal").ap(),
        "xn1h": dt_("i_xn1h", [T, D], BF16, kind="Internal").ap(),
        "xn1l": dt_("i_xn1l", [T, D], BF16, kind="Internal").ap(),
        "dtv": dt_("i_dtv", [D, T], F32, kind="Internal").ap(),
        "Bf": dt_("i_Bf", [16, T], F32, kind="Internal").ap(),
        "Cf": dt_("i_Cf", [16, T], F32, kind="Internal").ap(),
        "wgu_d": dt_("i_wgu", [2 * DI, D], BF16, kind="Internal").ap(),
        "wdn_d": dt_("i_wdn", [D, DI], BF16, kind="Internal").ap(),
        "ysh": dt_("i_ysh", [D, T], BF16, kind="Internal").ap(),
        "ysl": dt_("i_ysl", [D, T], BF16, kind="Internal").ap(),
        "hact": dt_("i_hact", [DI, T], BF16, kind="Internal").ap(),
    }

    build_fused.phase_bounds = bounds = []

    def phase(pre, emitfn, io, *args):
        p.pre = pre
        p.open_frame()
        emitfn(p, io, *args)
        p.global_barrier()
        bounds.append((pre, {e: len(p.ops[e]) for e in ENG_ORDER}))

    phase("A_", emit_ln, {"x": ext_in["x"], "xn_hi": mid["xn0"],
                          "xn_lo": None}, False)
    phase("B_", emit_qkv, {"xn_hi": mid["xn0"], "wq": ext_in["wq"],
                           "wk": ext_in["wk"], "wv": ext_in["wv"],
                           "QT": mid["QT"], "KT": mid["KT"], "VT": mid["VT"]})
    p.pre = "C_"
    p.open_frame()
    _ga = gen_attn(p, {"QT": mid["QT"], "KT": mid["KT"],
                       "VT": mid["VT"], "YT": mid["YT"]}, NH, TC, NKV)
    _gd = gen_dequant(p, {"gup": ext_in["gup"], "gus": ext_in["gus"],
                          "dnp": ext_in["dnp"], "dns": ext_in["dns"],
                          "wgu_d": mid["wgu_d"], "wdn_d": mid["wdn_d"]})
    _live_a, _live_d = True, True
    while _live_a or _live_d:
        if _live_a:
            try:
                next(_ga)
            except StopIteration:
                _live_a = False
        if _live_d:
            for _ in range(3):
                try:
                    next(_gd)
                except StopIteration:
                    _live_d = False
                    break
    p.global_barrier()
    bounds.append(("C_", {e: len(p.ops[e]) for e in ENG_ORDER}))
    phase("D_", emit_wo_ffn0, {"x": ext_in["x"], "xn_hi": mid["xn0"],
                               "YT": mid["YT"], "wo": ext_in["wo"],
                               "fw": ext_in["fw"], "fb": ext_in["fb"],
                               "x1": mid["x1"]})
    phase("E_", emit_ln, {"x": mid["x1"], "xn_hi": mid["xn1h"],
                          "xn_lo": mid["xn1l"]}, True)
    phase("F_", emit_xproj, {"xn_hi": mid["xn1h"], "W_x": ext_in["W_x"],
                             "W_dt": ext_in["W_dt"], "b_dt": ext_in["b_dt"],
                             "dtv": mid["dtv"], "Bf": mid["Bf"],
                             "Cf": mid["Cf"]})
    phase("H_", emit_scan, {"dtv": mid["dtv"], "xn_hi": mid["xn1h"],
                            "xn_lo": mid["xn1l"], "A_log": ext_in["A_log"],
                            "Dv": ext_in["Dv"], "Bf": mid["Bf"],
                            "Cf": mid["Cf"],
                            "yssm_hi": mid["ysh"], "yssm_lo": mid["ysl"]})
    phase("I_", emit_gateup, {"xn_hi": mid["xn1h"], "wgu_d": mid["wgu_d"],
                              "hact": mid["hact"]})
    phase("J_", emit_down, {"x1": mid["x1"], "hact": mid["hact"],
                            "wdn_d": mid["wdn_d"], "yssm_hi": mid["ysh"],
                            "yssm_lo": mid["ysl"], "out": OUT})
    return p.finalize()


def kernel_fused(L0_attn_x, L0_attn_wq, L0_attn_wk, L0_attn_wv, L0_attn_wo,
                 L0_ffn_weight, L0_ffn_bias, L1_attn_W_x, L1_attn_W_dt,
                 L1_attn_b_dt, L1_attn_A_log, L1_attn_D, L1_ffn_gate_up_packed,
                 L1_ffn_gate_up_scales, L1_ffn_down_packed, L1_ffn_down_scales):
    n = NCORE
    shared = dict(
        wq=np.ascontiguousarray(L0_attn_wq),
        wk=np.ascontiguousarray(L0_attn_wk),
        wv=np.ascontiguousarray(L0_attn_wv),
        wo=np.ascontiguousarray(L0_attn_wo),
        fw=np.ascontiguousarray(L0_ffn_weight),
        fb=np.ascontiguousarray(L0_ffn_bias).reshape(1, D),
        W_x=np.ascontiguousarray(L1_attn_W_x),
        W_dt=np.ascontiguousarray(L1_attn_W_dt),
        b_dt=np.ascontiguousarray(L1_attn_b_dt).reshape(1, D),
        A_log=np.ascontiguousarray(L1_attn_A_log),
        Dv=np.ascontiguousarray(L1_attn_D).reshape(1, D),
        gup=np.ascontiguousarray(L1_ffn_gate_up_packed),
        gus=np.ascontiguousarray(L1_ffn_gate_up_scales),
        dnp=np.ascontiguousarray(L1_ffn_down_packed),
        dns=np.ascontiguousarray(L1_ffn_down_scales),
    )
    in_maps = [dict(shared, x=np.ascontiguousarray(L0_attn_x[b]))
               for b in range(n)]
    r = run_launch(_get("fused", build_fused), in_maps)
    return np.stack([np.asarray(r[b]["out"], dtype=np.float32)
                     for b in range(n)], axis=0)
